# version 02
# baseline (speedup 1.0000x reference)
import numpy as np
import ml_dtypes
import concourse.bass as bass
import concourse.mybir as mybir
from concourse.bass_utils import run_bass_kernel_spmd

F32 = mybir.dt.float32
BF16 = mybir.dt.bfloat16
I32 = mybir.dt.int32
AF = mybir.ActivationFunctionType
ALU = mybir.AluOpType
AX = mybir.AxisListType
BF = ml_dtypes.bfloat16

T = 2048
E = 1024
H = 16
D = 64
NCH = 8
NTT = 16
MMB = 512
MMF = 512

ENGS = ("sync", "gpsimd", "tensor", "vector", "scalar")


class Prog:

    def __init__(self, nc):
        self.nc = nc
        self.steps = []
        self.last_writer = {}
        self.readers = {}
        self.val_after = []

    def step(self, eng, emit, reads=(), writes=(), deps=(), inc=1,
             sync_same=None, name=""):
        if sync_same is None:
            sync_same = eng in ("vector", "scalar")
        i = len(self.steps)
        dd = set()
        for k in reads:
            if k in self.last_writer:
                dd.add(self.last_writer[k])
        for k in writes:
            if k in self.last_writer:
                dd.add(self.last_writer[k])
            dd.update(self.readers.get(k, ()))
        dd.update(d for d in deps if d is not None)
        for k in reads:
            self.readers.setdefault(k, set()).add(i)
        for k in writes:
            self.last_writer[k] = i
            self.readers[k] = set()
        dd.discard(i)
        self.steps.append(dict(eng=eng, emit=emit, deps=sorted(dd), inc=inc,
                               sync_same=sync_same, name=name))
        return i

    def dma(self, eng, out, in_, reads=(), writes=(), deps=(), name="",
            sync_same=True):
        return self.step(
            eng, (lambda e, o=out, s=in_: e.dma_start(out=o, in_=s)),
            reads=reads, writes=writes, deps=deps, inc=16, name=name,
            sync_same=sync_same)

    def barrier_step(self, eng, deps, name="final"):
        return self.step(eng, (lambda e: e.nop()), deps=deps, inc=1,
                         sync_same=True, name=name)

    def emit(self, block):
        nc = self.nc
        sems = {en: nc.alloc_semaphore(name=f"prog_{en}") for en in ENGS}
        cum = {en: 0 for en in ENGS}
        self.val_after = []
        for st in self.steps:
            cum[st["eng"]] += st["inc"]
            self.val_after.append(cum[st["eng"]])

        def body(eng_name):
            def run(eng):
                watermark = {en: 0 for en in ENGS}
                for i, st in enumerate(self.steps):
                    if st["eng"] != eng_name:
                        continue
                    for j in st["deps"]:
                        dj = self.steps[j]
                        if dj["eng"] == eng_name and not st["sync_same"]:
                            continue
                        v = self.val_after[j]
                        if watermark[dj["eng"]] >= v:
                            continue
                        eng.wait_ge(sems[dj["eng"]], v)
                        watermark[dj["eng"]] = v
                    inst = st["emit"](eng)
                    assert inst is not None, f"step {i} {st['name']} returned None"
                    inst.then_inc(sems[st["eng"]], st["inc"])
            return run

        block.sync(body("sync"))
        block.tensor(body("tensor"))
        block.vector(body("vector"))
        block.scalar(body("scalar"))

        def gps_body(eng):
            body("gpsimd")(eng)
            import os as _os
            _tail = _os.environ.get("PROG_TAIL", "full")
            if _tail in ("full", "waits"):
                for en in ENGS:
                    if cum[en] > 0 and en != "gpsimd":
                        eng.wait_ge(sems[en], cum[en])
            if _tail in ("full", "clears"):
                nums = sorted(sems[en].num for en in ENGS)
                assert nums == list(range(nums[0], nums[0] + len(nums))), nums
                rng = range(nums[0], nums[-1] + 1)
                eng.dma_reset(rng)
                eng.sem_clear(rng)
        block.gpsimd(gps_body)
        return sems


def piece_splits(ncols, nmax, bank_off=0):
    out = []
    c = 0
    while c < ncols:
        step = min(nmax, ncols - c, 512 - ((bank_off + c) % 512))
        out.append((c, c + step))
        c += step
    return out


def mm_group(psum_ap, pairs, nmax, start=True, stop=True, bank_off=0):
    def emit(eng):
        last = None
        for c0, c1 in piece_splits(psum_ap.shape[-1], nmax, bank_off):
            for k, (lhsT, rhs) in enumerate(pairs):
                last = eng.matmul(psum_ap[:, c0:c1], lhsT, rhs[:, c0:c1],
                                  start=(start and k == 0),
                                  stop=(stop and k == len(pairs) - 1))
        return last
    return emit


def build_launch1():
    nc = bass.Bass("TRN2", target_bir_lowering=False)

    def din(name, shape, dt=F32):
        return nc.dram_tensor(name, list(shape), dt, kind="ExternalInput").ap()

    def dout(name, shape, dt=F32):
        return nc.dram_tensor(name, list(shape), dt, kind="ExternalOutput").ap()

    xT = din("xT", [E, T])
    wqT = din("wqT", [E, E]); wkT = din("wkT", [E, E])
    wvT = din("wvT", [E, E]); wgT = din("wgT", [E, E])
    bq = din("bq", [NCH, 128, 1]); bk = din("bk", [NCH, 128, 1])
    bg = din("bg", [NCH, 128, 1]); bv_row = din("bv_row", [1, E])
    gamma = din("gamma", [1, H])
    gwT = din("gwT", [E, 32])
    coss = din("coss", [128, T]); sins = din("sins", [128, T])
    rotT2 = din("rotT2", [128, D])

    xn_out = dout("xn_out", [E, T], BF16)
    s0T = dout("s0T", [E, T], BF16)
    selp = dout("selp", [T, 32])

    a = nc.alloc_sbuf_tensor
    xn = [a(f"xn{c}", [128, T], BF16) for c in range(NCH)]
    Vt = [a(f"Vt{t}", [128, E], BF16) for t in range(NTT)]
    OT = [a(f"OT{c}", [128, T], BF16) for c in range(NCH)]
    f0 = a("f0", [128, T], F32); f1 = a("f1", [128, T], F32)
    f2 = a("f2", [128, T], F32); f3 = a("f3", [128, T], F32)
    b0 = a("b0", [128, T], BF16); b1 = a("b1", [128, T], BF16)
    b2 = a("b2", [128, T], BF16); b3 = a("b3", [128, T], BF16)
    wA = a("wA", [128, 8 * E], BF16)
    wB = a("wB", [128, 8 * E], BF16)
    tabs = a("tabs", [128, 2 * T], F32)
    rot_s = a("rot_s", [128, D], BF16)
    gw_s = a("gw_s", [128, 8 * 32], BF16)
    bias_s = a("bias_s", [128, 4], F32)
    bvb = a("bvb", [128, E], F32)
    ones_col = a("ones_col", [128, 1], BF16)
    ones_sq = a("ones_sq", [128, 128], BF16)
    ones_row = a("ones_row", [1, 128], F32)
    gsig = a("gsig", [1, H], F32)
    eps_t = a("eps_t", [1, 1], F32)
    gnb = a("gnb", [128, H], F32)
    lg_s = a("lg_s", [32, T], F32)
    idf = a("identS", [128, 128], F32)
    rt = {n: a("rt_" + n, [128, 32], F32)
          for n in ("ltok", "exp", "probs", "rep", "mask", "tmp", "sel")}
    rt_t8 = a("rt_t8", [128, 8], F32)
    rt_nmx = a("rt_nmx", [128, 1], F32)
    rt_sum = a("rt_sum", [128, 1], F32)
    rt_rcp = a("rt_rcp", [128, 1], F32)

    ps0 = nc.alloc_psum_tensor("ps0", [128, T], F32)
    ps1 = nc.alloc_psum_tensor("ps1", [128, T], F32)

    ident = nc.inline_tensor(np.eye(128, dtype=np.float32), name="identF32")

    p = Prog(nc)
    stores = []

    p.step("vector", lambda e: e.memset(ones_col[:, :], 1.0), writes=["ones_col"])
    p.step("vector", lambda e: e.memset(ones_sq[:, :], 1.0), writes=["ones_sq"])
    p.step("vector", lambda e: e.memset(eps_t[:, :], 1e-5), writes=["eps_t"])
    p.step("vector", lambda e: e.memset(ones_row[:, :], 1.0), writes=["ones_row"])
    p.dma("gpsimd", idf[:, :], ident.ap(), writes=["idf"], name="ident")
    for i, tsrc in enumerate((coss, sins)):
        p.dma("sync", tabs[:, i * T:(i + 1) * T], tsrc, writes=["tabs"])
    p.dma("gpsimd", rot_s[:, :], rotT2, writes=["rot_s"])
    p.dma("gpsimd", gw_s[:, :].rearrange("p (c e) -> p c e", c=NCH),
          gwT.rearrange("(c p) e -> p c e", p=128), writes=["gw_s"])
    p.dma("sync", gsig[0:1, 0:H], gamma, writes=["gsig"])
    p.step("scalar", lambda e: e.activation(gsig[0:1, :], gsig[0:1, :], AF.Sigmoid),
           reads=["gsig"], writes=["gsig"], sync_same=True)
    p.step("vector", lambda e: e.tensor_scalar(gsig[0:1, :], gsig[0:1, :], -1.0,
                                               None, op0=ALU.mult),
           reads=["gsig"], writes=["gsig"])
    p.step("tensor", mm_group(ps1[:, 0:H], [(ones_row[:, :], gsig[0:1, :])], MMF),
           reads=["ones_row", "gsig"], writes=["ps1"])
    p.step("scalar", lambda e: e.copy(gnb[:, :], ps1[:, 0:H]),
           reads=["ps1"], writes=["gnb"])
    p.dma("sync", f0[0:1, 0:E], bv_row, writes=["f0"])
    p.step("tensor", mm_group(ps0[:, 0:E], [(ones_row[:, :], f0[0:1, 0:E])], MMF),
           reads=["ones_row", "f0"], writes=["ps0"])
    p.step("scalar", lambda e: e.copy(bvb[:, :], ps0[:, 0:E]),
           reads=["ps0"], writes=["bvb"])

    for c in range(NCH):
        xf = (f0, f1)[c % 2]
        xb = (b0, b1)[c % 2]
        xq = (b2, b3)[c % 2]
        kf, kb, kq = ("f0", "f1")[c % 2], ("b0", "b1")[c % 2], ("b2", "b3")[c % 2]
        p.dma("sync", xf[:, :], xT[c * 128:(c + 1) * 128, :], writes=[kf],
              name=f"ldx{c}")
        p.step("vector", lambda e, xb=xb, xf=xf: e.tensor_copy(xb[:, :], xf[:, :]),
               reads=[kf], writes=[kb])
        p.step("scalar", lambda e, xq=xq, xb=xb: e.activation(xq[:, :], xb[:, :], AF.Square),
               reads=[kb], writes=[kq])
        p.step("tensor", mm_group(ps0[:, :], [(ones_sq[:, :], xb[:, :])], MMB,
                                  start=(c == 0), stop=(c == NCH - 1)),
               reads=[kb, "ones_sq"], writes=["ps0"], name=f"sx{c}")
        p.step("tensor", mm_group(ps1[:, :], [(ones_sq[:, :], xq[:, :])], MMB,
                                  start=(c == 0), stop=(c == NCH - 1)),
               reads=[kq, "ones_sq"], writes=["ps1"], name=f"sq{c}")

    p.step("vector", lambda e: e.tensor_scalar(f0[0:1, :], ps0[0:1, :], 1.0 / E,
                                               None, op0=ALU.mult),
           reads=["ps0"], writes=["f0"], name="mu")
    p.step("vector", lambda e: e.tensor_scalar(f1[0:1, :], ps1[0:1, :], 1.0 / E,
                                               None, op0=ALU.mult),
           reads=["ps1"], writes=["f1"], name="msq")
    p.step("vector", lambda e: e.tensor_mul(f2[0:1, :], f0[0:1, :], f0[0:1, :]),
           reads=["f0"], writes=["f2"], name="mu2")
    p.step("vector", lambda e: e.tensor_sub(f3[0:1, :], f1[0:1, :], f2[0:1, :]),
           reads=["f1", "f2"], writes=["f3"], name="var")
    p.step("scalar", lambda e: e.activation(f3[0:1, :], f3[0:1, :], AF.Sqrt,
                                            bias=eps_t[0:1, :]),
           reads=["f3", "eps_t"], writes=["f3"], name="sd")
    p.step("vector", lambda e: e.reciprocal(f2[0:1, :], f3[0:1, :]),
           reads=["f3"], writes=["f2"], name="rstd")
    p.step("vector", lambda e: e.tensor_mul(f1[0:1, :], f0[0:1, :], f2[0:1, :]),
           reads=["f0", "f2"], writes=["f1"], name="murstd")
    p.step("vector", lambda e: e.tensor_scalar(f1[0:1, :], f1[0:1, :], -1.0,
                                               None, op0=ALU.mult),
           reads=["f1"], writes=["f1"], sync_same=True, name="nmr")
    p.step("tensor", mm_group(ps0[:, :], [(ones_row[:, :], f2[0:1, :])], MMF),
           reads=["ones_row", "f2"], writes=["ps0"], name="bc_rstd")
    p.step("tensor", mm_group(ps1[:, :], [(ones_row[:, :], f1[0:1, :])], MMF),
           reads=["ones_row", "f1"], writes=["ps1"], name="bc_nmr")
    for c in range(NCH):
        xf = (f0, f1)[c % 2]
        kf = ("f0", "f1")[c % 2]
        p.dma("sync", xf[:, :], xT[c * 128:(c + 1) * 128, :], writes=[kf])
        p.step("vector", lambda e, xf=xf: e.tensor_mul(f2[:, :], xf[:, :], ps0[:, :]),
               reads=[kf, "ps0"], writes=["f2"])
        p.step("vector", lambda e, c=c: e.tensor_add(xn[c][:, :], f2[:, :], ps1[:, :]),
               reads=["f2", "ps1"], writes=[("xn", c)])
        stores.append(p.dma("sync", xn_out[c * 128:(c + 1) * 128, :], xn[c][:, :],
                            reads=[("xn", c)], name=f"st_xn{c}"))

    xn_keys = [("xn", c) for c in range(NCH)]

    for ic in range(NCH):
        p.dma("gpsimd", wA[:, ic * E:(ic + 1) * E],
              wvT[ic * 128:(ic + 1) * 128, :], writes=["wA"], name=f"ldwv{ic}")
    for tt in range(NTT):
        pairs = [(xn[ic][:, tt * 128:(tt + 1) * 128], wA[:, ic * E:(ic + 1) * E])
                 for ic in range(NCH)]
        p.step("tensor", mm_group(ps0[:, 0:E], pairs, MMB),
               reads=xn_keys + ["wA"], writes=["ps0"], name=f"vproj{tt}")
        p.step("vector", lambda e, tt=tt: e.scalar_tensor_tensor(
            Vt[tt][:, :], ps0[:, 0:E], 1.0, bvb[:, :], op0=ALU.mult, op1=ALU.add),
            reads=["ps0", "bvb"], writes=[("Vt", tt)], name=f"vevac{tt}")

    for ic in range(NCH):
        p.dma("gpsimd", wA[:, ic * E:(ic + 1) * E], wqT[ic * 128:(ic + 1) * 128, :],
              writes=["wA"], name=f"ldwq{ic}")
        p.dma("gpsimd", wB[:, ic * E:(ic + 1) * E], wkT[ic * 128:(ic + 1) * 128, :],
              writes=["wB"], name=f"ldwk{ic}")

    def proj_rot(oc, which, wbuf, wkey, btab, bcol, cos_off, sin_off, dst, dkey):
        p.dma("sync", bias_s[:, bcol:bcol + 1], btab[oc], writes=["bias_s"],
              name=f"ldb{which}{oc}")
        pairs = [(wbuf[:, ic * E + oc * 128: ic * E + oc * 128 + 128], xn[ic][:, :])
                 for ic in range(NCH)]
        p.step("tensor", mm_group(ps0[:, :], pairs, MMB),
               reads=[wkey] + xn_keys, writes=["ps0"], name=f"proj{which}{oc}")
        p.step("scalar", lambda e: e.activation(b2[:, :], ps0[:, :], AF.Identity,
                                                bias=bias_s[:, bcol:bcol + 1]),
               reads=["ps0", "bias_s"], writes=["b2"], name=f"evac{which}{oc}")
        for h2 in range(2):
            rows = slice(h2 * D, (h2 + 1) * D)
            p.step("tensor", mm_group(ps1[h2 * D:(h2 + 1) * D, :],
                                      [(rot_s[rows, :], b2[rows, :])], MMB),
                   reads=["rot_s", "b2"], writes=["ps1"], name=f"rot{which}{oc}_{h2}")
            p.step("vector", lambda e, rows=rows, co=cos_off: e.tensor_mul(
                f2[rows, :], b2[rows, :], tabs[rows, co * T:(co + 1) * T]),
                reads=["b2", "tabs"], writes=["f2"])
            p.step("vector", lambda e, rows=rows, so=sin_off: e.tensor_mul(
                f3[rows, :], ps1[rows, :], tabs[rows, so * T:(so + 1) * T]),
                reads=["ps1", "tabs"], writes=["f3"])
            p.step("vector", lambda e, dst=dst, rows=rows: e.tensor_add(
                dst[rows, :], f2[rows, :], f3[rows, :]),
                reads=["f2", "f3"], writes=[dkey], name=f"rotc{which}{oc}_{h2}")

    for oc in range(NCH):
        proj_rot(oc, "q", wA, "wA", bq, 0, 0, 1, b0, "b0")
        proj_rot(oc, "k", wB, "wB", bk, 1, 0, 1, b1, "b1")
        for h2 in range(2):
            h = 2 * oc + h2
            rows = slice(h2 * D, (h2 + 1) * D)
            for j in range(NTT):
                dbuf, kd = ((f0, "f0"), (f1, "f1"))[j % 2]
                tbuf, kt = ((f2, "f2"), (f3, "f3"))[j % 2]
                pbuf, kp = ((b2, "b2"), (b3, "b3"))[j % 2]
                p.step("tensor", mm_group(ps0[:, :],
                                          [(b1[rows, j * 128:(j + 1) * 128],
                                            b0[rows, :])], MMB),
                       reads=["b0", "b1"], writes=["ps0"], name=f"S{h}_{j}")
                p.step("gpsimd", lambda e, dbuf=dbuf, j=j: e.iota(
                    dbuf[:, :], pattern=[[1, T]], base=-128 * j,
                    channel_multiplier=-1, allow_small_or_imprecise_dtypes=True),
                    writes=[kd], name=f"iota{h}_{j}")
                p.step("vector", lambda e, tbuf=tbuf, dbuf=dbuf: e.tensor_relu(
                    tbuf[:, :], dbuf[:, :]), reads=[kd], writes=[kt])
                p.step("scalar", lambda e, tbuf=tbuf, h=h: e.activation(
                    tbuf[:, :], tbuf[:, :], AF.Exp, scale=gnb[:, h:h + 1]),
                    reads=[kt, "gnb"], writes=[kt], sync_same=True,
                    name=f"tw{h}_{j}")
                p.step("vector", lambda e, pbuf=pbuf, tbuf=tbuf: e.tensor_mul(
                    pbuf[:, :], ps0[:, :], tbuf[:, :]),
                    reads=["ps0", kt], writes=[kp], name=f"ptile{h}_{j}")
                p.step("tensor", mm_group(ps1[0:D, :],
                                          [(Vt[j][:, h * D:(h + 1) * D], pbuf[:, :])],
                                          MMB, start=(j == 0), stop=(j == NTT - 1)),
                       reads=[("Vt", j), kp], writes=["ps1"], name=f"O{h}_{j}")
            p.step("scalar", lambda e, oc=oc, rows=rows: e.copy(
                OT[oc][rows, :], ps1[0:D, :]),
                reads=["ps1"], writes=[("OT", oc)], name=f"oevac{h}")

    for ic in range(NCH):
        p.dma("gpsimd", wA[:, ic * E:(ic + 1) * E], wgT[ic * 128:(ic + 1) * 128, :],
              writes=["wA"], name=f"ldwg{ic}")
    for oc in range(NCH):
        p.dma("sync", bias_s[:, 2:3], bg[oc], writes=["bias_s"])
        pairs = [(wA[:, ic * E + oc * 128: ic * E + oc * 128 + 128], xn[ic][:, :])
                 for ic in range(NCH)]
        p.step("tensor", mm_group(ps0[:, :], pairs, MMB),
               reads=["wA"] + xn_keys, writes=["ps0"], name=f"gproj{oc}")
        p.step("scalar", lambda e: e.activation(b0[:, :], ps0[:, :], AF.Silu,
                                                bias=bias_s[:, 2:3]),
               reads=["ps0", "bias_s"], writes=["b0"], name=f"gevac{oc}")
        p.step("vector", lambda e, oc=oc: e.tensor_mul(b1[:, :], OT[oc][:, :], b0[:, :]),
               reads=[("OT", oc), "b0"], writes=["b1"], name=f"s0m{oc}")
        stores.append(p.dma("sync", s0T[oc * 128:(oc + 1) * 128, :], b1[:, :],
                            reads=["b1"], name=f"st_s0{oc}"))

    pairs = [(gw_s[:, ic * 32:(ic + 1) * 32], xn[ic][:, :]) for ic in range(NCH)]
    p.step("tensor", mm_group(ps0[0:32, :], pairs, MMB),
           reads=["gw_s"] + xn_keys, writes=["ps0"], name="router_mm")
    p.step("scalar", lambda e: e.copy(lg_s[:, :], ps0[0:32, :]),
           reads=["ps0"], writes=["lg_s"])
    for tt in range(NTT):
        p.step("tensor", lambda e, tt=tt: e.transpose(
            ps1[:, 0:32], lg_s[:, tt * 128:(tt + 1) * 128], idf[0:32, 0:32]),
            reads=["lg_s", "idf"], writes=["ps1"], name=f"rt_tr{tt}")
        p.step("scalar", lambda e: e.copy(rt["ltok"][:, :], ps1[0:128, 0:32]),
               reads=["ps1"], writes=["ltok"])
        p.step("vector", lambda e: e.tensor_reduce(
            rt_nmx[:, :], rt["ltok"][:, :], axis=AX.X, op=ALU.max, negate=True),
            reads=["ltok"], writes=["nmx"])
        p.step("scalar", lambda e: e.activation(
            rt["exp"][:, :], rt["ltok"][:, :], AF.Exp, bias=rt_nmx[:, :],
            accum_out=rt_sum[:, :]),
            reads=["ltok", "nmx"], writes=["rexp", "rsum"])
        p.step("vector", lambda e: e.reciprocal(rt_rcp[:, :], rt_sum[:, :]),
               reads=["rsum"], writes=["rrcp"])
        p.step("vector", lambda e: e.tensor_scalar(
            rt["probs"][:, :], rt["exp"][:, :], rt_rcp[:, :], None, op0=ALU.mult),
            reads=["rexp", "rrcp"], writes=["probs"])
        p.step("vector", lambda e: e.max(rt_t8[:, :], rt["probs"][:, :]),
               reads=["probs"], writes=["t8"], sync_same=True, name=f"max8_{tt}")
        p.step("vector", lambda e: e.match_replace(
            rt["rep"][:, :], rt_t8[:, :], rt["probs"][:, :], -1.0),
            reads=["t8", "probs"], writes=["rrep"], sync_same=True,
            name=f"mrep_{tt}")
        p.step("vector", lambda e: e.tensor_tensor(
            rt["mask"][:, :], rt["rep"][:, :], rt["probs"][:, :], op=ALU.is_lt),
            reads=["rrep", "probs"], writes=["rmask"])
        p.step("vector", lambda e: e.tensor_scalar(
            rt["tmp"][:, :], rt["probs"][:, :], 1.0, None, op0=ALU.add),
            reads=["probs"], writes=["rtmp"])
        p.step("vector", lambda e: e.tensor_mul(
            rt["tmp"][:, :], rt["tmp"][:, :], rt["mask"][:, :]),
            reads=["rtmp", "rmask"], writes=["rtmp"], sync_same=True)
        p.step("vector", lambda e: e.tensor_scalar(
            rt["sel"][:, :], rt["tmp"][:, :], -1.0, None, op0=ALU.add),
            reads=["rtmp"], writes=["rsel"])
        stores.append(p.dma("sync", selp[tt * 128:(tt + 1) * 128, :],
                            rt["sel"][:, :], reads=["rsel"], name=f"st_selp{tt}"))

    p.barrier_step("sync", stores)
    with nc.Block() as block:
        p.emit(block)
    return nc



def build_launch2(dbg=False):
    nc = bass.Bass("TRN2", target_bir_lowering=False)
    ikind = "ExternalOutput" if dbg else "Internal"

    def din(name, shape, dt=F32):
        return nc.dram_tensor(name, list(shape), dt, kind="ExternalInput").ap()

    xT = din("xT", [E, T])
    xn0T = din("xn0T", [E, T], BF16)
    s0Ti = din("s0Ti", [E, T], BF16)
    selp_all = din("selp_all", [16384, 32])
    selp_my = din("selp_my", [T, 32])
    maskb = din("maskb", [128, 128, 1])
    gupT_all = din("gupT_all", [32, E, 1536])
    downT_all = din("downT_all", [32, 768, E])
    wqkv = din("wqkv", [E, 3 * E])
    scale_r = din("scale_r", [1, H])
    wo = din("wo", [E, E])
    b_out = din("b_out", [NCH, 128, 1])
    WrT = din("WrT", [E, 8])
    br_col = din("br_col", [8, 1])
    W1T_all = din("W1T_all", [8, E, 4096])
    b1_r = din("b1_r", [8, 128, 32])
    W2T_all = din("W2T_all", [8, 4096, E])
    b2_r = din("b2_r", [8, 128, 8])
    lnw = din("lnw", [NCH, 128, 1])
    lnb = din("lnb", [NCH, 128, 1])
    triA = din("triA", [128, 128])
    triB = din("triB", [128, 128])
    iotaP = din("iotaP", [128, 1])

    outT = nc.dram_tensor("outT", [E, T], F32, kind="ExternalOutput").ap()

    x1_d = nc.dram_tensor("x1_d", [E, T], F32, kind=ikind)
    s1_d = nc.dram_tensor("s1_d", [E, T], BF16, kind=ikind)
    mrow_d = nc.dram_tensor("mrow_d", [16, 128], F32, kind="Internal")
    cw_d = nc.dram_tensor("cw_d", [32, T], F32, kind=ikind)
    cw1_d = nc.dram_tensor("cw1_d", [8, T], F32, kind=ikind)
    g0_d = (nc.dram_tensor("g0_d", [E, T], BF16, kind="ExternalOutput")
            if dbg else None)
    xn1_d = (nc.dram_tensor("xn1_d", [E, T], BF16, kind="ExternalOutput")
             if dbg else None)
    qst_d = (nc.dram_tensor("qst_d", [128, T], BF16, kind="ExternalOutput")
             if dbg else None)
    kst_d = (nc.dram_tensor("kst_d", [128, T], BF16, kind="ExternalOutput")
             if dbg else None)
    pt_d = (nc.dram_tensor("pt_d", [128, T], BF16, kind="ExternalOutput")
            if dbg else None)
    dn_d = (nc.dram_tensor("dn_d", [1, T], F32, kind="ExternalOutput")
            if dbg else None)
    at0_d = (nc.dram_tensor("at0_d", [128, T], BF16, kind="ExternalOutput")
             if dbg else None)
    at_d = (nc.dram_tensor("at_d", [E, T], BF16, kind="ExternalOutput")
            if dbg else None)
    wa_d = (nc.dram_tensor("wa_d", [128, 8 * E], BF16, kind="ExternalOutput")
            if dbg else None)

    a = nc.alloc_sbuf_tensor
    xn = [a(f"xn{c}", [128, T], BF16) for c in range(NCH)]
    P1 = [a(f"P1_{c}", [128, T], BF16) for c in range(NCH)]
    P2 = [a(f"P2_{c}", [128, T], BF16) for c in range(NCH)]
    wA = a("wA", [128, 8 * E], BF16)
    wB = a("wB", [128, 8 * E], BF16)
    f0 = a("f0", [128, T], F32); f1 = a("f1", [128, T], F32)
    cwb = a("cwb", [128, T], F32)
    qst = a("qst", [128, T], BF16); kst = a("kst", [128, T], BF16)
    b2 = a("b2", [128, T], BF16); b4 = a("b4", [128, T], BF16)
    b5 = a("b5", [128, T], BF16)
    w2r = a("w2r", [128, T], BF16)
    qkw = a("qkw", [128, 2 * E], BF16)
    l1s = a("l1s", [8, T], F32)
    ones_row = a("ones_row", [1, 128], F32)
    ones_sq = a("ones_sq", [128, 128], BF16)
    idf = a("identS", [128, 128], F32)
    sU = a("sU", [128, 128], BF16)
    tA = a("tA", [128, 128], F32)
    tB = a("tB", [128, 128], F32)
    iop = a("iop", [128, 1], F32)
    rsel = a("rsel", [128, 32], F32)
    rtile = a("rtile", [128, 32], BF16)
    rmask = a("rmask", [128, 32], BF16)
    mbcol = a("mbcol", [128, 1], F32)
    base_s = a("base_s", [128, 32], F32)
    brun_s = a("brun_s", [128, 32], F32)
    posw = a("posw", [128, 32], F32)
    kept = a("kept", [128, 32], F32)
    wsel = a("wsel", [128, 32], F32)
    wsum = a("wsum", [128, 1], F32)
    wrcp = a("wrcp", [128, 1], F32)
    cwt = a("cwt", [128, 32], F32)
    trv = a("trv", [32, 128], F32)
    crow = a("crow", [1, T], F32)
    bias_s = a("bias_s", [128, 4], F32)
    scvb = a("scvb", [128, H], F32)
    mcol = a("mcol", [128, 16], F32)
    msA = a("msA", [16, 128], F32)
    mx1 = a("mx1", [128, 1], F32)
    mx2 = a("mx2", [128, 1], F32)
    b1c = a("b1c", [128, 32], F32)
    b2c = a("b2c", [128, 8], F32)
    lt8 = a("lt8", [128, 8], F32)
    l2f = a("l2f", [128, 8], F32)
    eq1 = a("eq1", [128, 8], F32)
    eq2 = a("eq2", [128, 8], F32)
    cw1 = a("cw1", [128, 8], F32)
    m1c = a("m1c", [128, 1], F32)
    m2c = a("m2c", [128, 1], F32)
    w1c = a("w1c", [128, 1], F32)
    w2c = a("w2c", [128, 1], F32)
    eps2 = a("eps2", [1, 1], F32)
    gsr = a("gsr", [1, H], F32)

    ps0 = nc.alloc_psum_tensor("ps0", [128, T], F32)
    ps1 = nc.alloc_psum_tensor("ps1", [128, T], F32)

    ident = nc.inline_tensor(np.eye(128, dtype=np.float32), name="identF32")
    sU_np = np.triu(np.ones((128, 128), np.float32), 1).astype(BF)
    sU_c = nc.inline_tensor(sU_np, name="strictU")

    p = Prog(nc)
    stores = []
    xk = [("xn", c) for c in range(NCH)]

    p.step("vector", lambda e: e.memset(ones_row[:, :], 1.0), writes=["ones_row"])
    p.step("vector", lambda e: e.memset(ones_sq[:, :], 1.0), writes=["ones_sq"])
    p.step("vector", lambda e: e.memset(eps2[:, :], 1e-12), writes=["eps2"])
    p.step("vector", lambda e: e.memset(brun_s[:, :], 0.0), writes=["brun_s"])
    p.dma("gpsimd", idf[:, :], ident.ap(), writes=["idf"])
    p.dma("gpsimd", sU[:, :], sU_c.ap(), writes=["sU"])
    p.dma("sync", tA[:, :], triA, writes=["tA"])
    p.dma("sync", tB[:, :], triB, writes=["tB"])
    p.dma("sync", iop[:, :], iotaP, writes=["iop"])
    for c in range(NCH):
        p.dma("sync", xn[c][:, :], xn0T[c * 128:(c + 1) * 128, :], writes=[("xn", c)])

    for jc in range(128):
        p.dma("sync", rsel[:, :], selp_all[jc * 128:(jc + 1) * 128, :],
              writes=["rsel"], name=f"ldsel{jc}")
        p.dma("sync", mbcol[:, :], maskb[jc], writes=["mbcol"])
        p.step("vector", lambda e: e.tensor_scalar(rtile[:, :], rsel[:, :], -0.5,
                                                   None, op0=ALU.is_gt),
               reads=["rsel"], writes=["rtile"])
        p.step("vector", lambda e: e.tensor_scalar(rmask[:, :], rtile[:, :],
                                                   mbcol[:, :], None, op0=ALU.mult),
               reads=["rtile", "mbcol"], writes=["rmask"])
        p.step("tensor", mm_group(ps0[:, 0:32], [(ones_sq[:, :], rmask[:, :])], MMB,
                                  start=(jc == 0), stop=(jc == 127)),
               reads=["rmask", "ones_sq"], writes=["ps0"], name=f"base{jc}")
    p.step("scalar", lambda e: e.copy(base_s[:, :], ps0[:, 0:32]),
           reads=["ps0"], writes=["base_s"])

    for tt in range(NTT):
        p.dma("sync", rsel[:, :], selp_my[tt * 128:(tt + 1) * 128, :],
              writes=["rsel"], name=f"ldselmy{tt}")
        p.step("vector", lambda e: e.tensor_scalar(rtile[:, :], rsel[:, :], -0.5,
                                                   None, op0=ALU.is_gt),
               reads=["rsel"], writes=["rtile"])
        p.step("tensor", mm_group(ps0[:, 32:64], [(sU[:, :], rtile[:, :])], MMB),
               reads=["rtile", "sU"], writes=["ps0"], name=f"posw{tt}")
        p.step("scalar", lambda e: e.copy(posw[:, :], ps0[:, 32:64]),
               reads=["ps0"], writes=["posw"])
        p.step("vector", lambda e: e.tensor_add(posw[:, :], posw[:, :], brun_s[:, :]),
               reads=["posw", "brun_s"], writes=["posw"])
        p.step("vector", lambda e: e.tensor_add(posw[:, :], posw[:, :], base_s[:, :]),
               reads=["posw", "base_s"], writes=["posw"])
        p.step("tensor", mm_group(ps1[:, 0:32], [(ones_sq[:, :], rtile[:, :])], MMB,
                                  start=(tt == 0), stop=(tt == NTT - 1)),
               reads=["rtile", "ones_sq"], writes=["ps1"], name=f"brun{tt}")
        p.step("scalar", lambda e: e.copy(brun_s[:, :], ps1[:, 0:32]),
               reads=["ps1"], writes=["brun_s"])
        p.step("vector", lambda e: e.tensor_scalar(kept[:, :], posw[:, :], 1023.5,
                                                   None, op0=ALU.is_le),
               reads=["posw"], writes=["kept"])
        p.step("vector", lambda e: e.tensor_relu(wsel[:, :], rsel[:, :]),
               reads=["rsel"], writes=["wsel"])
        p.step("vector", lambda e: e.tensor_reduce(wsum[:, :], wsel[:, :], axis=AX.X,
                                                   op=ALU.add),
               reads=["wsel"], writes=["wsum"])
        p.step("vector", lambda e: e.reciprocal(wrcp[:, :], wsum[:, :]),
               reads=["wsum"], writes=["wrcp"])
        p.step("vector", lambda e: e.tensor_scalar(wsel[:, :], wsel[:, :], wrcp[:, :],
                                                   None, op0=ALU.mult),
               reads=["wsel", "wrcp"], writes=["wsel"])
        p.step("vector", lambda e: e.tensor_mul(cwt[:, :], wsel[:, :], kept[:, :]),
               reads=["wsel", "kept"], writes=["cwt"])
        p.step("tensor", lambda e: e.transpose(ps1[0:32, 32:160], cwt[:, :], idf[:, :]),
               reads=["cwt", "idf"], writes=["ps1"], name=f"cwtr{tt}")
        p.step("scalar", lambda e: e.copy(trv[:, :], ps1[0:32, 32:160]),
               reads=["ps1"], writes=["trv"])
        stores.append(p.dma("sync", cw_d.ap()[:, tt * 128:(tt + 1) * 128], trv[:, :],
                            reads=["trv"], writes=["cw_d"], name=f"stcw{tt}"))

    for oc in range(NCH):
        p.step("vector", lambda e, oc=oc: e.memset(P1[oc][:, :], 0.0),
               writes=[("P1", oc)])
    for ee in range(32):
        p.dma("sync", crow[0:1, :], cw_d.ap()[ee:ee + 1, :], reads=["cw_d"],
              writes=["crow"], name=f"ldcw{ee}")
        p.step("tensor", mm_group(ps0[:, :], [(ones_row[:, :], crow[0:1, :])], MMF),
               reads=["ones_row", "crow"], writes=["ps0"], name=f"cwb{ee}")
        p.step("scalar", lambda e: e.copy(cwb[:, :], ps0[:, :]),
               reads=["ps0"], writes=["cwb"])
        for g3 in range(3):
            half = (g3 % 2)
            for ic in range(NCH):
                p.dma("gpsimd", wA[:, half * 4096 + ic * 512: half * 4096 + (ic + 1) * 512],
                      gupT_all[ee, ic * 128:(ic + 1) * 128, g3 * 512:(g3 + 1) * 512],
                      writes=[("wA", half)], name=f"gup{ee}_{g3}_{ic}")
            for k4 in range(4):
                fc = g3 * 4 + k4
                pairs = [(wA[:, half * 4096 + ic * 512 + k4 * 128:
                             half * 4096 + ic * 512 + k4 * 128 + 128], xn[ic][:, :])
                         for ic in range(NCH)]
                p.step("tensor", mm_group(ps1[:, :], pairs, MMB),
                       reads=[("wA", half)] + xk, writes=["ps1"], name=f"gu{ee}_{fc}")
                if fc < 6:
                    p.step("scalar", lambda e, fc=fc: e.activation(
                        P2[fc][:, :], ps1[:, :], AF.Silu),
                        reads=["ps1"], writes=[("P2", fc)], name=f"gate{ee}_{fc}")
                else:
                    p.step("vector", lambda e, fc=fc: e.tensor_mul(
                        P2[fc - 6][:, :], P2[fc - 6][:, :], ps1[:, :]),
                        reads=["ps1", ("P2", fc - 6)], writes=[("P2", fc - 6)],
                        name=f"hh{ee}_{fc}")
        for hc in range(6):
            p.dma("gpsimd", wB[:, hc * E:(hc + 1) * E],
                  downT_all[ee, hc * 128:(hc + 1) * 128, :], writes=["wB"],
                  name=f"dwn{ee}_{hc}")
        for oc in range(NCH):
            pairs = [(wB[:, hc * E + oc * 128: hc * E + oc * 128 + 128], P2[hc][:, :])
                     for hc in range(6)]
            p.step("tensor", mm_group(ps1[:, :], pairs, MMB),
                   reads=["wB"] + [("P2", hc) for hc in range(6)], writes=["ps1"],
                   name=f"eo{ee}_{oc}")
            p.step("vector", lambda e: e.tensor_mul(b4[:, :], ps1[:, :], cwb[:, :]),
                   reads=["ps1", "cwb"], writes=["b4"])
            p.step("vector", lambda e, oc=oc: e.tensor_add(
                P1[oc][:, :], P1[oc][:, :], b4[:, :]),
                reads=["b4", ("P1", oc)], writes=[("P1", oc)], name=f"acc{ee}_{oc}")

    for oc in range(NCH):
        p.step("scalar", lambda e, oc=oc: e.activation(P1[oc][:, :], P1[oc][:, :], AF.Silu),
               reads=[("P1", oc)], writes=[("P1", oc)], name=f"g0_{oc}")
        if dbg:
            stores.append(p.dma("sync", g0_d.ap()[oc * 128:(oc + 1) * 128, :],
                                P1[oc][:, :], reads=[("P1", oc)]))
        p.dma("sync", f0[:, :], xT[oc * 128:(oc + 1) * 128, :], writes=["f0"])
        p.dma("sync", b2[:, :], s0Ti[oc * 128:(oc + 1) * 128, :], writes=["b2"])
        p.step("vector", lambda e, oc=oc: e.tensor_mul(b4[:, :], b2[:, :], P1[oc][:, :]),
               reads=["b2", ("P1", oc)], writes=["b4"])
        p.step("vector", lambda e: e.tensor_add(f1[:, :], f0[:, :], b4[:, :]),
               reads=["f0", "b4"], writes=["f1"])
        stores.append(p.dma("sync", x1_d.ap()[oc * 128:(oc + 1) * 128, :], f1[:, :],
                            reads=["f1"], writes=[("x1_d", oc)], name=f"stx1{oc}"))
        p.step("vector", lambda e: e.tensor_copy(b4[:, :], f1[:, :]),
               reads=["f1"], writes=["b4"])
        p.step("scalar", lambda e: e.activation(b5[:, :], b4[:, :], AF.Square),
               reads=["b4"], writes=["b5"])
        p.step("tensor", mm_group(ps0[:, :], [(ones_sq[:, :], b4[:, :])], MMB,
                                  start=(oc == 0), stop=(oc == NCH - 1)),
               reads=["b4", "ones_sq"], writes=["ps0"], name=f"s1x{oc}")
        p.step("tensor", mm_group(ps1[:, :], [(ones_sq[:, :], b5[:, :])], MMB,
                                  start=(oc == 0), stop=(oc == NCH - 1)),
               reads=["b5", "ones_sq"], writes=["ps1"], name=f"s1q{oc}")

    def ln_finalize(eps_ap, eps_key, name):
        p.step("vector", lambda e: e.tensor_scalar(f0[0:1, :], ps0[0:1, :], 1.0 / E,
                                                   None, op0=ALU.mult),
               reads=["ps0"], writes=["f0"], name=f"{name}mu")
        p.step("vector", lambda e: e.tensor_scalar(f1[0:1, :], ps1[0:1, :], 1.0 / E,
                                                   None, op0=ALU.mult),
               reads=["ps1"], writes=["f1"], name=f"{name}msq")
        p.step("vector", lambda e: e.tensor_mul(crow[0:1, :], f0[0:1, :], f0[0:1, :]),
               reads=["f0"], writes=["crow"])
        p.step("vector", lambda e: e.tensor_sub(f1[0:1, :], f1[0:1, :], crow[0:1, :]),
               reads=["f1", "crow"], writes=["f1"])
        p.step("scalar", lambda e: e.activation(f1[0:1, :], f1[0:1, :], AF.Sqrt,
                                                bias=eps_ap),
               reads=["f1", eps_key], writes=["f1"], name=f"{name}sd")
        p.step("vector", lambda e: e.reciprocal(crow[0:1, :], f1[0:1, :]),
               reads=["f1"], writes=["crow"], name=f"{name}rstd")
        p.step("vector", lambda e: e.tensor_mul(f0[0:1, :], f0[0:1, :], crow[0:1, :]),
               reads=["f0", "crow"], writes=["f0"])
        p.step("vector", lambda e: e.tensor_scalar(f0[0:1, :], f0[0:1, :], -1.0,
                                                   None, op0=ALU.mult),
               reads=["f0"], writes=["f0"])
        p.step("tensor", mm_group(ps0[:, :], [(ones_row[:, :], crow[0:1, :])], MMF),
               reads=["ones_row", "crow"], writes=["ps0"], name=f"{name}bcr")
        p.step("tensor", mm_group(ps1[:, :], [(ones_row[:, :], f0[0:1, :])], MMF),
               reads=["ones_row", "f0"], writes=["ps1"], name=f"{name}bcn")

    eps5 = a("eps5", [1, 1], F32)
    p.step("vector", lambda e: e.memset(eps5[:, :], 1e-5), writes=["eps5"])
    ln_finalize(eps5[0:1, :], "eps5", "ln1")
    for oc in range(NCH):
        p.dma("sync", f0[:, :], x1_d.ap()[oc * 128:(oc + 1) * 128, :],
              reads=[("x1_d", oc)], writes=["f0"])
        p.step("vector", lambda e: e.tensor_mul(f1[:, :], f0[:, :], ps0[:, :]),
               reads=["f0", "ps0"], writes=["f1"])
        p.step("vector", lambda e, oc=oc: e.tensor_add(xn[oc][:, :], f1[:, :], ps1[:, :]),
               reads=["f1", "ps1"], writes=[("xn", oc)], name=f"xn1_{oc}")
        if dbg:
            stores.append(p.dma("sync", xn1_d.ap()[oc * 128:(oc + 1) * 128, :],
                                xn[oc][:, :], reads=[("xn", oc)]))

    for ic in range(NCH):
        p.dma("gpsimd", wB[:, ic * 512:(ic + 1) * 512],
              wqkv[ic * 128:(ic + 1) * 128, 2 * E:2 * E + 512], writes=["wB"],
              name=f"ldwv_a{ic}")
        p.dma("gpsimd", wB[:, 4096 + ic * 512: 4096 + (ic + 1) * 512],
              wqkv[ic * 128:(ic + 1) * 128, 2 * E + 512:3 * E], writes=["wB"],
              name=f"ldwv_b{ic}")
    for tt in range(NTT):
        pairs = [(xn[ic][:, tt * 128:(tt + 1) * 128], wB[:, ic * 512:(ic + 1) * 512])
                 for ic in range(NCH)]
        pairs2 = [(xn[ic][:, tt * 128:(tt + 1) * 128],
                   wB[:, 4096 + ic * 512:4096 + (ic + 1) * 512]) for ic in range(NCH)]
        p.step("tensor", mm_group(ps0[:, 0:512], pairs, MMB),
               reads=["wB"] + xk, writes=["ps0"], name=f"v1a{tt}")
        p.step("tensor", mm_group(ps0[:, 512:1024], pairs2, MMB),
               reads=["wB"] + xk, writes=["ps0"], name=f"v1b{tt}")
        p.step("scalar", lambda e, tt=tt: e.copy(
            P1[tt // 2][:, (tt % 2) * 1024:(tt % 2) * 1024 + 1024], ps0[:, 0:1024]),
            reads=["ps0"], writes=[("P1", tt // 2)], name=f"v1e{tt}")

    p.dma("sync", gsr[0:1, 0:H], scale_r, writes=["gsr"])
    p.step("tensor", mm_group(ps1[:, 0:H], [(ones_row[:, :], gsr[0:1, :])], MMF),
           reads=["ones_row", "gsr"], writes=["ps1"])
    p.step("scalar", lambda e: e.copy(scvb[:, :], ps1[:, 0:H]),
           reads=["ps1"], writes=["scvb"])
    p.step("vector", lambda e: e.memset(kst[64:65, :], 1.0), writes=["kstones"])

    def v1ap(j, h):
        return P1[j // 2][:, (j % 2) * 1024 + h * 64:(j % 2) * 1024 + h * 64 + 64]

    for h in range(H):
        oc, h2 = h // 2, h % 2
        for ic in range(NCH):
            p.dma("gpsimd", qkw[:, ic * 64:(ic + 1) * 64],
                  wqkv[ic * 128:(ic + 1) * 128, h * 64:(h + 1) * 64],
                  writes=["qkw"], name=f"ldq{h}_{ic}")
            p.dma("gpsimd", qkw[:, 512 + ic * 64: 512 + (ic + 1) * 64],
                  wqkv[ic * 128:(ic + 1) * 128, E + h * 64:E + (h + 1) * 64],
                  writes=["qkw"], name=f"ldk{h}_{ic}")
        pairsQ = [(qkw[:, ic * 64:(ic + 1) * 64], xn[ic][:, :]) for ic in range(NCH)]
        pairsK = [(qkw[:, 512 + ic * 64:512 + (ic + 1) * 64], xn[ic][:, :])
                  for ic in range(NCH)]
        p.step("tensor", mm_group(ps0[0:64, :], pairsQ, MMB),
               reads=["qkw"] + xk, writes=["ps0"], name=f"qproj{h}")
        p.step("scalar", lambda e, h=h: e.activation(
            qst[0:64, :], ps0[0:64, :], AF.Identity, scale=scvb[0:64, h:h + 1]),
            reads=["ps0", "scvb"], writes=["qst"], name=f"qev{h}")
        p.step("tensor", mm_group(ps0[0:64, :], pairsK, MMB),
               reads=["qkw"] + xk, writes=["ps0"], name=f"kproj{h}")
        p.step("scalar", lambda e: e.copy(kst[0:64, :], ps0[0:64, :]),
               reads=["ps0"], writes=["kst"], name=f"kev{h}")
        for i in range(NTT):
            p.step("tensor", mm_group(ps0[:, :],
                                      [(qst[0:64, i * 128:(i + 1) * 128], kst[0:64, :])],
                                      MMB),
                   reads=["qst", "kst", "kstones"], writes=["ps0"], name=f"SA{h}_{i}")
            p.step("vector", lambda e, i=i: e.tensor_add(
                f0[:, 0:128], ps0[:, i * 128:(i + 1) * 128], tA[:, :]),
                reads=["ps0", "tA"], writes=["f0"])
            p.step("vector", lambda e: e.tensor_reduce(mx2[:, :], f0[:, 0:128],
                                                       axis=AX.X, op=ALU.max),
                   reads=["f0"], writes=["mx2"])
            if i > 0:
                p.step("vector", lambda e, i=i: e.tensor_reduce(
                    mx1[:, :], ps0[:, 0:i * 128], axis=AX.X, op=ALU.max),
                    reads=["ps0"], writes=["mx1"])
                p.step("vector", lambda e: e.tensor_max(mx2[:, :], mx2[:, :], mx1[:, :]),
                       reads=["mx1", "mx2"], writes=["mx2"])
            p.step("vector", lambda e, i=i: e.tensor_scalar(
                mcol[:, i:i + 1], mx2[:, :], -1e30, -1.0, op0=ALU.max, op1=ALU.mult),
                reads=["mx2"], writes=["mcol"], name=f"mneg{h}_{i}")
        p.step("tensor", lambda e: e.transpose(ps1[0:16, 0:128], mcol[:, :], idf[:, :]),
               reads=["mcol", "idf"], writes=["ps1"], name=f"mtr{h}")
        p.step("scalar", lambda e: e.copy(msA[:, :], ps1[0:16, 0:128]),
               reads=["ps1"], writes=["msA"], name=f"msev{h}")
        sid = p.dma("sync", mrow_d.ap(), msA[:, :], reads=["msA"],
                    writes=["mrow_d"], name=f"stm{h}")
        p.dma("gpsimd", qst[64:65, :],
              mrow_d.ap().rearrange("a b -> (a b)")[None, :], reads=["mrow_d"],
              writes=["qstm"], deps=[sid], name=f"ldm{h}")
        for j in range(NTT):
            n0 = j * 128
            p.step("tensor", mm_group(ps0[:, n0:T],
                                      [(kst[0:65, n0:n0 + 128], qst[0:65, n0:T])], MMB,
                                      bank_off=n0),
                   reads=["qst", "kst", "qstm", "kstones"], writes=["ps0"],
                   name=f"SB{h}_{j}")
            p.step("vector", lambda e, n0=n0: e.tensor_add(
                f0[:, 0:128], ps0[:, n0:n0 + 128], tB[:, :]),
                reads=["ps0", "tB"], writes=["f0"])
            p.step("scalar", lambda e, n0=n0: e.activation(
                b2[:, n0:n0 + 128], f0[:, 0:128], AF.Exp),
                reads=["f0"], writes=["b2"], name=f"expd{h}_{j}")
            if j < NTT - 1:
                p.step("scalar", lambda e, n0=n0: e.activation(
                    b2[:, n0 + 128:T], ps0[:, n0 + 128:T], AF.Exp),
                    reads=["ps0"], writes=["b2"], name=f"expr{h}_{j}")
            p.step("tensor", mm_denom(ps1, ones_sq, b2, n0, j),
                   reads=["b2", "ones_sq"], writes=["ps1"], name=f"dn{h}_{j}")
            p.step("tensor", mm_o(ps1, v1ap(j, h), b2, n0, j),
                   reads=["b2", ("P1", j // 2)], writes=["ps1"], name=f"O{h}_{j}")
        p.step("scalar", lambda e: e.copy(f0[0:1, :], ps1[0:1, :]),
               reads=["ps1"], writes=["f0"], name=f"dnev{h}")
        p.step("vector", lambda e: e.tensor_scalar(f1[0:1, :], f0[0:1, :], 0.0, None,
                                                   op0=ALU.is_equal),
               reads=["f0"], writes=["f1"])
        p.step("vector", lambda e: e.tensor_add(f0[0:1, :], f0[0:1, :], f1[0:1, :]),
               reads=["f0", "f1"], writes=["f0"])
        p.step("vector", lambda e: e.reciprocal(f0[0:1, :], f0[0:1, :]),
               reads=["f0"], writes=["f0"], name=f"rcp{h}")
        p.step("tensor", mm_group(ps0[:, :], [(ones_row[:, :], f0[0:1, :])], MMF),
               reads=["ones_row", "f0"], writes=["ps0"], name=f"rbb{h}")
        p.step("scalar", lambda e: e.copy(b4[:, :], ps0[:, :]),
               reads=["ps0"], writes=["b4"], name=f"rbe{h}")
        if dbg and h == 0:
            stores.append(p.dma("sync", qst_d.ap(), qst[:, :],
                                reads=["qst", "qstm"]))
            stores.append(p.dma("sync", kst_d.ap(), kst[:, :],
                                reads=["kst", "kstones"]))
            stores.append(p.dma("sync", pt_d.ap(), b2[:, :], reads=["b2"]))
            stores.append(p.dma("sync", dn_d.ap(), f0[0:1, :], reads=["f0"]))
        rows = slice(h2 * 64, h2 * 64 + 64)
        p.step("scalar", lambda e, rows=rows: e.copy(b5[rows, :], ps1[64:128, :]),
               reads=["ps1"], writes=["b5"], name=f"oev{h}")
        p.step("vector", lambda e, oc=oc, rows=rows: e.tensor_mul(
            P2[oc][rows, :], b5[rows, :], b4[rows, :]),
            reads=["b5", "b4"], writes=[("P2", oc)], name=f"onorm{h}")
        if dbg and h == 0:
            stores.append(p.dma("sync", at0_d.ap(), P2[0][:, :],
                                reads=[("P2", 0)]))

    if dbg:
        for ic in range(NCH):
            stores.append(p.dma("sync", at_d.ap()[ic * 128:(ic + 1) * 128, :],
                                P2[ic][:, :], reads=[("P2", ic)]))
    for ic in range(NCH):
        p.dma("gpsimd", wA[:, ic * E:(ic + 1) * E], wo[ic * 128:(ic + 1) * 128, :],
              writes=[("wA", 0), ("wA", 1)], name=f"ldwo{ic}")
    if dbg:
        stores.append(p.dma("sync", wa_d.ap(), wA[:, :], reads=[("wA", 0), ("wA", 1)]))
    for oc in range(NCH):
        p.dma("sync", bias_s[:, 0:1], b_out[oc], writes=["bias_s"])
        pairs = [(wA[:, ic * E + oc * 128: ic * E + oc * 128 + 128], P2[ic][:, :])
                 for ic in range(NCH)]
        p.step("tensor", mm_group(ps0[:, :], pairs, MMB),
               reads=[("wA", 0), ("wA", 1)] + [("P2", ic) for ic in range(NCH)],
               writes=["ps0"], name=f"oproj{oc}")
        p.step("scalar", lambda e: e.activation(b4[:, :], ps0[:, :], AF.Identity,
                                                bias=bias_s[:, 0:1]),
               reads=["ps0", "bias_s"], writes=["b4"], name=f"opev{oc}")
        stores.append(p.dma("sync", s1_d.ap()[oc * 128:(oc + 1) * 128, :], b4[:, :],
                            reads=["b4"], writes=[("s1_d", oc)], name=f"sts1{oc}"))

    for ic in range(NCH):
        p.dma("gpsimd", qkw[:, 1024 + ic * 8:1024 + (ic + 1) * 8],
              WrT[ic * 128:(ic + 1) * 128, :], writes=["qkw"], name=f"ldwr{ic}")
    p.dma("sync", bias_s[0:8, 1:2], br_col, writes=["bias_s"])
    pairs = [(qkw[:, 1024 + ic * 8:1024 + (ic + 1) * 8], xn[ic][:, :])
             for ic in range(NCH)]
    p.step("tensor", mm_group(ps0[0:8, :], pairs, MMB),
           reads=["qkw"] + xk, writes=["ps0"], name="r1mm")
    p.step("scalar", lambda e: e.activation(l1s[:, :], ps0[0:8, :], AF.Identity,
                                            bias=bias_s[0:8, 1:2]),
           reads=["ps0", "bias_s"], writes=["l1s"], name="r1ev")
    for tt in range(NTT):
        p.step("tensor", lambda e, tt=tt: e.transpose(
            ps1[:, 0:8], l1s[:, tt * 128:(tt + 1) * 128], idf[0:8, 0:8]),
            reads=["l1s", "idf"], writes=["ps1"], name=f"r1tr{tt}")
        p.step("scalar", lambda e: e.copy(lt8[:, :], ps1[0:128, 0:8]),
               reads=["ps1"], writes=["lt8"])
        p.step("vector", lambda e: e.tensor_reduce(m1c[:, :], lt8[:, :], axis=AX.X,
                                                   op=ALU.max),
               reads=["lt8"], writes=["m1c"])
        p.step("vector", lambda e: e.tensor_scalar(eq1[:, :], lt8[:, :], m1c[:, :],
                                                   None, op0=ALU.is_ge),
               reads=["lt8", "m1c"], writes=["eq1"])
        p.step("vector", lambda e: e.scalar_tensor_tensor(
            l2f[:, :], eq1[:, :], -1e30, lt8[:, :], op0=ALU.mult, op1=ALU.add),
            reads=["eq1", "lt8"], writes=["l2f"])
        p.step("vector", lambda e: e.tensor_reduce(m2c[:, :], l2f[:, :], axis=AX.X,
                                                   op=ALU.max),
               reads=["l2f"], writes=["m2c"])
        p.step("vector", lambda e: e.tensor_sub(w1c[:, :], m1c[:, :], m2c[:, :]),
               reads=["m1c", "m2c"], writes=["w1c"])
        p.step("scalar", lambda e: e.activation(w1c[:, :], w1c[:, :], AF.Sigmoid),
               reads=["w1c"], writes=["w1c"])
        p.step("vector", lambda e: e.tensor_scalar(w2c[:, :], w1c[:, :], -1.0, 1.0,
                                                   op0=ALU.mult, op1=ALU.add),
               reads=["w1c"], writes=["w2c"])
        p.step("vector", lambda e: e.tensor_scalar(eq2[:, :], l2f[:, :], m2c[:, :],
                                                   None, op0=ALU.is_ge),
               reads=["l2f", "m2c"], writes=["eq2"])
        p.step("vector", lambda e: e.tensor_scalar(cw1[:, :], eq1[:, :], w1c[:, :],
                                                   None, op0=ALU.mult),
               reads=["eq1", "w1c"], writes=["cw1"])
        p.step("vector", lambda e: e.tensor_scalar(eq2[:, :], eq2[:, :], w2c[:, :],
                                                   None, op0=ALU.mult),
               reads=["eq2", "w2c"], writes=["eq2"])
        p.step("vector", lambda e: e.tensor_add(cw1[:, :], cw1[:, :], eq2[:, :]),
               reads=["cw1", "eq2"], writes=["cw1"])
        p.step("tensor", lambda e, tt=tt: e.transpose(ps1[0:8, 8:136], cw1[:, :], idf[:, :]),
               reads=["cw1", "idf"], writes=["ps1"], name=f"cw1tr{tt}")
        p.step("scalar", lambda e: e.copy(trv[0:8, :], ps1[0:8, 8:136]),
               reads=["ps1"], writes=["trv"])
        stores.append(p.dma("sync", cw1_d.ap()[:, tt * 128:(tt + 1) * 128],
                            trv[0:8, :], reads=["trv"], writes=["cw1_d"],
                            name=f"stcw1{tt}"))

    for oc in range(NCH):
        p.step("vector", lambda e, oc=oc: e.memset(P2[oc][:, :], 0.0),
               writes=[("P2", oc)], name=f"acc1z{oc}")

    for ee in range(8):
        p.dma("sync", crow[0:1, :], cw1_d.ap()[ee:ee + 1, :], reads=["cw1_d"],
              writes=["crow"], name=f"ldc1{ee}")
        p.step("tensor", mm_group(ps0[:, :], [(ones_row[:, :], crow[0:1, :])], MMF),
               reads=["ones_row", "crow"], writes=["ps0"], name=f"c1b{ee}")
        p.step("scalar", lambda e: e.copy(cwb[:, :], ps0[:, :]),
               reads=["ps0"], writes=["cwb"], name=f"c1e{ee}")
        p.dma("sync", b1c[:, :], b1_r[ee], writes=["b1c"])
        p.dma("sync", b2c[:, :], b2_r[ee], writes=["b2c"])
        for q in range(4):
            q0 = q * 512
            for fcg in range(8):
                half = fcg % 2
                for ic in range(NCH):
                    p.dma("gpsimd",
                          wA[:, half * 4096 + ic * 512: half * 4096 + (ic + 1) * 512],
                          W1T_all[ee, ic * 128:(ic + 1) * 128,
                                  fcg * 512:(fcg + 1) * 512],
                          writes=[("wA", half)], name=f"w1{ee}_{q}_{fcg}_{ic}")
                for k4 in range(4):
                    fc = fcg * 4 + k4
                    pairs = [(wA[:, half * 4096 + ic * 512 + k4 * 128:
                                 half * 4096 + ic * 512 + k4 * 128 + 128],
                              xn[ic][:, q0:q0 + 512]) for ic in range(NCH)]
                    p.step("tensor", mm_group(ps0[:, 0:512], pairs, MMB),
                           reads=[("wA", half)] + xk, writes=["ps0"],
                           name=f"h{ee}_{q}_{fc}")
                    p.step("scalar", lambda e, fc=fc: e.activation(
                        P1[fc // 4][:, (fc % 4) * 512:(fc % 4) * 512 + 512],
                        ps0[:, 0:512], AF.Gelu, bias=b1c[:, fc:fc + 1]),
                        reads=["ps0", "b1c"], writes=[("P1", fc // 4)],
                        name=f"gelu{ee}_{q}_{fc}")
            for fc in range(32):
                half = fc % 2
                p.dma("gpsimd", w2r[:, half * 1024:(half + 1) * 1024],
                      W2T_all[ee, fc * 128:(fc + 1) * 128, :],
                      writes=[("w2r", half)], name=f"w2{ee}_{q}_{fc}")
                def w2emit(e, half=half, fc=fc):
                    last = None
                    for oc in range(NCH):
                        psv = (ps0 if oc < 4 else ps1)
                        col = (oc % 4) * 512
                        last = e.matmul(
                            psv[:, col:col + 512],
                            w2r[:, half * 1024 + oc * 128: half * 1024 + oc * 128 + 128],
                            P1[fc // 4][:, (fc % 4) * 512:(fc % 4) * 512 + 512],
                            start=(fc == 0), stop=(fc == 31))
                    return last
                p.step("tensor", w2emit,
                       reads=[("w2r", half), ("P1", fc // 4)],
                       writes=["ps0", "ps1"], name=f"w2m{ee}_{q}_{fc}")
            for oc in range(NCH):
                psv = (ps0 if oc < 4 else ps1)
                col = (oc % 4) * 512
                p.step("scalar", lambda e, psv=psv, col=col, oc=oc: e.activation(
                    f0[:, 0:512], psv[:, col:col + 512], AF.Identity,
                    bias=b2c[:, oc:oc + 1]),
                    reads=["ps0" if oc < 4 else "ps1", "b2c"], writes=["f0"], name=f"ye{ee}_{q}_{oc}")
                p.step("vector", lambda e, q0=q0: e.tensor_mul(
                    b4[:, 0:512], f0[:, 0:512], cwb[:, q0:q0 + 512]),
                    reads=["f0", "cwb"], writes=["b4"])
                p.step("vector", lambda e, oc=oc, q0=q0: e.tensor_add(
                    P2[oc][:, q0:q0 + 512], P2[oc][:, q0:q0 + 512], b4[:, 0:512]),
                    reads=["b4", ("P2", oc)], writes=[("P2", oc)],
                    name=f"a1{ee}_{q}_{oc}")

    for oc in range(NCH):
        p.step("vector", lambda e, oc=oc: e.tensor_add(
            P1[oc][:, :], P2[oc][:, :], xn[oc][:, :]),
            reads=[("P2", oc), ("xn", oc)], writes=[("P1", oc)], name=f"z{oc}")
        p.step("scalar", lambda e, oc=oc: e.activation(b5[:, :], P1[oc][:, :], AF.Square),
               reads=[("P1", oc)], writes=["b5"])
        p.step("tensor", mm_group(ps0[:, :], [(ones_sq[:, :], P1[oc][:, :])], MMB,
                                  start=(oc == 0), stop=(oc == NCH - 1)),
               reads=[("P1", oc), "ones_sq"], writes=["ps0"], name=f"s2x{oc}")
        p.step("tensor", mm_group(ps1[:, :], [(ones_sq[:, :], b5[:, :])], MMB,
                                  start=(oc == 0), stop=(oc == NCH - 1)),
               reads=["b5", "ones_sq"], writes=["ps1"], name=f"s2q{oc}")
    ln_finalize(eps2[0:1, :], "eps2", "ln2")
    for oc in range(NCH):
        p.dma("sync", bias_s[:, 2:3], lnw[oc], writes=["bias_s"])
        p.dma("sync", bias_s[:, 3:4], lnb[oc], writes=["bias_s2"])
        p.step("vector", lambda e, oc=oc: e.tensor_mul(f0[:, :], P1[oc][:, :], ps0[:, :]),
               reads=[("P1", oc), "ps0"], writes=["f0"], name=f"zn1_{oc}")
        p.step("vector", lambda e: e.tensor_add(f0[:, :], f0[:, :], ps1[:, :]),
               reads=["f0", "ps1"], writes=["f0"])
        p.step("vector", lambda e: e.tensor_scalar(f0[:, :], f0[:, :], bias_s[:, 2:3],
                                                   bias_s[:, 3:4], op0=ALU.mult,
                                                   op1=ALU.add),
               reads=["f0", "bias_s", "bias_s2"], writes=["f0"], name=f"affine{oc}")
        p.step("scalar", lambda e: e.activation(b4[:, :], f0[:, :], AF.Silu),
               reads=["f0"], writes=["b4"], name=f"g1_{oc}")
        p.dma("sync", f0[:, :], x1_d.ap()[oc * 128:(oc + 1) * 128, :],
              reads=[("x1_d", oc)], writes=["f0"])
        p.dma("sync", b5[:, :], s1_d.ap()[oc * 128:(oc + 1) * 128, :],
              reads=[("s1_d", oc)], writes=["b5"])
        p.step("vector", lambda e: e.tensor_mul(b5[:, :], b5[:, :], b4[:, :]),
               reads=["b5", "b4"], writes=["b5"])
        p.step("vector", lambda e: e.tensor_add(f1[:, :], f0[:, :], b5[:, :]),
               reads=["f0", "b5"], writes=["f1"], name=f"x2_{oc}")
        stores.append(p.dma("sync", outT[oc * 128:(oc + 1) * 128, :], f1[:, :],
                            reads=["f1"], name=f"stout{oc}"))

    p.barrier_step("sync", stores)
    with nc.Block() as block:
        p.emit(block)
    return nc


def mm_denom(ps1, ones_sq, b2, n0, j):
    def emit(eng):
        last = None
        for c0, c1 in piece_splits(T - n0, MMB, n0):
            last = eng.matmul(ps1[0:64, n0 + c0:n0 + c1], ones_sq[:, 0:64],
                              b2[:, n0 + c0:n0 + c1],
                              start=(j == 0), stop=(j == NTT - 1))
        return last
    return emit


def mm_o(ps1, vap, b2, n0, j):
    def emit(eng):
        last = None
        for c0, c1 in piece_splits(T - n0, MMB, n0):
            last = eng.matmul(ps1[64:128, n0 + c0:n0 + c1], vap,
                              b2[:, n0 + c0:n0 + c1],
                              start=(j == 0), stop=(j == NTT - 1))
        return last
    return emit

def _rotary_tables():
    inv = (1.0 / 10000.0 ** (np.arange(0, D, 2, dtype=np.float64) / D))
    t = np.arange(T, dtype=np.float64)
    freqs = (t[:, None] * inv[None, :]).astype(np.float32)
    emb = np.concatenate([freqs, freqs], axis=-1)
    cos = np.ascontiguousarray(np.cos(emb).T.astype(np.float32))
    sin = np.ascontiguousarray(np.sin(emb).T.astype(np.float32))
    return cos, sin


def _rot_mat2():
    R = np.zeros((D, D), np.float32)
    for i in range(D // 2):
        R[2 * i, 2 * i + 1] = -1.0
        R[2 * i + 1, 2 * i] = 1.0
    RT = np.ascontiguousarray(R.T)
    return np.ascontiguousarray(np.vstack([RT, RT]))


def _chunk_bias(b):
    return np.ascontiguousarray(b.reshape(NCH, 128, 1).astype(np.float32))


def launch1_inputs(inp):
    cos, sin = _rotary_tables()
    shared = {
        "wqT": np.ascontiguousarray(inp["L0_attn_Wq"].T),
        "wkT": np.ascontiguousarray(inp["L0_attn_Wk"].T),
        "wvT": np.ascontiguousarray(inp["L0_attn_Wv"].T),
        "wgT": np.ascontiguousarray(inp["L0_attn_Wg"].T),
        "bq": _chunk_bias(np.asarray(inp["L0_attn_bq"])),
        "bk": _chunk_bias(np.asarray(inp["L0_attn_bk"])),
        "bg": _chunk_bias(np.asarray(inp["L0_attn_bg"])),
        "bv_row": np.ascontiguousarray(np.asarray(inp["L0_attn_bv"]).reshape(1, E)),
        "gamma": np.ascontiguousarray(np.asarray(inp["L0_attn_gamma"]).reshape(1, H)),
        "gwT": np.ascontiguousarray(np.asarray(inp["L0_ffn_gate_weight"]).T),
        "coss": np.vstack([cos, cos]) / np.sqrt(8.0),
        "sins": np.vstack([sin, sin]) / np.sqrt(8.0),
        "rotT2": _rot_mat2(),
    }
    shared = {k: np.ascontiguousarray(v.astype(np.float32)) for k, v in shared.items()}
    x = np.asarray(inp["L0_attn_x"])
    maps = []
    for b in range(8):
        m = dict(shared)
        m["xT"] = np.ascontiguousarray(x[b].T.astype(np.float32))
        maps.append(m)
    return maps


def launch2_inputs(inp, res1):
    f32 = np.float32
    b1 = np.asarray(inp["L1_ffn_b1"], f32)
    b2 = np.asarray(inp["L1_ffn_b2"], f32)
    triA = np.where(np.arange(128)[None, :] < np.arange(128)[:, None], 0.0,
                    -1e38).astype(f32)
    triB = np.where(np.arange(128)[None, :] > np.arange(128)[:, None], 0.0,
                    -1e38).astype(f32)
    shared = {
        "gupT_all": np.ascontiguousarray(
            np.asarray(inp["L0_ffn_gate_up_proj"], f32).transpose(0, 2, 1)),
        "downT_all": np.ascontiguousarray(
            np.asarray(inp["L0_ffn_down_proj"], f32).transpose(0, 2, 1)),
        "wqkv": np.ascontiguousarray(np.asarray(inp["L1_attn_w_qkv"], f32)),
        "scale_r": np.asarray(inp["L1_attn_scale"], f32).reshape(1, H).copy(),
        "wo": np.ascontiguousarray(np.asarray(inp["L1_attn_w_out"], f32)),
        "b_out": _chunk_bias(np.asarray(inp["L1_attn_b_out"], f32)),
        "WrT": np.ascontiguousarray(np.asarray(inp["L1_ffn_Wr"], f32).T),
        "br_col": np.asarray(inp["L1_ffn_br"], f32).reshape(8, 1).copy(),
        "W1T_all": np.ascontiguousarray(
            np.asarray(inp["L1_ffn_W1"], f32).transpose(0, 2, 1)),
        "b1_r": np.ascontiguousarray(b1.reshape(8, 32, 128).transpose(0, 2, 1)),
        "W2T_all": np.ascontiguousarray(
            np.asarray(inp["L1_ffn_W2"], f32).transpose(0, 2, 1)),
        "b2_r": np.ascontiguousarray(b2.reshape(8, 8, 128).transpose(0, 2, 1)),
        "lnw": _chunk_bias(np.asarray(inp["L1_ffn_ln_w"], f32)),
        "lnb": _chunk_bias(np.asarray(inp["L1_ffn_ln_b"], f32)),
        "triA": triA, "triB": triB,
        "iotaP": np.arange(128, dtype=f32).reshape(128, 1).copy(),
    }
    selp_all = np.ascontiguousarray(
        np.concatenate([r["selp"] for r in res1.results], axis=0))
    x = np.asarray(inp["L0_attn_x"])
    maps = []
    for b in range(8):
        m = dict(shared)
        m["xT"] = np.ascontiguousarray(x[b].T.astype(f32))
        m["xn0T"] = res1.results[b]["xn_out"]
        m["s0Ti"] = res1.results[b]["s0T"]
        m["selp_all"] = selp_all
        m["selp_my"] = np.ascontiguousarray(selp_all[b * T:(b + 1) * T])
        m["maskb"] = np.ascontiguousarray(
            (np.arange(16384).reshape(128, 128, 1) < T * b).astype(f32))
        maps.append(m)
    return maps


def kernel(**inp):
    maps1 = launch1_inputs(inp)
    nc1 = build_launch1()
    res1 = run_bass_kernel_spmd(nc1, maps1, core_ids=list(range(8)))
    maps2 = launch2_inputs(inp, res1)
    nc2 = build_launch2()
    res2 = run_bass_kernel_spmd(nc2, maps2, core_ids=list(range(8)))
    out = np.stack([res2.results[b]["outT"].T for b in range(8)])
    return np.ascontiguousarray(out.astype(np.float32))


# version 04 (final)
# speedup vs baseline: 1.2101x; 1.2101x over previous
import numpy as np
import ml_dtypes
import concourse.bass as bass
import concourse.mybir as mybir
from concourse.bass_utils import run_bass_kernel_spmd

F32 = mybir.dt.float32
BF16 = mybir.dt.bfloat16
I32 = mybir.dt.int32
AF = mybir.ActivationFunctionType
ALU = mybir.AluOpType
AX = mybir.AxisListType
BF = ml_dtypes.bfloat16

T = 2048
E = 1024
H = 16
D = 64
NCH = 8
NTT = 16
MMB = 512
MMF = 512

ENGS = ("sync", "gpsimd", "tensor", "vector", "scalar")


class Prog:

    def __init__(self, nc):
        self.nc = nc
        self.steps = []
        self.last_writer = {}
        self.readers = {}
        self.val_after = []

    def step(self, eng, emit, reads=(), writes=(), deps=(), inc=1,
             sync_same=None, name=""):
        if sync_same is None:
            sync_same = eng in ("vector", "scalar")
        i = len(self.steps)
        dd = set()
        for k in reads:
            if k in self.last_writer:
                dd.add(self.last_writer[k])
        for k in writes:
            if k in self.last_writer:
                dd.add(self.last_writer[k])
            dd.update(self.readers.get(k, ()))
        dd.update(d for d in deps if d is not None)
        for k in reads:
            self.readers.setdefault(k, set()).add(i)
        for k in writes:
            self.last_writer[k] = i
            self.readers[k] = set()
        dd.discard(i)
        self.steps.append(dict(eng=eng, emit=emit, deps=sorted(dd), inc=inc,
                               sync_same=sync_same, name=name))
        return i

    def dma(self, eng, out, in_, reads=(), writes=(), deps=(), name="",
            sync_same=True):
        return self.step(
            eng, (lambda e, o=out, s=in_: e.dma_start(out=o, in_=s)),
            reads=reads, writes=writes, deps=deps, inc=16, name=name,
            sync_same=sync_same)

    def barrier_step(self, eng, deps, name="final"):
        return self.step(eng, (lambda e: e.nop()), deps=deps, inc=1,
                         sync_same=True, name=name)

    def emit(self, block):
        nc = self.nc
        sems = {en: nc.alloc_semaphore(name=f"prog_{en}") for en in ENGS}
        cum = {en: 0 for en in ENGS}
        self.val_after = []
        for st in self.steps:
            cum[st["eng"]] += st["inc"]
            self.val_after.append(cum[st["eng"]])

        def body(eng_name):
            def run(eng):
                watermark = {en: 0 for en in ENGS}
                for i, st in enumerate(self.steps):
                    if st["eng"] != eng_name:
                        continue
                    for j in st["deps"]:
                        dj = self.steps[j]
                        if dj["eng"] == eng_name and not st["sync_same"]:
                            continue
                        v = self.val_after[j]
                        if watermark[dj["eng"]] >= v:
                            continue
                        eng.wait_ge(sems[dj["eng"]], v)
                        watermark[dj["eng"]] = v
                    inst = st["emit"](eng)
                    assert inst is not None, f"step {i} {st['name']} returned None"
                    inst.then_inc(sems[st["eng"]], st["inc"])
            return run

        block.sync(body("sync"))
        block.tensor(body("tensor"))
        block.vector(body("vector"))
        block.scalar(body("scalar"))

        def gps_body(eng):
            body("gpsimd")(eng)
            import os as _os
            _tail = _os.environ.get("PROG_TAIL", "full")
            if _tail in ("full", "waits"):
                for en in ENGS:
                    if cum[en] > 0 and en != "gpsimd":
                        eng.wait_ge(sems[en], cum[en])
            if _tail in ("full", "clears"):
                nums = sorted(sems[en].num for en in ENGS)
                assert nums == list(range(nums[0], nums[0] + len(nums))), nums
                rng = range(nums[0], nums[-1] + 1)
                eng.dma_reset(rng)
                eng.sem_clear(rng)
        block.gpsimd(gps_body)
        return sems


def piece_splits(ncols, nmax, bank_off=0):
    out = []
    c = 0
    while c < ncols:
        step = min(nmax, ncols - c, 512 - ((bank_off + c) % 512))
        out.append((c, c + step))
        c += step
    return out


def mm_group(psum_ap, pairs, nmax, start=True, stop=True, bank_off=0):
    def emit(eng):
        last = None
        for c0, c1 in piece_splits(psum_ap.shape[-1], nmax, bank_off):
            for k, (lhsT, rhs) in enumerate(pairs):
                last = eng.matmul(psum_ap[:, c0:c1], lhsT, rhs[:, c0:c1],
                                  start=(start and k == 0),
                                  stop=(stop and k == len(pairs) - 1))
        return last
    return emit


def build_launch1():
    nc = bass.Bass("TRN2", target_bir_lowering=False)

    def din(name, shape, dt=F32):
        return nc.dram_tensor(name, list(shape), dt, kind="ExternalInput").ap()

    def dout(name, shape, dt=F32):
        return nc.dram_tensor(name, list(shape), dt, kind="ExternalOutput").ap()

    xT = din("xT", [E, T])
    wqT = din("wqT", [E, E]); wkT = din("wkT", [E, E])
    wvT = din("wvT", [E, E]); wgT = din("wgT", [E, E])
    bq = din("bq", [NCH, 128, 1]); bk = din("bk", [NCH, 128, 1])
    bg = din("bg", [NCH, 128, 1]); bv_row = din("bv_row", [1, E])
    gamma = din("gamma", [1, H])
    gwT = din("gwT", [E, 32])
    coss = din("coss", [128, T]); sins = din("sins", [128, T])
    rotT2 = din("rotT2", [128, D])

    xn_out = dout("xn_out", [E, T], BF16)
    s0T = dout("s0T", [E, T], BF16)
    selp = dout("selp", [T, 32])

    a = nc.alloc_sbuf_tensor
    xn = [a(f"xn{c}", [128, T], BF16) for c in range(NCH)]
    Vt = [a(f"Vt{t}", [128, E], BF16) for t in range(NTT)]
    OT = [a(f"OT{c}", [128, T], BF16) for c in range(NCH)]
    f0 = a("f0", [128, T], F32); f1 = a("f1", [128, T], F32)
    f2 = a("f2", [128, T], F32); f3 = a("f3", [128, T], F32)
    b0 = a("b0", [128, T], BF16); b1 = a("b1", [128, T], BF16)
    b2 = a("b2", [128, T], BF16); b3 = a("b3", [128, T], BF16)
    wA = a("wA", [128, 8 * E], BF16)
    wB = a("wB", [128, 8 * E], BF16)
    tabs = a("tabs", [128, 2 * T], F32)
    rot_s = a("rot_s", [128, D], BF16)
    gw_s = a("gw_s", [128, 8 * 32], BF16)
    bias_s = a("bias_s", [128, 4], F32)
    bvb = a("bvb", [128, E], F32)
    ones_col = a("ones_col", [128, 1], BF16)
    ones_sq = a("ones_sq", [128, 128], BF16)
    ones_row = a("ones_row", [1, 128], F32)
    gsig = a("gsig", [1, H], F32)
    eps_t = a("eps_t", [1, 1], F32)
    gnb = a("gnb", [128, H], F32)
    lg_s = a("lg_s", [32, T], F32)
    idf = a("identS", [128, 128], F32)
    rt = {n: a("rt_" + n, [128, 32], F32)
          for n in ("ltok", "exp", "probs", "rep", "mask", "tmp", "sel")}
    rt_t8 = a("rt_t8", [128, 8], F32)
    rt_nmx = a("rt_nmx", [128, 1], F32)
    rt_sum = a("rt_sum", [128, 1], F32)
    rt_rcp = a("rt_rcp", [128, 1], F32)

    ps0 = nc.alloc_psum_tensor("ps0", [128, T], F32)
    ps1 = nc.alloc_psum_tensor("ps1", [128, T], F32)

    ident = nc.inline_tensor(np.eye(128, dtype=np.float32), name="identF32")

    p = Prog(nc)
    stores = []

    p.step("vector", lambda e: e.memset(ones_col[:, :], 1.0), writes=["ones_col"])
    p.step("vector", lambda e: e.memset(ones_sq[:, :], 1.0), writes=["ones_sq"])
    p.step("vector", lambda e: e.memset(eps_t[:, :], 1e-5), writes=["eps_t"])
    p.step("vector", lambda e: e.memset(ones_row[:, :], 1.0), writes=["ones_row"])
    p.dma("gpsimd", idf[:, :], ident.ap(), writes=["idf"], name="ident")
    for i, tsrc in enumerate((coss, sins)):
        p.dma("sync", tabs[:, i * T:(i + 1) * T], tsrc, writes=["tabs"])
    p.dma("gpsimd", rot_s[:, :], rotT2, writes=["rot_s"])
    p.dma("gpsimd", gw_s[:, :].rearrange("p (c e) -> p c e", c=NCH),
          gwT.rearrange("(c p) e -> p c e", p=128), writes=["gw_s"])
    p.dma("sync", gsig[0:1, 0:H], gamma, writes=["gsig"])
    p.step("scalar", lambda e: e.activation(gsig[0:1, :], gsig[0:1, :], AF.Sigmoid),
           reads=["gsig"], writes=["gsig"], sync_same=True)
    p.step("vector", lambda e: e.tensor_scalar(gsig[0:1, :], gsig[0:1, :], -1.0,
                                               None, op0=ALU.mult),
           reads=["gsig"], writes=["gsig"])
    p.step("tensor", mm_group(ps1[:, 0:H], [(ones_row[:, :], gsig[0:1, :])], MMF),
           reads=["ones_row", "gsig"], writes=["ps1"])
    p.step("scalar", lambda e: e.copy(gnb[:, :], ps1[:, 0:H]),
           reads=["ps1"], writes=["gnb"])
    p.dma("sync", f0[0:1, 0:E], bv_row, writes=["f0"])
    p.step("tensor", mm_group(ps0[:, 0:E], [(ones_row[:, :], f0[0:1, 0:E])], MMF),
           reads=["ones_row", "f0"], writes=["ps0"])
    p.step("scalar", lambda e: e.copy(bvb[:, :], ps0[:, 0:E]),
           reads=["ps0"], writes=["bvb"])

    for c in range(NCH):
        xf = (f0, f1)[c % 2]
        xb = (b0, b1)[c % 2]
        xq = (b2, b3)[c % 2]
        kf, kb, kq = ("f0", "f1")[c % 2], ("b0", "b1")[c % 2], ("b2", "b3")[c % 2]
        p.dma("sync", xf[:, :], xT[c * 128:(c + 1) * 128, :], writes=[kf],
              name=f"ldx{c}")
        p.step("vector", lambda e, xb=xb, xf=xf: e.tensor_copy(xb[:, :], xf[:, :]),
               reads=[kf], writes=[kb])
        p.step("scalar", lambda e, xq=xq, xb=xb: e.activation(xq[:, :], xb[:, :], AF.Square),
               reads=[kb], writes=[kq])
        p.step("tensor", mm_group(ps0[:, :], [(ones_sq[:, :], xb[:, :])], MMB,
                                  start=(c == 0), stop=(c == NCH - 1)),
               reads=[kb, "ones_sq"], writes=["ps0"], name=f"sx{c}")
        p.step("tensor", mm_group(ps1[:, :], [(ones_sq[:, :], xq[:, :])], MMB,
                                  start=(c == 0), stop=(c == NCH - 1)),
               reads=[kq, "ones_sq"], writes=["ps1"], name=f"sq{c}")

    p.step("vector", lambda e: e.tensor_scalar(f0[0:1, :], ps0[0:1, :], 1.0 / E,
                                               None, op0=ALU.mult),
           reads=["ps0"], writes=["f0"], name="mu")
    p.step("vector", lambda e: e.tensor_scalar(f1[0:1, :], ps1[0:1, :], 1.0 / E,
                                               None, op0=ALU.mult),
           reads=["ps1"], writes=["f1"], name="msq")
    p.step("vector", lambda e: e.tensor_mul(f2[0:1, :], f0[0:1, :], f0[0:1, :]),
           reads=["f0"], writes=["f2"], name="mu2")
    p.step("vector", lambda e: e.tensor_sub(f3[0:1, :], f1[0:1, :], f2[0:1, :]),
           reads=["f1", "f2"], writes=["f3"], name="var")
    p.step("scalar", lambda e: e.activation(f3[0:1, :], f3[0:1, :], AF.Sqrt,
                                            bias=eps_t[0:1, :]),
           reads=["f3", "eps_t"], writes=["f3"], name="sd")
    p.step("vector", lambda e: e.reciprocal(f2[0:1, :], f3[0:1, :]),
           reads=["f3"], writes=["f2"], name="rstd")
    p.step("vector", lambda e: e.tensor_mul(f1[0:1, :], f0[0:1, :], f2[0:1, :]),
           reads=["f0", "f2"], writes=["f1"], name="murstd")
    p.step("vector", lambda e: e.tensor_scalar(f1[0:1, :], f1[0:1, :], -1.0,
                                               None, op0=ALU.mult),
           reads=["f1"], writes=["f1"], sync_same=True, name="nmr")
    p.step("tensor", mm_group(ps0[:, :], [(ones_row[:, :], f2[0:1, :])], MMF),
           reads=["ones_row", "f2"], writes=["ps0"], name="bc_rstd")
    p.step("tensor", mm_group(ps1[:, :], [(ones_row[:, :], f1[0:1, :])], MMF),
           reads=["ones_row", "f1"], writes=["ps1"], name="bc_nmr")
    for c in range(NCH):
        xf = (f0, f1)[c % 2]
        kf = ("f0", "f1")[c % 2]
        p.dma("sync", xf[:, :], xT[c * 128:(c + 1) * 128, :], writes=[kf])
        p.step("vector", lambda e, xf=xf: e.tensor_mul(f2[:, :], xf[:, :], ps0[:, :]),
               reads=[kf, "ps0"], writes=["f2"])
        p.step("vector", lambda e, c=c: e.tensor_add(xn[c][:, :], f2[:, :], ps1[:, :]),
               reads=["f2", "ps1"], writes=[("xn", c)])
        stores.append(p.dma("sync", xn_out[c * 128:(c + 1) * 128, :], xn[c][:, :],
                            reads=[("xn", c)], name=f"st_xn{c}"))

    xn_keys = [("xn", c) for c in range(NCH)]

    for ic in range(NCH):
        p.dma("gpsimd", wA[:, ic * E:(ic + 1) * E],
              wvT[ic * 128:(ic + 1) * 128, :], writes=["wA"], name=f"ldwv{ic}")
    for tt in range(NTT):
        pairs = [(xn[ic][:, tt * 128:(tt + 1) * 128], wA[:, ic * E:(ic + 1) * E])
                 for ic in range(NCH)]
        p.step("tensor", mm_group(ps0[:, 0:E], pairs, MMB),
               reads=xn_keys + ["wA"], writes=["ps0"], name=f"vproj{tt}")
        p.step("vector", lambda e, tt=tt: e.scalar_tensor_tensor(
            Vt[tt][:, :], ps0[:, 0:E], 1.0, bvb[:, :], op0=ALU.mult, op1=ALU.add),
            reads=["ps0", "bvb"], writes=[("Vt", tt)], name=f"vevac{tt}")

    for ic in range(NCH):
        p.dma("gpsimd", wA[:, ic * E:(ic + 1) * E], wqT[ic * 128:(ic + 1) * 128, :],
              writes=["wA"], name=f"ldwq{ic}")
        p.dma("gpsimd", wB[:, ic * E:(ic + 1) * E], wkT[ic * 128:(ic + 1) * 128, :],
              writes=["wB"], name=f"ldwk{ic}")

    def proj_rot(oc, which, wbuf, wkey, btab, bcol, cos_off, sin_off, dst, dkey):
        p.dma("sync", bias_s[:, bcol:bcol + 1], btab[oc], writes=["bias_s"],
              name=f"ldb{which}{oc}")
        pairs = [(wbuf[:, ic * E + oc * 128: ic * E + oc * 128 + 128], xn[ic][:, :])
                 for ic in range(NCH)]
        p.step("tensor", mm_group(ps0[:, :], pairs, MMB),
               reads=[wkey] + xn_keys, writes=["ps0"], name=f"proj{which}{oc}")
        p.step("scalar", lambda e: e.activation(b2[:, :], ps0[:, :], AF.Identity,
                                                bias=bias_s[:, bcol:bcol + 1]),
               reads=["ps0", "bias_s"], writes=["b2"], name=f"evac{which}{oc}")
        for h2 in range(2):
            rows = slice(h2 * D, (h2 + 1) * D)
            p.step("tensor", mm_group(ps1[h2 * D:(h2 + 1) * D, :],
                                      [(rot_s[rows, :], b2[rows, :])], MMB),
                   reads=["rot_s", "b2"], writes=["ps1"], name=f"rot{which}{oc}_{h2}")
            p.step("vector", lambda e, rows=rows, co=cos_off: e.tensor_mul(
                f2[rows, :], b2[rows, :], tabs[rows, co * T:(co + 1) * T]),
                reads=["b2", "tabs"], writes=["f2"])
            p.step("vector", lambda e, rows=rows, so=sin_off: e.tensor_mul(
                f3[rows, :], ps1[rows, :], tabs[rows, so * T:(so + 1) * T]),
                reads=["ps1", "tabs"], writes=["f3"])
            p.step("vector", lambda e, dst=dst, rows=rows: e.tensor_add(
                dst[rows, :], f2[rows, :], f3[rows, :]),
                reads=["f2", "f3"], writes=[dkey], name=f"rotc{which}{oc}_{h2}")

    for oc in range(NCH):
        proj_rot(oc, "q", wA, "wA", bq, 0, 0, 1, b0, "b0")
        proj_rot(oc, "k", wB, "wB", bk, 1, 0, 1, b1, "b1")
        for h2 in range(2):
            h = 2 * oc + h2
            rows = slice(h2 * D, (h2 + 1) * D)
            for j in range(NTT):
                dbuf, kd = ((f0, "f0"), (f1, "f1"))[j % 2]
                tbuf, kt = ((f2, "f2"), (f3, "f3"))[j % 2]
                pbuf, kp = ((b2, "b2"), (b3, "b3"))[j % 2]
                p.step("tensor", mm_group(ps0[:, :],
                                          [(b1[rows, j * 128:(j + 1) * 128],
                                            b0[rows, :])], MMB),
                       reads=["b0", "b1"], writes=["ps0"], name=f"S{h}_{j}")
                p.step("gpsimd", lambda e, dbuf=dbuf, j=j: e.iota(
                    dbuf[:, :], pattern=[[1, T]], base=-128 * j,
                    channel_multiplier=-1, allow_small_or_imprecise_dtypes=True),
                    writes=[kd], name=f"iota{h}_{j}")
                p.step("vector", lambda e, tbuf=tbuf, dbuf=dbuf: e.tensor_relu(
                    tbuf[:, :], dbuf[:, :]), reads=[kd], writes=[kt])
                p.step("scalar", lambda e, tbuf=tbuf, h=h: e.activation(
                    tbuf[:, :], tbuf[:, :], AF.Exp, scale=gnb[:, h:h + 1]),
                    reads=[kt, "gnb"], writes=[kt], sync_same=True,
                    name=f"tw{h}_{j}")
                p.step("vector", lambda e, pbuf=pbuf, tbuf=tbuf: e.tensor_mul(
                    pbuf[:, :], ps0[:, :], tbuf[:, :]),
                    reads=["ps0", kt], writes=[kp], name=f"ptile{h}_{j}")
                p.step("tensor", mm_group(ps1[0:D, :],
                                          [(Vt[j][:, h * D:(h + 1) * D], pbuf[:, :])],
                                          MMB, start=(j == 0), stop=(j == NTT - 1)),
                       reads=[("Vt", j), kp], writes=["ps1"], name=f"O{h}_{j}")
            p.step("scalar", lambda e, oc=oc, rows=rows: e.copy(
                OT[oc][rows, :], ps1[0:D, :]),
                reads=["ps1"], writes=[("OT", oc)], name=f"oevac{h}")

    for ic in range(NCH):
        p.dma("gpsimd", wA[:, ic * E:(ic + 1) * E], wgT[ic * 128:(ic + 1) * 128, :],
              writes=["wA"], name=f"ldwg{ic}")
    for oc in range(NCH):
        p.dma("sync", bias_s[:, 2:3], bg[oc], writes=["bias_s"])
        pairs = [(wA[:, ic * E + oc * 128: ic * E + oc * 128 + 128], xn[ic][:, :])
                 for ic in range(NCH)]
        p.step("tensor", mm_group(ps0[:, :], pairs, MMB),
               reads=["wA"] + xn_keys, writes=["ps0"], name=f"gproj{oc}")
        p.step("scalar", lambda e: e.activation(b0[:, :], ps0[:, :], AF.Silu,
                                                bias=bias_s[:, 2:3]),
               reads=["ps0", "bias_s"], writes=["b0"], name=f"gevac{oc}")
        p.step("vector", lambda e, oc=oc: e.tensor_mul(b1[:, :], OT[oc][:, :], b0[:, :]),
               reads=[("OT", oc), "b0"], writes=["b1"], name=f"s0m{oc}")
        stores.append(p.dma("sync", s0T[oc * 128:(oc + 1) * 128, :], b1[:, :],
                            reads=["b1"], name=f"st_s0{oc}"))

    pairs = [(gw_s[:, ic * 32:(ic + 1) * 32], xn[ic][:, :]) for ic in range(NCH)]
    p.step("tensor", mm_group(ps0[0:32, :], pairs, MMB),
           reads=["gw_s"] + xn_keys, writes=["ps0"], name="router_mm")
    p.step("scalar", lambda e: e.copy(lg_s[:, :], ps0[0:32, :]),
           reads=["ps0"], writes=["lg_s"])
    for tt in range(NTT):
        p.step("tensor", lambda e, tt=tt: e.transpose(
            ps1[:, 0:32], lg_s[:, tt * 128:(tt + 1) * 128], idf[0:32, 0:32]),
            reads=["lg_s", "idf"], writes=["ps1"], name=f"rt_tr{tt}")
        p.step("scalar", lambda e: e.copy(rt["ltok"][:, :], ps1[0:128, 0:32]),
               reads=["ps1"], writes=["ltok"])
        p.step("vector", lambda e: e.tensor_reduce(
            rt_nmx[:, :], rt["ltok"][:, :], axis=AX.X, op=ALU.max, negate=True),
            reads=["ltok"], writes=["nmx"])
        p.step("scalar", lambda e: e.activation(
            rt["exp"][:, :], rt["ltok"][:, :], AF.Exp, bias=rt_nmx[:, :],
            accum_out=rt_sum[:, :]),
            reads=["ltok", "nmx"], writes=["rexp", "rsum"])
        p.step("vector", lambda e: e.reciprocal(rt_rcp[:, :], rt_sum[:, :]),
               reads=["rsum"], writes=["rrcp"])
        p.step("vector", lambda e: e.tensor_scalar(
            rt["probs"][:, :], rt["exp"][:, :], rt_rcp[:, :], None, op0=ALU.mult),
            reads=["rexp", "rrcp"], writes=["probs"])
        p.step("vector", lambda e: e.max(rt_t8[:, :], rt["probs"][:, :]),
               reads=["probs"], writes=["t8"], sync_same=True, name=f"max8_{tt}")
        p.step("vector", lambda e: e.match_replace(
            rt["rep"][:, :], rt_t8[:, :], rt["probs"][:, :], -1.0),
            reads=["t8", "probs"], writes=["rrep"], sync_same=True,
            name=f"mrep_{tt}")
        p.step("vector", lambda e: e.tensor_tensor(
            rt["mask"][:, :], rt["rep"][:, :], rt["probs"][:, :], op=ALU.is_lt),
            reads=["rrep", "probs"], writes=["rmask"])
        p.step("vector", lambda e: e.tensor_scalar(
            rt["tmp"][:, :], rt["probs"][:, :], 1.0, None, op0=ALU.add),
            reads=["probs"], writes=["rtmp"])
        p.step("vector", lambda e: e.tensor_mul(
            rt["tmp"][:, :], rt["tmp"][:, :], rt["mask"][:, :]),
            reads=["rtmp", "rmask"], writes=["rtmp"], sync_same=True)
        p.step("vector", lambda e: e.tensor_scalar(
            rt["sel"][:, :], rt["tmp"][:, :], -1.0, None, op0=ALU.add),
            reads=["rtmp"], writes=["rsel"])
        stores.append(p.dma("sync", selp[tt * 128:(tt + 1) * 128, :],
                            rt["sel"][:, :], reads=["rsel"], name=f"st_selp{tt}"))

    p.barrier_step("sync", stores)
    with nc.Block() as block:
        p.emit(block)
    return nc



def build_launch2(dbg=False):
    nc = bass.Bass("TRN2", target_bir_lowering=False)
    ikind = "ExternalOutput" if dbg else "Internal"

    def din(name, shape, dt=F32):
        return nc.dram_tensor(name, list(shape), dt, kind="ExternalInput").ap()

    xT = din("xT", [E, T])
    xn0T = din("xn0T", [E, T], BF16)
    s0Ti = din("s0Ti", [E, T], BF16)
    selp_all = din("selp_all", [16384, 32])
    selp_my = din("selp_my", [T, 32])
    maskb = din("maskb", [128, 128, 1])
    gupT_all = din("gupT_all", [32, E, 1536])
    downT_all = din("downT_all", [32, 768, E])
    wqkv = din("wqkv", [E, 3 * E])
    scale_r = din("scale_r", [1, H])
    wo = din("wo", [E, E])
    b_out = din("b_out", [NCH, 128, 1])
    WrT = din("WrT", [E, 8])
    br_col = din("br_col", [8, 1])
    W1T_all = din("W1T_all", [8, E, 4096])
    b1_r = din("b1_r", [8, 128, 32])
    W2T_all = din("W2T_all", [8, 4096, E])
    b2_r = din("b2_r", [8, 128, 8])
    lnw = din("lnw", [NCH, 128, 1])
    lnb = din("lnb", [NCH, 128, 1])
    triA = din("triA", [128, 128])
    triB = din("triB", [128, 128])
    iotaP = din("iotaP", [128, 1])

    outT = nc.dram_tensor("outT", [E, T], F32, kind="ExternalOutput").ap()

    x1_d = nc.dram_tensor("x1_d", [E, T], F32, kind=ikind)
    s1_d = nc.dram_tensor("s1_d", [E, T], BF16, kind=ikind)
    mrow_d = nc.dram_tensor("mrow_d", [16, 128], F32, kind="Internal")
    cw_d = nc.dram_tensor("cw_d", [32, T], F32, kind=ikind)
    cw1_d = nc.dram_tensor("cw1_d", [8, T], F32, kind=ikind)
    g0_d = (nc.dram_tensor("g0_d", [E, T], BF16, kind="ExternalOutput")
            if dbg else None)
    xn1_d = (nc.dram_tensor("xn1_d", [E, T], BF16, kind="ExternalOutput")
             if dbg else None)
    qst_d = (nc.dram_tensor("qst_d", [128, T], BF16, kind="ExternalOutput")
             if dbg else None)
    kst_d = (nc.dram_tensor("kst_d", [128, T], BF16, kind="ExternalOutput")
             if dbg else None)
    pt_d = (nc.dram_tensor("pt_d", [128, T], BF16, kind="ExternalOutput")
            if dbg else None)
    dn_d = (nc.dram_tensor("dn_d", [1, T], F32, kind="ExternalOutput")
            if dbg else None)
    at0_d = (nc.dram_tensor("at0_d", [128, T], BF16, kind="ExternalOutput")
             if dbg else None)
    at_d = (nc.dram_tensor("at_d", [E, T], BF16, kind="ExternalOutput")
            if dbg else None)
    wa_d = (nc.dram_tensor("wa_d", [128, 8 * E], BF16, kind="ExternalOutput")
            if dbg else None)

    a = nc.alloc_sbuf_tensor
    xn = [a(f"xn{c}", [128, T], BF16) for c in range(NCH)]
    P1 = [a(f"P1_{c}", [128, T], BF16) for c in range(NCH)]
    P2 = [a(f"P2_{c}", [128, T], BF16) for c in range(NCH)]
    wA = a("wA", [128, 8 * E], BF16)
    wB = a("wB", [128, 8 * E], BF16)
    f0 = a("f0", [128, T], F32); f1 = a("f1", [128, T], F32)
    cwb = a("cwb", [128, T], F32)
    qst = a("qst", [128, T], BF16); kst = a("kst", [128, T], BF16)
    b2 = a("b2", [128, T], BF16); b4 = a("b4", [128, T], BF16)
    b5 = a("b5", [128, T], BF16)
    w2r = a("w2r", [128, T], BF16)
    qkw = a("qkw", [128, 2 * E], BF16)
    l1s = a("l1s", [8, T], F32)
    ones_row = a("ones_row", [1, 128], F32)
    ones_sq = a("ones_sq", [128, 128], BF16)
    idf = a("identS", [128, 128], F32)
    sU = a("sU", [128, 128], BF16)
    tA = a("tA", [128, 128], F32)
    tB = a("tB", [128, 128], F32)
    iop = a("iop", [128, 1], F32)
    rsel = a("rsel", [128, 32], F32)
    rtile = a("rtile", [128, 32], BF16)
    rmask = a("rmask", [128, 32], BF16)
    mbcol = a("mbcol", [128, 1], F32)
    base_s = a("base_s", [128, 32], F32)
    brun_s = a("brun_s", [128, 32], F32)
    posw = a("posw", [128, 32], F32)
    kept = a("kept", [128, 32], F32)
    wsel = a("wsel", [128, 32], F32)
    wsum = a("wsum", [128, 1], F32)
    wrcp = a("wrcp", [128, 1], F32)
    cwt = a("cwt", [128, 32], F32)
    trv = a("trv", [32, 128], F32)
    crow = a("crow", [1, T], F32)
    bias_s = a("bias_s", [128, 4], F32)
    scvb = a("scvb", [128, H], F32)
    mcol = a("mcol", [128, 16], F32)
    msA = a("msA", [16, 128], F32)
    mx1 = a("mx1", [128, 1], F32)
    mx2 = a("mx2", [128, 1], F32)
    b1c = a("b1c", [128, 32], F32)
    b2c = a("b2c", [128, 8], F32)
    lt8 = a("lt8", [128, 8], F32)
    l2f = a("l2f", [128, 8], F32)
    eq1 = a("eq1", [128, 8], F32)
    eq2 = a("eq2", [128, 8], F32)
    cw1 = a("cw1", [128, 8], F32)
    m1c = a("m1c", [128, 1], F32)
    m2c = a("m2c", [128, 1], F32)
    w1c = a("w1c", [128, 1], F32)
    w2c = a("w2c", [128, 1], F32)
    eps2 = a("eps2", [1, 1], F32)
    gsr = a("gsr", [1, H], F32)

    ps0 = nc.alloc_psum_tensor("ps0", [128, T], F32)
    ps1 = nc.alloc_psum_tensor("ps1", [128, T], F32)

    ident = nc.inline_tensor(np.eye(128, dtype=np.float32), name="identF32")
    sU_np = np.triu(np.ones((128, 128), np.float32), 1).astype(BF)
    sU_c = nc.inline_tensor(sU_np, name="strictU")

    p = Prog(nc)
    stores = []
    xk = [("xn", c) for c in range(NCH)]

    p.step("vector", lambda e: e.memset(ones_row[:, :], 1.0), writes=["ones_row"])
    p.step("vector", lambda e: e.memset(ones_sq[:, :], 1.0), writes=["ones_sq"])
    p.step("vector", lambda e: e.memset(eps2[:, :], 1e-12), writes=["eps2"])
    p.step("vector", lambda e: e.memset(brun_s[:, :], 0.0), writes=["brun_s"])
    p.dma("gpsimd", idf[:, :], ident.ap(), writes=["idf"])
    p.dma("gpsimd", sU[:, :], sU_c.ap(), writes=["sU"])
    p.dma("sync", tA[:, :], triA, writes=["tA"])
    p.dma("sync", tB[:, :], triB, writes=["tB"])
    p.dma("sync", iop[:, :], iotaP, writes=["iop"])
    for c in range(NCH):
        p.dma("sync", xn[c][:, :], xn0T[c * 128:(c + 1) * 128, :], writes=[("xn", c)])

    for jc in range(128):
        p.dma("sync", rsel[:, :], selp_all[jc * 128:(jc + 1) * 128, :],
              writes=["rsel"], name=f"ldsel{jc}")
        p.dma("sync", mbcol[:, :], maskb[jc], writes=["mbcol"])
        p.step("vector", lambda e: e.tensor_scalar(rtile[:, :], rsel[:, :], -0.5,
                                                   None, op0=ALU.is_gt),
               reads=["rsel"], writes=["rtile"])
        p.step("vector", lambda e: e.tensor_scalar(rmask[:, :], rtile[:, :],
                                                   mbcol[:, :], None, op0=ALU.mult),
               reads=["rtile", "mbcol"], writes=["rmask"])
        p.step("tensor", mm_group(ps0[:, 0:32], [(ones_sq[:, :], rmask[:, :])], MMB,
                                  start=(jc == 0), stop=(jc == 127)),
               reads=["rmask", "ones_sq"], writes=["ps0"], name=f"base{jc}")
    p.step("scalar", lambda e: e.copy(base_s[:, :], ps0[:, 0:32]),
           reads=["ps0"], writes=["base_s"])

    for tt in range(NTT):
        p.dma("sync", rsel[:, :], selp_my[tt * 128:(tt + 1) * 128, :],
              writes=["rsel"], name=f"ldselmy{tt}")
        p.step("vector", lambda e: e.tensor_scalar(rtile[:, :], rsel[:, :], -0.5,
                                                   None, op0=ALU.is_gt),
               reads=["rsel"], writes=["rtile"])
        p.step("tensor", mm_group(ps0[:, 32:64], [(sU[:, :], rtile[:, :])], MMB),
               reads=["rtile", "sU"], writes=["ps0"], name=f"posw{tt}")
        p.step("scalar", lambda e: e.copy(posw[:, :], ps0[:, 32:64]),
               reads=["ps0"], writes=["posw"])
        p.step("vector", lambda e: e.tensor_add(posw[:, :], posw[:, :], brun_s[:, :]),
               reads=["posw", "brun_s"], writes=["posw"])
        p.step("vector", lambda e: e.tensor_add(posw[:, :], posw[:, :], base_s[:, :]),
               reads=["posw", "base_s"], writes=["posw"])
        p.step("tensor", mm_group(ps1[:, 0:32], [(ones_sq[:, :], rtile[:, :])], MMB,
                                  start=(tt == 0), stop=(tt == NTT - 1)),
               reads=["rtile", "ones_sq"], writes=["ps1"], name=f"brun{tt}")
        p.step("scalar", lambda e: e.copy(brun_s[:, :], ps1[:, 0:32]),
               reads=["ps1"], writes=["brun_s"])
        p.step("vector", lambda e: e.tensor_scalar(kept[:, :], posw[:, :], 1023.5,
                                                   None, op0=ALU.is_le),
               reads=["posw"], writes=["kept"])
        p.step("vector", lambda e: e.tensor_relu(wsel[:, :], rsel[:, :]),
               reads=["rsel"], writes=["wsel"])
        p.step("vector", lambda e: e.tensor_reduce(wsum[:, :], wsel[:, :], axis=AX.X,
                                                   op=ALU.add),
               reads=["wsel"], writes=["wsum"])
        p.step("vector", lambda e: e.reciprocal(wrcp[:, :], wsum[:, :]),
               reads=["wsum"], writes=["wrcp"])
        p.step("vector", lambda e: e.tensor_scalar(wsel[:, :], wsel[:, :], wrcp[:, :],
                                                   None, op0=ALU.mult),
               reads=["wsel", "wrcp"], writes=["wsel"])
        p.step("vector", lambda e: e.tensor_mul(cwt[:, :], wsel[:, :], kept[:, :]),
               reads=["wsel", "kept"], writes=["cwt"])
        p.step("tensor", lambda e: e.transpose(ps1[0:32, 32:160], cwt[:, :], idf[:, :]),
               reads=["cwt", "idf"], writes=["ps1"], name=f"cwtr{tt}")
        p.step("scalar", lambda e: e.copy(trv[:, :], ps1[0:32, 32:160]),
               reads=["ps1"], writes=["trv"])
        stores.append(p.dma("sync", cw_d.ap()[:, tt * 128:(tt + 1) * 128], trv[:, :],
                            reads=["trv"], writes=["cw_d"], name=f"stcw{tt}"))

    for oc in range(NCH):
        p.step("vector", lambda e, oc=oc: e.memset(P1[oc][:, :], 0.0),
               writes=[("P1", oc)])
    for ee in range(32):
        p.dma("sync", crow[0:1, :], cw_d.ap()[ee:ee + 1, :], reads=["cw_d"],
              writes=["crow"], name=f"ldcw{ee}")
        p.step("tensor", mm_group(ps0[:, :], [(ones_row[:, :], crow[0:1, :])], MMF),
               reads=["ones_row", "crow"], writes=["ps0"], name=f"cwb{ee}")
        p.step("scalar", lambda e: e.copy(cwb[:, :], ps0[:, :]),
               reads=["ps0"], writes=["cwb"])
        for g3 in range(3):
            half = (g3 % 2)
            for ic in range(NCH):
                p.dma("gpsimd", wA[:, half * 4096 + ic * 512: half * 4096 + (ic + 1) * 512],
                      gupT_all[ee, ic * 128:(ic + 1) * 128, g3 * 512:(g3 + 1) * 512],
                      writes=[("wA", half)], name=f"gup{ee}_{g3}_{ic}")
            for k4 in range(4):
                fc = g3 * 4 + k4
                pairs = [(wA[:, half * 4096 + ic * 512 + k4 * 128:
                             half * 4096 + ic * 512 + k4 * 128 + 128], xn[ic][:, :])
                         for ic in range(NCH)]
                p.step("tensor", mm_group(ps1[:, :], pairs, MMB),
                       reads=[("wA", half)] + xk, writes=["ps1"], name=f"gu{ee}_{fc}")
                if fc < 6:
                    p.step("scalar", lambda e, fc=fc: e.activation(
                        P2[fc][:, :], ps1[:, :], AF.Silu),
                        reads=["ps1"], writes=[("P2", fc)], name=f"gate{ee}_{fc}")
                else:
                    p.step("vector", lambda e, fc=fc: e.tensor_mul(
                        P2[fc - 6][:, :], P2[fc - 6][:, :], ps1[:, :]),
                        reads=["ps1", ("P2", fc - 6)], writes=[("P2", fc - 6)],
                        name=f"hh{ee}_{fc}")
        for hc in range(6):
            p.dma("gpsimd", wB[:, hc * E:(hc + 1) * E],
                  downT_all[ee, hc * 128:(hc + 1) * 128, :], writes=["wB"],
                  name=f"dwn{ee}_{hc}")
        for oc in range(NCH):
            pairs = [(wB[:, hc * E + oc * 128: hc * E + oc * 128 + 128], P2[hc][:, :])
                     for hc in range(6)]
            p.step("tensor", mm_group(ps1[:, :], pairs, MMB),
                   reads=["wB"] + [("P2", hc) for hc in range(6)], writes=["ps1"],
                   name=f"eo{ee}_{oc}")
            p.step("vector", lambda e: e.tensor_mul(b4[:, :], ps1[:, :], cwb[:, :]),
                   reads=["ps1", "cwb"], writes=["b4"])
            p.step("vector", lambda e, oc=oc: e.tensor_add(
                P1[oc][:, :], P1[oc][:, :], b4[:, :]),
                reads=["b4", ("P1", oc)], writes=[("P1", oc)], name=f"acc{ee}_{oc}")

    for oc in range(NCH):
        p.step("scalar", lambda e, oc=oc: e.activation(P1[oc][:, :], P1[oc][:, :], AF.Silu),
               reads=[("P1", oc)], writes=[("P1", oc)], name=f"g0_{oc}")
        if dbg:
            stores.append(p.dma("sync", g0_d.ap()[oc * 128:(oc + 1) * 128, :],
                                P1[oc][:, :], reads=[("P1", oc)]))
        p.dma("sync", f0[:, :], xT[oc * 128:(oc + 1) * 128, :], writes=["f0"])
        p.dma("sync", b2[:, :], s0Ti[oc * 128:(oc + 1) * 128, :], writes=["b2"])
        p.step("vector", lambda e, oc=oc: e.tensor_mul(b4[:, :], b2[:, :], P1[oc][:, :]),
               reads=["b2", ("P1", oc)], writes=["b4"])
        p.step("vector", lambda e: e.tensor_add(f1[:, :], f0[:, :], b4[:, :]),
               reads=["f0", "b4"], writes=["f1"])
        stores.append(p.dma("sync", x1_d.ap()[oc * 128:(oc + 1) * 128, :], f1[:, :],
                            reads=["f1"], writes=[("x1_d", oc)], name=f"stx1{oc}"))
        p.step("vector", lambda e: e.tensor_copy(b4[:, :], f1[:, :]),
               reads=["f1"], writes=["b4"])
        p.step("scalar", lambda e: e.activation(b5[:, :], b4[:, :], AF.Square),
               reads=["b4"], writes=["b5"])
        p.step("tensor", mm_group(ps0[:, :], [(ones_sq[:, :], b4[:, :])], MMB,
                                  start=(oc == 0), stop=(oc == NCH - 1)),
               reads=["b4", "ones_sq"], writes=["ps0"], name=f"s1x{oc}")
        p.step("tensor", mm_group(ps1[:, :], [(ones_sq[:, :], b5[:, :])], MMB,
                                  start=(oc == 0), stop=(oc == NCH - 1)),
               reads=["b5", "ones_sq"], writes=["ps1"], name=f"s1q{oc}")

    def ln_finalize(eps_ap, eps_key, name):
        p.step("vector", lambda e: e.tensor_scalar(f0[0:1, :], ps0[0:1, :], 1.0 / E,
                                                   None, op0=ALU.mult),
               reads=["ps0"], writes=["f0"], name=f"{name}mu")
        p.step("vector", lambda e: e.tensor_scalar(f1[0:1, :], ps1[0:1, :], 1.0 / E,
                                                   None, op0=ALU.mult),
               reads=["ps1"], writes=["f1"], name=f"{name}msq")
        p.step("vector", lambda e: e.tensor_mul(crow[0:1, :], f0[0:1, :], f0[0:1, :]),
               reads=["f0"], writes=["crow"])
        p.step("vector", lambda e: e.tensor_sub(f1[0:1, :], f1[0:1, :], crow[0:1, :]),
               reads=["f1", "crow"], writes=["f1"])
        p.step("scalar", lambda e: e.activation(f1[0:1, :], f1[0:1, :], AF.Sqrt,
                                                bias=eps_ap),
               reads=["f1", eps_key], writes=["f1"], name=f"{name}sd")
        p.step("vector", lambda e: e.reciprocal(crow[0:1, :], f1[0:1, :]),
               reads=["f1"], writes=["crow"], name=f"{name}rstd")
        p.step("vector", lambda e: e.tensor_mul(f0[0:1, :], f0[0:1, :], crow[0:1, :]),
               reads=["f0", "crow"], writes=["f0"])
        p.step("vector", lambda e: e.tensor_scalar(f0[0:1, :], f0[0:1, :], -1.0,
                                                   None, op0=ALU.mult),
               reads=["f0"], writes=["f0"])
        p.step("tensor", mm_group(ps0[:, :], [(ones_row[:, :], crow[0:1, :])], MMF),
               reads=["ones_row", "crow"], writes=["ps0"], name=f"{name}bcr")
        p.step("tensor", mm_group(ps1[:, :], [(ones_row[:, :], f0[0:1, :])], MMF),
               reads=["ones_row", "f0"], writes=["ps1"], name=f"{name}bcn")

    eps5 = a("eps5", [1, 1], F32)
    p.step("vector", lambda e: e.memset(eps5[:, :], 1e-5), writes=["eps5"])
    ln_finalize(eps5[0:1, :], "eps5", "ln1")
    for oc in range(NCH):
        p.dma("sync", f0[:, :], x1_d.ap()[oc * 128:(oc + 1) * 128, :],
              reads=[("x1_d", oc)], writes=["f0"])
        p.step("vector", lambda e: e.tensor_mul(f1[:, :], f0[:, :], ps0[:, :]),
               reads=["f0", "ps0"], writes=["f1"])
        p.step("vector", lambda e, oc=oc: e.tensor_add(xn[oc][:, :], f1[:, :], ps1[:, :]),
               reads=["f1", "ps1"], writes=[("xn", oc)], name=f"xn1_{oc}")
        if dbg:
            stores.append(p.dma("sync", xn1_d.ap()[oc * 128:(oc + 1) * 128, :],
                                xn[oc][:, :], reads=[("xn", oc)]))

    for ic in range(NCH):
        p.dma("gpsimd", wB[:, ic * 512:(ic + 1) * 512],
              wqkv[ic * 128:(ic + 1) * 128, 2 * E:2 * E + 512], writes=["wB"],
              name=f"ldwv_a{ic}")
        p.dma("gpsimd", wB[:, 4096 + ic * 512: 4096 + (ic + 1) * 512],
              wqkv[ic * 128:(ic + 1) * 128, 2 * E + 512:3 * E], writes=["wB"],
              name=f"ldwv_b{ic}")
    for tt in range(NTT):
        pairs = [(xn[ic][:, tt * 128:(tt + 1) * 128], wB[:, ic * 512:(ic + 1) * 512])
                 for ic in range(NCH)]
        pairs2 = [(xn[ic][:, tt * 128:(tt + 1) * 128],
                   wB[:, 4096 + ic * 512:4096 + (ic + 1) * 512]) for ic in range(NCH)]
        p.step("tensor", mm_group(ps0[:, 0:512], pairs, MMB),
               reads=["wB"] + xk, writes=["ps0"], name=f"v1a{tt}")
        p.step("tensor", mm_group(ps0[:, 512:1024], pairs2, MMB),
               reads=["wB"] + xk, writes=["ps0"], name=f"v1b{tt}")
        p.step("scalar", lambda e, tt=tt: e.copy(
            P1[tt // 2][:, (tt % 2) * 1024:(tt % 2) * 1024 + 1024], ps0[:, 0:1024]),
            reads=["ps0"], writes=[("P1", tt // 2)], name=f"v1e{tt}")

    p.dma("sync", gsr[0:1, 0:H], scale_r, writes=["gsr"])
    p.step("tensor", mm_group(ps1[:, 0:H], [(ones_row[:, :], gsr[0:1, :])], MMF),
           reads=["ones_row", "gsr"], writes=["ps1"])
    p.step("scalar", lambda e: e.copy(scvb[:, :], ps1[:, 0:H]),
           reads=["ps1"], writes=["scvb"])
    p.step("vector", lambda e: e.memset(kst[64:65, :], 1.0), writes=["kstones"])

    def v1ap(j, h):
        return P1[j // 2][:, (j % 2) * 1024 + h * 64:(j % 2) * 1024 + h * 64 + 64]

    for h in range(H):
        oc, h2 = h // 2, h % 2
        for ic in range(NCH):
            p.dma("gpsimd", qkw[:, ic * 64:(ic + 1) * 64],
                  wqkv[ic * 128:(ic + 1) * 128, h * 64:(h + 1) * 64],
                  writes=["qkw"], name=f"ldq{h}_{ic}")
            p.dma("gpsimd", qkw[:, 512 + ic * 64: 512 + (ic + 1) * 64],
                  wqkv[ic * 128:(ic + 1) * 128, E + h * 64:E + (h + 1) * 64],
                  writes=["qkw"], name=f"ldk{h}_{ic}")
        pairsQ = [(qkw[:, ic * 64:(ic + 1) * 64], xn[ic][:, :]) for ic in range(NCH)]
        pairsK = [(qkw[:, 512 + ic * 64:512 + (ic + 1) * 64], xn[ic][:, :])
                  for ic in range(NCH)]
        p.step("tensor", mm_group(ps0[0:64, :], pairsQ, MMB),
               reads=["qkw"] + xk, writes=["ps0"], name=f"qproj{h}")
        p.step("scalar", lambda e, h=h: e.activation(
            qst[0:64, :], ps0[0:64, :], AF.Identity, scale=scvb[0:64, h:h + 1]),
            reads=["ps0", "scvb"], writes=["qst"], name=f"qev{h}")
        p.step("tensor", mm_group(ps0[0:64, :], pairsK, MMB),
               reads=["qkw"] + xk, writes=["ps0"], name=f"kproj{h}")
        p.step("scalar", lambda e: e.copy(kst[0:64, :], ps0[0:64, :]),
               reads=["ps0"], writes=["kst"], name=f"kev{h}")
        for i in range(NTT):
            p.step("tensor", mm_group(ps0[:, :],
                                      [(qst[0:64, i * 128:(i + 1) * 128], kst[0:64, :])],
                                      MMB),
                   reads=["qst", "kst", "kstones"], writes=["ps0"], name=f"SA{h}_{i}")
            p.step("vector", lambda e, i=i: e.tensor_add(
                f0[:, 0:128], ps0[:, i * 128:(i + 1) * 128], tA[:, :]),
                reads=["ps0", "tA"], writes=["f0"])
            p.step("vector", lambda e: e.tensor_reduce(mx2[:, :], f0[:, 0:128],
                                                       axis=AX.X, op=ALU.max),
                   reads=["f0"], writes=["mx2"])
            if i > 0:
                p.step("vector", lambda e, i=i: e.tensor_reduce(
                    mx1[:, :], ps0[:, 0:i * 128], axis=AX.X, op=ALU.max),
                    reads=["ps0"], writes=["mx1"])
                p.step("vector", lambda e: e.tensor_max(mx2[:, :], mx2[:, :], mx1[:, :]),
                       reads=["mx1", "mx2"], writes=["mx2"])
            p.step("vector", lambda e, i=i: e.tensor_scalar(
                mcol[:, i:i + 1], mx2[:, :], -1e30, -1.0, op0=ALU.max, op1=ALU.mult),
                reads=["mx2"], writes=["mcol"], name=f"mneg{h}_{i}")
        p.step("tensor", lambda e: e.transpose(ps1[0:16, 0:128], mcol[:, :], idf[:, :]),
               reads=["mcol", "idf"], writes=["ps1"], name=f"mtr{h}")
        p.step("scalar", lambda e: e.copy(msA[:, :], ps1[0:16, 0:128]),
               reads=["ps1"], writes=["msA"], name=f"msev{h}")
        sid = p.dma("sync", mrow_d.ap(), msA[:, :], reads=["msA"],
                    writes=["mrow_d"], name=f"stm{h}")
        p.dma("gpsimd", qst[64:65, :],
              mrow_d.ap().rearrange("a b -> (a b)")[None, :], reads=["mrow_d"],
              writes=["qstm"], deps=[sid], name=f"ldm{h}")
        for j in range(NTT):
            n0 = j * 128
            p.step("tensor", mm_group(ps0[:, n0:T],
                                      [(kst[0:65, n0:n0 + 128], qst[0:65, n0:T])], MMB,
                                      bank_off=n0),
                   reads=["qst", "kst", "qstm", "kstones"], writes=["ps0"],
                   name=f"SB{h}_{j}")
            p.step("vector", lambda e, n0=n0: e.tensor_add(
                f0[:, 0:128], ps0[:, n0:n0 + 128], tB[:, :]),
                reads=["ps0", "tB"], writes=["f0"])
            p.step("scalar", lambda e, n0=n0: e.activation(
                b2[:, n0:n0 + 128], f0[:, 0:128], AF.Exp),
                reads=["f0"], writes=["b2"], name=f"expd{h}_{j}")
            if j < NTT - 1:
                p.step("scalar", lambda e, n0=n0: e.activation(
                    b2[:, n0 + 128:T], ps0[:, n0 + 128:T], AF.Exp),
                    reads=["ps0"], writes=["b2"], name=f"expr{h}_{j}")
            p.step("tensor", mm_denom(ps1, ones_sq, b2, n0, j),
                   reads=["b2", "ones_sq"], writes=["ps1"], name=f"dn{h}_{j}")
            p.step("tensor", mm_o(ps1, v1ap(j, h), b2, n0, j),
                   reads=["b2", ("P1", j // 2)], writes=["ps1"], name=f"O{h}_{j}")
        p.step("scalar", lambda e: e.copy(f0[0:1, :], ps1[0:1, :]),
               reads=["ps1"], writes=["f0"], name=f"dnev{h}")
        p.step("vector", lambda e: e.tensor_scalar(f1[0:1, :], f0[0:1, :], 0.0, None,
                                                   op0=ALU.is_equal),
               reads=["f0"], writes=["f1"])
        p.step("vector", lambda e: e.tensor_add(f0[0:1, :], f0[0:1, :], f1[0:1, :]),
               reads=["f0", "f1"], writes=["f0"])
        p.step("vector", lambda e: e.reciprocal(f0[0:1, :], f0[0:1, :]),
               reads=["f0"], writes=["f0"], name=f"rcp{h}")
        p.step("tensor", mm_group(ps0[:, :], [(ones_row[:, :], f0[0:1, :])], MMF),
               reads=["ones_row", "f0"], writes=["ps0"], name=f"rbb{h}")
        p.step("scalar", lambda e: e.copy(b4[:, :], ps0[:, :]),
               reads=["ps0"], writes=["b4"], name=f"rbe{h}")
        if dbg and h == 0:
            stores.append(p.dma("sync", qst_d.ap(), qst[:, :],
                                reads=["qst", "qstm"]))
            stores.append(p.dma("sync", kst_d.ap(), kst[:, :],
                                reads=["kst", "kstones"]))
            stores.append(p.dma("sync", pt_d.ap(), b2[:, :], reads=["b2"]))
            stores.append(p.dma("sync", dn_d.ap(), f0[0:1, :], reads=["f0"]))
        rows = slice(h2 * 64, h2 * 64 + 64)
        p.step("scalar", lambda e, rows=rows: e.copy(b5[rows, :], ps1[64:128, :]),
               reads=["ps1"], writes=["b5"], name=f"oev{h}")
        p.step("vector", lambda e, oc=oc, rows=rows: e.tensor_mul(
            P2[oc][rows, :], b5[rows, :], b4[rows, :]),
            reads=["b5", "b4"], writes=[("P2", oc)], name=f"onorm{h}")
        if dbg and h == 0:
            stores.append(p.dma("sync", at0_d.ap(), P2[0][:, :],
                                reads=[("P2", 0)]))

    if dbg:
        for ic in range(NCH):
            stores.append(p.dma("sync", at_d.ap()[ic * 128:(ic + 1) * 128, :],
                                P2[ic][:, :], reads=[("P2", ic)]))
    for ic in range(NCH):
        p.dma("gpsimd", wA[:, ic * E:(ic + 1) * E], wo[ic * 128:(ic + 1) * 128, :],
              writes=[("wA", 0), ("wA", 1)], name=f"ldwo{ic}")
    if dbg:
        stores.append(p.dma("sync", wa_d.ap(), wA[:, :], reads=[("wA", 0), ("wA", 1)]))
    for oc in range(NCH):
        p.dma("sync", bias_s[:, 0:1], b_out[oc], writes=["bias_s"])
        pairs = [(wA[:, ic * E + oc * 128: ic * E + oc * 128 + 128], P2[ic][:, :])
                 for ic in range(NCH)]
        p.step("tensor", mm_group(ps0[:, :], pairs, MMB),
               reads=[("wA", 0), ("wA", 1)] + [("P2", ic) for ic in range(NCH)],
               writes=["ps0"], name=f"oproj{oc}")
        p.step("scalar", lambda e: e.activation(b4[:, :], ps0[:, :], AF.Identity,
                                                bias=bias_s[:, 0:1]),
               reads=["ps0", "bias_s"], writes=["b4"], name=f"opev{oc}")
        stores.append(p.dma("sync", s1_d.ap()[oc * 128:(oc + 1) * 128, :], b4[:, :],
                            reads=["b4"], writes=[("s1_d", oc)], name=f"sts1{oc}"))

    for ic in range(NCH):
        p.dma("gpsimd", qkw[:, 1024 + ic * 8:1024 + (ic + 1) * 8],
              WrT[ic * 128:(ic + 1) * 128, :], writes=["qkw"], name=f"ldwr{ic}")
    p.dma("sync", bias_s[0:8, 1:2], br_col, writes=["bias_s"])
    pairs = [(qkw[:, 1024 + ic * 8:1024 + (ic + 1) * 8], xn[ic][:, :])
             for ic in range(NCH)]
    p.step("tensor", mm_group(ps0[0:8, :], pairs, MMB),
           reads=["qkw"] + xk, writes=["ps0"], name="r1mm")
    p.step("scalar", lambda e: e.activation(l1s[:, :], ps0[0:8, :], AF.Identity,
                                            bias=bias_s[0:8, 1:2]),
           reads=["ps0", "bias_s"], writes=["l1s"], name="r1ev")
    for tt in range(NTT):
        p.step("tensor", lambda e, tt=tt: e.transpose(
            ps1[:, 0:8], l1s[:, tt * 128:(tt + 1) * 128], idf[0:8, 0:8]),
            reads=["l1s", "idf"], writes=["ps1"], name=f"r1tr{tt}")
        p.step("scalar", lambda e: e.copy(lt8[:, :], ps1[0:128, 0:8]),
               reads=["ps1"], writes=["lt8"])
        p.step("vector", lambda e: e.tensor_reduce(m1c[:, :], lt8[:, :], axis=AX.X,
                                                   op=ALU.max),
               reads=["lt8"], writes=["m1c"])
        p.step("vector", lambda e: e.tensor_scalar(eq1[:, :], lt8[:, :], m1c[:, :],
                                                   None, op0=ALU.is_ge),
               reads=["lt8", "m1c"], writes=["eq1"])
        p.step("vector", lambda e: e.scalar_tensor_tensor(
            l2f[:, :], eq1[:, :], -1e30, lt8[:, :], op0=ALU.mult, op1=ALU.add),
            reads=["eq1", "lt8"], writes=["l2f"])
        p.step("vector", lambda e: e.tensor_reduce(m2c[:, :], l2f[:, :], axis=AX.X,
                                                   op=ALU.max),
               reads=["l2f"], writes=["m2c"])
        p.step("vector", lambda e: e.tensor_sub(w1c[:, :], m1c[:, :], m2c[:, :]),
               reads=["m1c", "m2c"], writes=["w1c"])
        p.step("scalar", lambda e: e.activation(w1c[:, :], w1c[:, :], AF.Sigmoid),
               reads=["w1c"], writes=["w1c"])
        p.step("vector", lambda e: e.tensor_scalar(w2c[:, :], w1c[:, :], -1.0, 1.0,
                                                   op0=ALU.mult, op1=ALU.add),
               reads=["w1c"], writes=["w2c"])
        p.step("vector", lambda e: e.tensor_scalar(eq2[:, :], l2f[:, :], m2c[:, :],
                                                   None, op0=ALU.is_ge),
               reads=["l2f", "m2c"], writes=["eq2"])
        p.step("vector", lambda e: e.tensor_scalar(cw1[:, :], eq1[:, :], w1c[:, :],
                                                   None, op0=ALU.mult),
               reads=["eq1", "w1c"], writes=["cw1"])
        p.step("vector", lambda e: e.tensor_scalar(eq2[:, :], eq2[:, :], w2c[:, :],
                                                   None, op0=ALU.mult),
               reads=["eq2", "w2c"], writes=["eq2"])
        p.step("vector", lambda e: e.tensor_add(cw1[:, :], cw1[:, :], eq2[:, :]),
               reads=["cw1", "eq2"], writes=["cw1"])
        p.step("tensor", lambda e, tt=tt: e.transpose(ps1[0:8, 8:136], cw1[:, :], idf[:, :]),
               reads=["cw1", "idf"], writes=["ps1"], name=f"cw1tr{tt}")
        p.step("scalar", lambda e: e.copy(trv[0:8, :], ps1[0:8, 8:136]),
               reads=["ps1"], writes=["trv"])
        stores.append(p.dma("sync", cw1_d.ap()[:, tt * 128:(tt + 1) * 128],
                            trv[0:8, :], reads=["trv"], writes=["cw1_d"],
                            name=f"stcw1{tt}"))

    for oc in range(NCH):
        p.step("vector", lambda e, oc=oc: e.memset(P2[oc][:, :], 0.0),
               writes=[("P2", oc)], name=f"acc1z{oc}")

    for ee in range(8):
        p.dma("sync", crow[0:1, :], cw1_d.ap()[ee:ee + 1, :], reads=["cw1_d"],
              writes=["crow"], name=f"ldc1{ee}")
        p.step("tensor", mm_group(ps0[:, :], [(ones_row[:, :], crow[0:1, :])], MMF),
               reads=["ones_row", "crow"], writes=["ps0"], name=f"c1b{ee}")
        p.step("scalar", lambda e: e.copy(cwb[:, :], ps0[:, :]),
               reads=["ps0"], writes=["cwb"], name=f"c1e{ee}")
        p.dma("sync", b1c[:, :], b1_r[ee], writes=["b1c"])
        p.dma("sync", b2c[:, :], b2_r[ee], writes=["b2c"])
        for q in range(4):
            q0 = q * 512
            for fcg in range(8):
                half = fcg % 2
                for ic in range(NCH):
                    p.dma("gpsimd",
                          wA[:, half * 4096 + ic * 512: half * 4096 + (ic + 1) * 512],
                          W1T_all[ee, ic * 128:(ic + 1) * 128,
                                  fcg * 512:(fcg + 1) * 512],
                          writes=[("wA", half)], name=f"w1{ee}_{q}_{fcg}_{ic}")
                for k4 in range(4):
                    fc = fcg * 4 + k4
                    pairs = [(wA[:, half * 4096 + ic * 512 + k4 * 128:
                                 half * 4096 + ic * 512 + k4 * 128 + 128],
                              xn[ic][:, q0:q0 + 512]) for ic in range(NCH)]
                    p.step("tensor", mm_group(ps0[:, 0:512], pairs, MMB),
                           reads=[("wA", half)] + xk, writes=["ps0"],
                           name=f"h{ee}_{q}_{fc}")
                    p.step("scalar", lambda e, fc=fc: e.activation(
                        P1[fc // 4][:, (fc % 4) * 512:(fc % 4) * 512 + 512],
                        ps0[:, 0:512], AF.Gelu, bias=b1c[:, fc:fc + 1]),
                        reads=["ps0", "b1c"], writes=[("P1", fc // 4)],
                        name=f"gelu{ee}_{q}_{fc}")
            for fc in range(32):
                half = fc % 2
                p.dma("gpsimd", w2r[:, half * 1024:(half + 1) * 1024],
                      W2T_all[ee, fc * 128:(fc + 1) * 128, :],
                      writes=[("w2r", half)], name=f"w2{ee}_{q}_{fc}")
                def w2emit(e, half=half, fc=fc):
                    last = None
                    for oc in range(NCH):
                        psv = (ps0 if oc < 4 else ps1)
                        col = (oc % 4) * 512
                        last = e.matmul(
                            psv[:, col:col + 512],
                            w2r[:, half * 1024 + oc * 128: half * 1024 + oc * 128 + 128],
                            P1[fc // 4][:, (fc % 4) * 512:(fc % 4) * 512 + 512],
                            start=(fc == 0), stop=(fc == 31))
                    return last
                p.step("tensor", w2emit,
                       reads=[("w2r", half), ("P1", fc // 4)],
                       writes=["ps0", "ps1"], name=f"w2m{ee}_{q}_{fc}")
            for oc in range(NCH):
                psv = (ps0 if oc < 4 else ps1)
                col = (oc % 4) * 512
                p.step("scalar", lambda e, psv=psv, col=col, oc=oc: e.activation(
                    f0[:, 0:512], psv[:, col:col + 512], AF.Identity,
                    bias=b2c[:, oc:oc + 1]),
                    reads=["ps0" if oc < 4 else "ps1", "b2c"], writes=["f0"], name=f"ye{ee}_{q}_{oc}")
                p.step("vector", lambda e, q0=q0: e.tensor_mul(
                    b4[:, 0:512], f0[:, 0:512], cwb[:, q0:q0 + 512]),
                    reads=["f0", "cwb"], writes=["b4"])
                p.step("vector", lambda e, oc=oc, q0=q0: e.tensor_add(
                    P2[oc][:, q0:q0 + 512], P2[oc][:, q0:q0 + 512], b4[:, 0:512]),
                    reads=["b4", ("P2", oc)], writes=[("P2", oc)],
                    name=f"a1{ee}_{q}_{oc}")

    for oc in range(NCH):
        p.step("vector", lambda e, oc=oc: e.tensor_add(
            P1[oc][:, :], P2[oc][:, :], xn[oc][:, :]),
            reads=[("P2", oc), ("xn", oc)], writes=[("P1", oc)], name=f"z{oc}")
        p.step("scalar", lambda e, oc=oc: e.activation(b5[:, :], P1[oc][:, :], AF.Square),
               reads=[("P1", oc)], writes=["b5"])
        p.step("tensor", mm_group(ps0[:, :], [(ones_sq[:, :], P1[oc][:, :])], MMB,
                                  start=(oc == 0), stop=(oc == NCH - 1)),
               reads=[("P1", oc), "ones_sq"], writes=["ps0"], name=f"s2x{oc}")
        p.step("tensor", mm_group(ps1[:, :], [(ones_sq[:, :], b5[:, :])], MMB,
                                  start=(oc == 0), stop=(oc == NCH - 1)),
               reads=["b5", "ones_sq"], writes=["ps1"], name=f"s2q{oc}")
    ln_finalize(eps2[0:1, :], "eps2", "ln2")
    for oc in range(NCH):
        p.dma("sync", bias_s[:, 2:3], lnw[oc], writes=["bias_s"])
        p.dma("sync", bias_s[:, 3:4], lnb[oc], writes=["bias_s2"])
        p.step("vector", lambda e, oc=oc: e.tensor_mul(f0[:, :], P1[oc][:, :], ps0[:, :]),
               reads=[("P1", oc), "ps0"], writes=["f0"], name=f"zn1_{oc}")
        p.step("vector", lambda e: e.tensor_add(f0[:, :], f0[:, :], ps1[:, :]),
               reads=["f0", "ps1"], writes=["f0"])
        p.step("vector", lambda e: e.tensor_scalar(f0[:, :], f0[:, :], bias_s[:, 2:3],
                                                   bias_s[:, 3:4], op0=ALU.mult,
                                                   op1=ALU.add),
               reads=["f0", "bias_s", "bias_s2"], writes=["f0"], name=f"affine{oc}")
        p.step("scalar", lambda e: e.activation(b4[:, :], f0[:, :], AF.Silu),
               reads=["f0"], writes=["b4"], name=f"g1_{oc}")
        p.dma("sync", f0[:, :], x1_d.ap()[oc * 128:(oc + 1) * 128, :],
              reads=[("x1_d", oc)], writes=["f0"])
        p.dma("sync", b5[:, :], s1_d.ap()[oc * 128:(oc + 1) * 128, :],
              reads=[("s1_d", oc)], writes=["b5"])
        p.step("vector", lambda e: e.tensor_mul(b5[:, :], b5[:, :], b4[:, :]),
               reads=["b5", "b4"], writes=["b5"])
        p.step("vector", lambda e: e.tensor_add(f1[:, :], f0[:, :], b5[:, :]),
               reads=["f0", "b5"], writes=["f1"], name=f"x2_{oc}")
        stores.append(p.dma("sync", outT[oc * 128:(oc + 1) * 128, :], f1[:, :],
                            reads=["f1"], name=f"stout{oc}"))

    p.barrier_step("sync", stores)
    with nc.Block() as block:
        p.emit(block)
    return nc


def mm_denom(ps1, ones_sq, b2, n0, j):
    def emit(eng):
        last = None
        for c0, c1 in piece_splits(T - n0, MMB, n0):
            last = eng.matmul(ps1[0:64, n0 + c0:n0 + c1], ones_sq[:, 0:64],
                              b2[:, n0 + c0:n0 + c1],
                              start=(j == 0), stop=(j == NTT - 1))
        return last
    return emit


def mm_o(ps1, vap, b2, n0, j):
    def emit(eng):
        last = None
        for c0, c1 in piece_splits(T - n0, MMB, n0):
            last = eng.matmul(ps1[64:128, n0 + c0:n0 + c1], vap,
                              b2[:, n0 + c0:n0 + c1],
                              start=(j == 0), stop=(j == NTT - 1))
        return last
    return emit



def build_fused():
    nc = bass.Bass("TRN2", target_bir_lowering=False)
    from concourse.bass_core_rust import IndirectOffsetOnAxis

    def din(name, shape, dt=F32):
        return nc.dram_tensor(name, list(shape), dt, kind="ExternalInput").ap()

    xTa = din("xTa", [E, 8 * T])
    xT = din("xT", [E, T])
    myoff = din("myoff", [128, 1])
    maskb = din("maskb", [128, 128, 1])
    wqT = din("wqT", [E, E]); wkT = din("wkT", [E, E])
    wvT = din("wvT", [E, E]); wgT = din("wgT", [E, E])
    bq = din("bq", [NCH, 128, 1]); bk = din("bk", [NCH, 128, 1])
    bg = din("bg", [NCH, 128, 1]); bv_row = din("bv_row", [1, E])
    gamma = din("gamma", [1, H])
    gwT = din("gwT", [E, 32])
    coss = din("coss", [128, T]); sins = din("sins", [128, T])
    rotT2 = din("rotT2", [128, D])
    gupT_all = din("gupT_all", [32, E, 1536])
    downT_all = din("downT_all", [32, 768, E])
    wqkv = din("wqkv", [E, 3 * E])
    scale_r = din("scale_r", [1, H])
    wo = din("wo", [E, E])
    b_out = din("b_out", [NCH, 128, 1])
    WrT = din("WrT", [E, 8])
    br_col = din("br_col", [8, 1])
    W1T_all = din("W1T_all", [8, E, 4096])
    b1_r = din("b1_r", [8, 128, 32])
    W2T_all = din("W2T_all", [8, 4096, E])
    b2_r = din("b2_r", [8, 128, 8])
    lnw = din("lnw", [NCH, 128, 1])
    lnb = din("lnb", [NCH, 128, 1])
    triA = din("triA", [128, 128])
    triB = din("triB", [128, 128])

    outT = nc.dram_tensor("outT", [E, T], F32, kind="ExternalOutput").ap()

    selp_d = nc.dram_tensor("selp_d", [8 * T, 32], F32, kind="Internal")
    s0_d = nc.dram_tensor("s0_d", [E, T], BF16, kind="Internal")
    x1_d = nc.dram_tensor("x1_d", [E, T], F32, kind="Internal")
    s1_d = nc.dram_tensor("s1_d", [E, T], BF16, kind="Internal")
    mrow_d = nc.dram_tensor("mrow_d", [16, 128], F32, kind="Internal")
    cw_d = nc.dram_tensor("cw_d", [32, T], F32, kind="Internal")
    cw1_d = nc.dram_tensor("cw1_d", [8, T], F32, kind="Internal")

    a = nc.alloc_sbuf_tensor
    xn = [a(f"xn{c}", [128, T], BF16) for c in range(NCH)]
    P1 = [a(f"P1_{c}", [128, T], BF16) for c in range(NCH)]
    P2 = [a(f"P2_{c}", [128, T], BF16) for c in range(NCH)]
    wA = a("wA", [128, 8 * E], BF16)
    wB = a("wB", [128, 6 * E], BF16)
    f0 = a("f0", [128, T], F32); f1 = a("f1", [128, T], F32)
    qst = a("qst", [128, T], BF16); kst = a("kst", [128, T], BF16)
    b2 = a("b2", [128, T], BF16); b4 = a("b4", [128, T], BF16)
    w2r = a("w2r", [128, T], BF16)
    qkw = a("qkw", [128, 1152], BF16)
    cwb = a("cwb", [128, T], BF16)
    tabs = a("tabs", [128, 2 * T], BF16)
    crow = a("crow", [1, T], F32)
    rot_s = a("rot_s", [128, D], BF16)
    gw_s = a("gw_s", [128, 8 * 32], BF16)
    bias_s = a("bias_s", [128, 4], F32)
    bvb = a("bvb", [128, E], F32)
    ones_col = a("ones_col", [128, 1], BF16)
    ones_row = a("ones_row", [1, 128], F32)
    ones_sq = a("ones_sq", [128, 128], BF16)
    idf = a("identS", [128, 128], F32)
    sU = a("sU", [128, 128], BF16)
    tA = a("tA", [128, 128], F32)
    tB = a("tB", [128, 128], F32)
    gsig = a("gsig", [1, H], F32)
    gnb = a("gnb", [128, H], F32)
    scvb = a("scvb", [128, H], F32)
    eps5 = a("eps5", [1, 1], F32)
    eps2 = a("eps2", [1, 1], F32)
    rt = {n: a("rt_" + n, [128, 32], F32)
          for n in ("ltok", "exp", "probs", "rep", "mask", "tmp", "sel")}
    rt_t8 = a("rt_t8", [128, 8], F32)
    rt_nmx = a("rt_nmx", [128, 1], F32)
    rt_sum = a("rt_sum", [128, 1], F32)
    rt_rcp = a("rt_rcp", [128, 1], F32)
    rsel = a("rsel", [128, 32], F32)
    rtile = a("rtile", [128, 32], BF16)
    rmask = a("rmask", [128, 32], BF16)
    mbcol = a("mbcol", [128, 1], F32)
    base_s = a("base_s", [128, 32], F32)
    brun_s = a("brun_s", [128, 32], F32)
    posw = a("posw", [128, 32], F32)
    kept = a("kept", [128, 32], F32)
    wsel = a("wsel", [128, 32], F32)
    wsum = a("wsum", [128, 1], F32)
    wrcp = a("wrcp", [128, 1], F32)
    cwt = a("cwt", [128, 32], F32)
    trv = a("trv", [32, 128], F32)
    mcol = a("mcol", [128, 16], F32)
    msA = a("msA", [16, 128], F32)
    mx1 = a("mx1", [128, 1], F32)
    mx2 = a("mx2", [128, 1], F32)
    b1c = a("b1c", [128, 32], F32)
    b2c = a("b2c", [128, 8], F32)
    r8 = a("r8", [8, 128], F32)
    lt8 = a("lt8", [128, 8], F32)
    l2f = a("l2f", [128, 8], F32)
    eq1 = a("eq1", [128, 8], F32)
    eq2 = a("eq2", [128, 8], F32)
    cw1 = a("cw1", [128, 8], F32)
    m1c = a("m1c", [128, 1], F32)
    m2c = a("m2c", [128, 1], F32)
    w1c = a("w1c", [128, 1], F32)
    w2c = a("w2c", [128, 1], F32)
    idxf = a("idxf", [128, 1], F32)
    idxi = a("idxi", [128, 1], I32)

    ps0 = nc.alloc_psum_tensor("ps0", [128, T], F32)
    ps1 = nc.alloc_psum_tensor("ps1", [128, T], F32)

    ident = nc.inline_tensor(np.eye(128, dtype=np.float32), name="identF32")
    sU_c = nc.inline_tensor(np.triu(np.ones((128, 128), np.float32), 1).astype(BF),
                            name="strictU")

    p = Prog(nc)
    stores = []
    xk = [("xn", c) for c in range(NCH)]
    p1k = [("P1", c) for c in range(NCH)]
    p2k = [("P2", c) for c in range(NCH)]

    p.step("vector", lambda e: e.memset(ones_col[:, :], 1.0), writes=["ones_col"])
    p.step("vector", lambda e: e.memset(ones_row[:, :], 1.0), writes=["ones_row"])
    p.step("vector", lambda e: e.memset(ones_sq[:, :], 1.0), writes=["ones_sq"])
    p.step("vector", lambda e: e.memset(eps5[:, :], 1e-5), writes=["eps5"])
    p.step("vector", lambda e: e.memset(eps2[:, :], 1e-12), writes=["eps2"])
    p.step("vector", lambda e: e.memset(brun_s[:, :], 0.0), writes=["brun_s"])
    p.dma("gpsimd", idf[:, :], ident.ap(), writes=["idf"])
    p.dma("gpsimd", sU[:, :], sU_c.ap(), writes=["sU"])
    p.dma("sync", tA[:, :], triA, writes=["tA"])
    p.dma("sync", tB[:, :], triB, writes=["tB"])
    p.dma("gpsimd", tabs[:, 0:T], coss, writes=["tabs"])
    p.dma("gpsimd", tabs[:, T:2 * T], sins, writes=["tabs"])
    p.dma("gpsimd", rot_s[:, :], rotT2, writes=["rot_s"])
    p.dma("gpsimd", gw_s[:, :].rearrange("p (c e) -> p c e", c=NCH),
          gwT.rearrange("(c p) e -> p c e", p=128), writes=["gw_s"])
    p.dma("sync", gsig[0:1, 0:H], gamma, writes=["gsig"])
    p.step("scalar", lambda e: e.activation(gsig[0:1, :], gsig[0:1, :], AF.Sigmoid),
           reads=["gsig"], writes=["gsig"])
    p.step("vector", lambda e: e.tensor_scalar(gsig[0:1, :], gsig[0:1, :], -1.0,
                                               None, op0=ALU.mult),
           reads=["gsig"], writes=["gsig"])
    p.step("tensor", mm_group(ps1[:, 0:H], [(ones_row[:, :], gsig[0:1, :])], MMF),
           reads=["ones_row", "gsig"], writes=["ps1"])
    p.step("scalar", lambda e: e.copy(gnb[:, :], ps1[:, 0:H]),
           reads=["ps1"], writes=["gnb"])
    p.dma("sync", crow[0:1, 0:E], bv_row, writes=["crow"])
    p.step("tensor", mm_group(ps0[:, 0:E], [(ones_row[:, :], crow[0:1, 0:E])], MMF),
           reads=["ones_row", "crow"], writes=["ps0"])
    p.step("scalar", lambda e: e.copy(bvb[:, :], ps0[:, 0:E]),
           reads=["ps0"], writes=["bvb"])

    def ln_stats_and_bcast(src_tiles, sq_tiles, eps_ap, eps_key, name):
        for c in range(NCH):
            p.step("tensor", mm_group(ps0[:, :], [(ones_sq[:, :], src_tiles[c][1])],
                                      MMB, start=(c == 0), stop=(c == NCH - 1)),
                   reads=[src_tiles[c][0], "ones_sq"], writes=["ps0"],
                   name=f"{name}sx{c}")
            p.step("tensor", mm_group(ps1[:, :], [(ones_sq[:, :], sq_tiles[c][1])],
                                      MMB, start=(c == 0), stop=(c == NCH - 1)),
                   reads=[sq_tiles[c][0], "ones_sq"], writes=["ps1"],
                   name=f"{name}sq{c}")
        p.step("vector", lambda e: e.tensor_scalar(f0[0:1, :], ps0[0:1, :], 1.0 / E,
                                                   None, op0=ALU.mult),
               reads=["ps0"], writes=["f0"], name=f"{name}mu")
        p.step("vector", lambda e: e.tensor_scalar(f1[0:1, :], ps1[0:1, :], 1.0 / E,
                                                   None, op0=ALU.mult),
               reads=["ps1"], writes=["f1"], name=f"{name}msq")
        p.step("vector", lambda e: e.tensor_mul(crow[0:1, :], f0[0:1, :], f0[0:1, :]),
               reads=["f0"], writes=["crow"])
        p.step("vector", lambda e: e.tensor_sub(f1[0:1, :], f1[0:1, :], crow[0:1, :]),
               reads=["f1", "crow"], writes=["f1"])
        p.step("scalar", lambda e: e.activation(f1[0:1, :], f1[0:1, :], AF.Sqrt,
                                                bias=eps_ap),
               reads=["f1", eps_key], writes=["f1"], name=f"{name}sd")
        p.step("vector", lambda e: e.reciprocal(crow[0:1, :], f1[0:1, :]),
               reads=["f1"], writes=["crow"], name=f"{name}rstd")
        p.step("vector", lambda e: e.tensor_mul(f0[0:1, :], f0[0:1, :], crow[0:1, :]),
               reads=["f0", "crow"], writes=["f0"])
        p.step("vector", lambda e: e.tensor_scalar(f0[0:1, :], f0[0:1, :], -1.0,
                                                   None, op0=ALU.mult),
               reads=["f0"], writes=["f0"])
        p.step("tensor", mm_group(ps0[:, :], [(ones_row[:, :], crow[0:1, :])], MMF),
               reads=["ones_row", "crow"], writes=["ps0"], name=f"{name}bcr")
        p.step("tensor", mm_group(ps1[:, :], [(ones_row[:, :], f0[0:1, :])], MMF),
               reads=["ones_row", "f0"], writes=["ps1"], name=f"{name}bcn")

    def router_tiles(n_tt, logit_key, store_fn, name):
        p.step("scalar", lambda e: e.copy(trv[:, :], ps0[0:32, 0:128]),
               reads=[logit_key], writes=["trv"])
        for tt in range(n_tt):
            p.step("tensor", lambda e, tt=tt: e.transpose(
                ps1[:, 0:32], ps0x_ref[0](tt), idf[0:32, 0:32]),
                reads=[logit_key, "idf", "lgsb"], writes=["ps1"],
                name=f"{name}tr{tt}")
            p.step("scalar", lambda e: e.copy(rt["ltok"][:, :], ps1[0:128, 0:32]),
                   reads=["ps1"], writes=["ltok"])
            p.step("vector", lambda e: e.tensor_reduce(
                rt_nmx[:, :], rt["ltok"][:, :], axis=AX.X, op=ALU.max, negate=True),
                reads=["ltok"], writes=["nmx"])
            p.step("scalar", lambda e: e.activation(
                rt["exp"][:, :], rt["ltok"][:, :], AF.Exp, bias=rt_nmx[:, :],
                accum_out=rt_sum[:, :]),
                reads=["ltok", "nmx"], writes=["rexp", "rsum"])
            p.step("vector", lambda e: e.reciprocal(rt_rcp[:, :], rt_sum[:, :]),
                   reads=["rsum"], writes=["rrcp"])
            p.step("vector", lambda e: e.tensor_scalar(
                rt["probs"][:, :], rt["exp"][:, :], rt_rcp[:, :], None, op0=ALU.mult),
                reads=["rexp", "rrcp"], writes=["probs"])
            p.step("vector", lambda e: e.max(rt_t8[:, :], rt["probs"][:, :]),
                   reads=["probs"], writes=["t8"], sync_same=True)
            p.step("vector", lambda e: e.match_replace(
                rt["rep"][:, :], rt_t8[:, :], rt["probs"][:, :], -1.0),
                reads=["t8", "probs"], writes=["rrep"], sync_same=True)
            p.step("vector", lambda e: e.tensor_tensor(
                rt["mask"][:, :], rt["rep"][:, :], rt["probs"][:, :], op=ALU.is_lt),
                reads=["rrep", "probs"], writes=["rmask"])
            p.step("vector", lambda e: e.tensor_scalar(
                rt["tmp"][:, :], rt["probs"][:, :], 1.0, None, op0=ALU.add),
                reads=["probs"], writes=["rtmp"])
            p.step("vector", lambda e: e.tensor_mul(
                rt["tmp"][:, :], rt["tmp"][:, :], rt["mask"][:, :]),
                reads=["rtmp", "rmask"], writes=["rtmp"])
            p.step("vector", lambda e: e.tensor_scalar(
                rt["sel"][:, :], rt["tmp"][:, :], -1.0, None, op0=ALU.add),
                reads=["rtmp"], writes=["rsel2"])
            store_fn(tt)

    lgsb = a("lgsb", [32, 128], F32)
    ps0x_ref = [None]

    for blk in range(8):
        t0 = blk * T
        for ic in range(NCH):
            p.dma("gpsimd", P1[ic][:, :], xTa[ic * 128:(ic + 1) * 128, t0:t0 + T],
                  writes=[("P1", ic)], name=f"rax{blk}_{ic}")
            p.step("scalar", lambda e, ic=ic: e.activation(P2[ic][:, :], P1[ic][:, :],
                                                           AF.Square),
                   reads=[("P1", ic)], writes=[("P2", ic)])
        ln_stats_and_bcast([(("P1", c), P1[c][:, :]) for c in range(NCH)],
                           [(("P2", c), P2[c][:, :]) for c in range(NCH)],
                           eps5[0:1, :], "eps5", f"ra{blk}")
        for ic in range(NCH):
            p.dma("sync", f0[:, :], xTa[ic * 128:(ic + 1) * 128, t0:t0 + T],
                  writes=["f0"], name=f"raxf{blk}_{ic}")
            p.step("vector", lambda e: e.tensor_mul(f1[:, :], f0[:, :], ps0[:, :]),
                   reads=["f0", "ps0"], writes=["f1"])
            p.step("vector", lambda e, ic=ic: e.tensor_add(
                P1[ic][:, :], f1[:, :], ps1[:, :]),
                reads=["f1", "ps1"], writes=[("P1", ic)], name=f"raxn{blk}_{ic}")
        pairs = [(gw_s[:, ic * 32:(ic + 1) * 32], P1[ic][:, :]) for ic in range(NCH)]
        p.step("tensor", mm_group(ps0[0:32, :], pairs, MMB),
               reads=["gw_s"] + p1k, writes=["ps0"], name=f"ralg{blk}")

        def _tr_src(tt):
            return lgsb[:, :]
        ps0x_ref[0] = _tr_src

        def store_sel(tt, blk=blk):
            stores.append(p.dma(
                "sync", selp_d.ap()[blk * T + tt * 128: blk * T + (tt + 1) * 128, :],
                rt["sel"][:, :], reads=["rsel2"], writes=["selp_d"],
                name=f"rast{blk}_{tt}"))

        for tt in range(NTT):
            p.step("scalar", lambda e, tt=tt: e.copy(
                lgsb[:, :], ps0[0:32, tt * 128:(tt + 1) * 128]),
                reads=["ps0"], writes=["lgsb"], name=f"ralg_ev{blk}_{tt}")
            p.step("tensor", lambda e: e.transpose(ps1[:, 0:32], lgsb[:, :],
                                                   idf[0:32, 0:32]),
                   reads=["lgsb", "idf"], writes=["ps1"], name=f"ratr{blk}_{tt}")
            p.step("scalar", lambda e: e.copy(rt["ltok"][:, :], ps1[0:128, 0:32]),
                   reads=["ps1"], writes=["ltok"])
            p.step("vector", lambda e: e.tensor_reduce(
                rt_nmx[:, :], rt["ltok"][:, :], axis=AX.X, op=ALU.max, negate=True),
                reads=["ltok"], writes=["nmx"])
            p.step("scalar", lambda e: e.activation(
                rt["exp"][:, :], rt["ltok"][:, :], AF.Exp, bias=rt_nmx[:, :],
                accum_out=rt_sum[:, :]),
                reads=["ltok", "nmx"], writes=["rexp", "rsum"])
            p.step("vector", lambda e: e.reciprocal(rt_rcp[:, :], rt_sum[:, :]),
                   reads=["rsum"], writes=["rrcp"])
            p.step("vector", lambda e: e.tensor_scalar(
                rt["probs"][:, :], rt["exp"][:, :], rt_rcp[:, :], None, op0=ALU.mult),
                reads=["rexp", "rrcp"], writes=["probs"])
            p.step("vector", lambda e: e.max(rt_t8[:, :], rt["probs"][:, :]),
                   reads=["probs"], writes=["t8"], sync_same=True)
            p.step("vector", lambda e: e.match_replace(
                rt["rep"][:, :], rt_t8[:, :], rt["probs"][:, :], -1.0),
                reads=["t8", "probs"], writes=["rrep"], sync_same=True)
            p.step("vector", lambda e: e.tensor_tensor(
                rt["mask"][:, :], rt["rep"][:, :], rt["probs"][:, :], op=ALU.is_lt),
                reads=["rrep", "probs"], writes=["rmask"])
            p.step("vector", lambda e: e.tensor_scalar(
                rt["tmp"][:, :], rt["probs"][:, :], 1.0, None, op0=ALU.add),
                reads=["probs"], writes=["rtmp"])
            p.step("vector", lambda e: e.tensor_mul(
                rt["tmp"][:, :], rt["tmp"][:, :], rt["mask"][:, :]),
                reads=["rtmp", "rmask"], writes=["rtmp"])
            p.step("vector", lambda e: e.tensor_scalar(
                rt["sel"][:, :], rt["tmp"][:, :], -1.0, None, op0=ALU.add),
                reads=["rtmp"], writes=["rsel2"])
            store_sel(tt)

    for c in range(NCH):
        xb, kb = (qst, "qst") if c % 2 == 0 else (kst, "kst")
        xq, kq = (b2, "b2") if c % 2 == 0 else (b4, "b4")
        p.dma("gpsimd", xb[:, :], xT[c * 128:(c + 1) * 128, :], writes=[kb],
              name=f"l0x{c}")
        p.step("scalar", lambda e, xq=xq, xb=xb: e.activation(xq[:, :], xb[:, :],
                                                              AF.Square),
               reads=[kb], writes=[kq])
        p.step("tensor", mm_group(ps0[:, :], [(ones_sq[:, :], xb[:, :])], MMB,
                                  start=(c == 0), stop=(c == NCH - 1)),
               reads=[kb, "ones_sq"], writes=["ps0"], name=f"l0sx{c}")
        p.step("tensor", mm_group(ps1[:, :], [(ones_sq[:, :], xq[:, :])], MMB,
                                  start=(c == 0), stop=(c == NCH - 1)),
               reads=[kq, "ones_sq"], writes=["ps1"], name=f"l0sq{c}")
    p.step("vector", lambda e: e.tensor_scalar(f0[0:1, :], ps0[0:1, :], 1.0 / E,
                                               None, op0=ALU.mult),
           reads=["ps0"], writes=["f0"], name="l0mu")
    p.step("vector", lambda e: e.tensor_scalar(f1[0:1, :], ps1[0:1, :], 1.0 / E,
                                               None, op0=ALU.mult),
           reads=["ps1"], writes=["f1"])
    p.step("vector", lambda e: e.tensor_mul(crow[0:1, :], f0[0:1, :], f0[0:1, :]),
           reads=["f0"], writes=["crow"])
    p.step("vector", lambda e: e.tensor_sub(f1[0:1, :], f1[0:1, :], crow[0:1, :]),
           reads=["f1", "crow"], writes=["f1"])
    p.step("scalar", lambda e: e.activation(f1[0:1, :], f1[0:1, :], AF.Sqrt,
                                            bias=eps5[0:1, :]),
           reads=["f1", "eps5"], writes=["f1"])
    p.step("vector", lambda e: e.reciprocal(crow[0:1, :], f1[0:1, :]),
           reads=["f1"], writes=["crow"])
    p.step("vector", lambda e: e.tensor_mul(f0[0:1, :], f0[0:1, :], crow[0:1, :]),
           reads=["f0", "crow"], writes=["f0"])
    p.step("vector", lambda e: e.tensor_scalar(f0[0:1, :], f0[0:1, :], -1.0,
                                               None, op0=ALU.mult),
           reads=["f0"], writes=["f0"])
    p.step("tensor", mm_group(ps0[:, :], [(ones_row[:, :], crow[0:1, :])], MMF),
           reads=["ones_row", "crow"], writes=["ps0"], name="l0bcr")
    p.step("tensor", mm_group(ps1[:, :], [(ones_row[:, :], f0[0:1, :])], MMF),
           reads=["ones_row", "f0"], writes=["ps1"], name="l0bcn")
    for c in range(NCH):
        p.dma("sync", f0[:, :], xT[c * 128:(c + 1) * 128, :], writes=["f0"])
        p.step("vector", lambda e: e.tensor_mul(f1[:, :], f0[:, :], ps0[:, :]),
               reads=["f0", "ps0"], writes=["f1"])
        p.step("vector", lambda e, c=c: e.tensor_add(xn[c][:, :], f1[:, :], ps1[:, :]),
               reads=["f1", "ps1"], writes=[("xn", c)], name=f"l0xn{c}")

    for ic in range(NCH):
        p.dma("gpsimd", wA[:, ic * E:(ic + 1) * E],
              wvT[ic * 128:(ic + 1) * 128, :], writes=[("wA", 0), ("wA", 1)],
              name=f"l0wv{ic}")
    for tt in range(NTT):
        pairs = [(xn[ic][:, tt * 128:(tt + 1) * 128], wA[:, ic * E:(ic + 1) * E])
                 for ic in range(NCH)]
        p.step("tensor", mm_group(ps0[:, 0:E], pairs, MMB),
               reads=xk + [("wA", h_, ic) for h_ in (0, 1) for ic in range(NCH)], writes=["ps0"], name=f"l0v{tt}")
        p.step("vector", lambda e, tt=tt: e.scalar_tensor_tensor(
            P1[tt // 2][:, (tt % 2) * 1024:(tt % 2) * 1024 + 1024],
            ps0[:, 0:E], 1.0, bvb[:, :], op0=ALU.mult, op1=ALU.add),
            reads=["ps0", "bvb"], writes=[("P1", tt // 2)], name=f"l0ve{tt}")

    for ic in range(NCH):
        p.dma("gpsimd", wA[:, ic * E:(ic + 1) * E], wqT[ic * 128:(ic + 1) * 128, :],
              writes=[("wA", 0, ic), ("wA", 1, ic)], name=f"l0wq{ic}", sync_same=False)
        p.dma("gpsimd", wB[:, ic * 768:(ic + 1) * 768],
              wkT[ic * 128:(ic + 1) * 128, 0:768], writes=[("wB", ic % 6)], name=f"l0wka{ic}", sync_same=False)
    wk_sp = a("wk_sp", [128, 2048], BF16)
    for ic in range(NCH):
        p.dma("gpsimd", wk_sp[:, ic * 256:(ic + 1) * 256],
              wkT[ic * 128:(ic + 1) * 128, 768:1024], writes=["wk_sp"],
              name=f"l0wkb{ic}", sync_same=False)

    def wk_slice(ic, oc):
        if oc < 6:
            return wB[:, ic * 768 + oc * 128: ic * 768 + oc * 128 + 128]
        return wk_sp[:, ic * 256 + (oc - 6) * 128: ic * 256 + (oc - 6) * 128 + 128]

    def l0_proj_rot(oc, which, wsl, wkeys, btab, bcol, dst, dkey):
        p.dma("sync", bias_s[:, bcol:bcol + 1], btab[oc], writes=["bias_s"],
              name=f"l0b{which}{oc}")
        pairs = [(wsl(ic, oc), xn[ic][:, :]) for ic in range(NCH)]
        p.step("tensor", mm_group(ps0[:, :], pairs, MMB),
               reads=wkeys + xk, writes=["ps0"], name=f"l0p{which}{oc}")
        p.step("scalar", lambda e: e.activation(b2[:, :], ps0[:, :], AF.Identity,
                                                bias=bias_s[:, bcol:bcol + 1]),
               reads=["ps0", "bias_s"], writes=["b2"], name=f"l0e{which}{oc}")
        for h2 in range(2):
            rows = slice(h2 * D, (h2 + 1) * D)
            p.step("tensor", mm_group(ps1[h2 * D:(h2 + 1) * D, :],
                                      [(rot_s[rows, :], b2[rows, :])], MMB),
                   reads=["rot_s", "b2"], writes=["ps1"], name=f"l0r{which}{oc}_{h2}")
            p.step("vector", lambda e, rows=rows: e.tensor_mul(
                f0[rows, :], b2[rows, :], tabs[rows, 0:T]),
                reads=["b2", "tabs"], writes=["f0"])
            p.step("vector", lambda e, rows=rows: e.tensor_mul(
                f1[rows, :], ps1[rows, :], tabs[rows, T:2 * T]),
                reads=["ps1", "tabs"], writes=["f1"])
            p.step("vector", lambda e, dst=dst, rows=rows: e.tensor_add(
                dst[rows, :], f0[rows, :], f1[rows, :]),
                reads=["f0", "f1"], writes=[dkey], name=f"l0rc{which}{oc}_{h2}")

    for oc in range(NCH):
        l0_proj_rot(oc, "q", lambda ic, oc: wA[:, ic * E + oc * 128: ic * E + oc * 128 + 128],
                    [("wA", h_, ic) for h_ in (0, 1) for ic in range(NCH)],
                    bq, 0, qst, "qst")
        l0_proj_rot(oc, "k", wk_slice, [("wB", i6) for i6 in range(6)] + ["wk_sp"], bk, 1, kst, "kst")
        for h2 in range(2):
            h = 2 * oc + h2
            rows = slice(h2 * D, (h2 + 1) * D)
            for j in range(NTT):
                p.step("tensor", mm_group(ps0[:, :],
                                          [(kst[rows, j * 128:(j + 1) * 128],
                                            qst[rows, :])], MMB),
                       reads=["qst", "kst"], writes=["ps0"], name=f"l0S{h}_{j}")
                p.step("gpsimd", lambda e, j=j: e.iota(
                    f0[:, :], pattern=[[1, T]], base=-128 * j,
                    channel_multiplier=-1, allow_small_or_imprecise_dtypes=True),
                    writes=["f0"], name=f"l0io{h}_{j}")
                p.step("vector", lambda e: e.tensor_relu(f1[:, :], f0[:, :]),
                       reads=["f0"], writes=["f1"])
                p.step("scalar", lambda e, h=h: e.activation(
                    f1[:, :], f1[:, :], AF.Exp, scale=gnb[:, h:h + 1]),
                    reads=["f1", "gnb"], writes=["f1"], name=f"l0tw{h}_{j}")
                p.step("vector", lambda e: e.tensor_mul(b4[:, :], ps0[:, :], f1[:, :]),
                       reads=["ps0", "f1"], writes=["b4"], name=f"l0pt{h}_{j}")
                p.step("tensor", mm_group(ps1[0:D, :],
                                          [(P1[j // 2][:, (j % 2) * 1024 + h * 64:
                                                       (j % 2) * 1024 + h * 64 + 64],
                                            b4[:, :])],
                                          MMB, start=(j == 0), stop=(j == NTT - 1)),
                       reads=[("P1", j // 2), "b4"], writes=["ps1"],
                       name=f"l0O{h}_{j}")
            p.step("scalar", lambda e, oc=oc, rows=rows: e.copy(
                P2[oc][rows, :], ps1[0:D, :]),
                reads=["ps1"], writes=[("P2", oc)], name=f"l0oe{h}")

    for ic in range(NCH):
        p.dma("gpsimd", wA[:, ic * E:(ic + 1) * E], wgT[ic * 128:(ic + 1) * 128, :],
              writes=[("wA", 0, ic), ("wA", 1, ic)], name=f"l0wg{ic}", sync_same=False)
    for oc in range(NCH):
        p.dma("sync", bias_s[:, 2:3], bg[oc], writes=["bias_s"])
        pairs = [(wA[:, ic * E + oc * 128: ic * E + oc * 128 + 128], xn[ic][:, :])
                 for ic in range(NCH)]
        p.step("tensor", mm_group(ps0[:, :], pairs, MMB),
               reads=[("wA", h_, ic) for h_ in (0, 1) for ic in range(NCH)] + xk, writes=["ps0"], name=f"l0g{oc}")
        p.step("scalar", lambda e: e.activation(b2[:, :], ps0[:, :], AF.Silu,
                                                bias=bias_s[:, 2:3]),
               reads=["ps0", "bias_s"], writes=["b2"], name=f"l0ge{oc}")
        p.step("vector", lambda e, oc=oc: e.tensor_mul(b4[:, :], P2[oc][:, :], b2[:, :]),
               reads=[("P2", oc), "b2"], writes=["b4"], name=f"l0s0{oc}")
        stores.append(p.dma("sync", s0_d.ap()[oc * 128:(oc + 1) * 128, :], b4[:, :],
                            reads=["b4"], writes=[("s0_d", oc)], name=f"l0st{oc}"))

    myoff_s = a("myoff_s", [128, 1], F32)
    p.dma("sync", myoff_s[:, :], myoff, writes=["myoff_s"])

    for jc in range(128):
        p.dma("sync", rsel[:, :], selp_d.ap()[jc * 128:(jc + 1) * 128, :],
              reads=["selp_d"], writes=["rsel"], name=f"ldsel{jc}")
        p.dma("sync", mbcol[:, :], maskb[jc], writes=["mbcol"])
        p.step("vector", lambda e: e.tensor_scalar(rtile[:, :], rsel[:, :], -0.5,
                                                   None, op0=ALU.is_gt),
               reads=["rsel"], writes=["rtile"])
        p.step("vector", lambda e: e.tensor_scalar(rmask[:, :], rtile[:, :],
                                                   mbcol[:, :], None, op0=ALU.mult),
               reads=["rtile", "mbcol"], writes=["rmask"])
        p.step("tensor", mm_group(ps0[:, 0:32], [(ones_sq[:, :], rmask[:, :])], MMB,
                                  start=(jc == 0), stop=(jc == 127)),
               reads=["rmask", "ones_sq"], writes=["ps0"], name=f"base{jc}")
    p.step("scalar", lambda e: e.copy(base_s[:, :], ps0[:, 0:32]),
           reads=["ps0"], writes=["base_s"])

    for tt in range(NTT):
        p.step("vector", lambda e, tt=tt: e.tensor_scalar(
            idxf[:, :], myoff_s[:, :], float(128 * tt), None, op0=ALU.add),
            reads=["myoff_s"], writes=["idxf"])
        p.step("vector", lambda e: e.tensor_copy(idxi[:, :], idxf[:, :]),
               reads=["idxf"], writes=["idxi"])
        p.step("gpsimd", lambda e: e.indirect_dma_start(
            out=rsel[:, :], out_offset=None, in_=selp_d.ap(),
            in_offset=IndirectOffsetOnAxis(idxi[:, :], 0),
            bounds_check=8 * T - 1, oob_is_err=False),
            reads=["selp_d", "idxi"], writes=["rsel"], inc=16,
            name=f"ldselmy{tt}")
        p.step("vector", lambda e: e.tensor_scalar(rtile[:, :], rsel[:, :], -0.5,
                                                   None, op0=ALU.is_gt),
               reads=["rsel"], writes=["rtile"])
        p.step("tensor", mm_group(ps0[:, 32:64], [(sU[:, :], rtile[:, :])], MMB),
               reads=["rtile", "sU"], writes=["ps0"], name=f"posw{tt}")
        p.step("scalar", lambda e: e.copy(posw[:, :], ps0[:, 32:64]),
               reads=["ps0"], writes=["posw"])
        p.step("vector", lambda e: e.tensor_add(posw[:, :], posw[:, :], brun_s[:, :]),
               reads=["posw", "brun_s"], writes=["posw"])
        p.step("vector", lambda e: e.tensor_add(posw[:, :], posw[:, :], base_s[:, :]),
               reads=["posw", "base_s"], writes=["posw"])
        p.step("tensor", mm_group(ps1[:, 0:32], [(ones_sq[:, :], rtile[:, :])], MMB,
                                  start=(tt == 0), stop=(tt == NTT - 1)),
               reads=["rtile", "ones_sq"], writes=["ps1"], name=f"brun{tt}")
        p.step("scalar", lambda e: e.copy(brun_s[:, :], ps1[:, 0:32]),
               reads=["ps1"], writes=["brun_s"])
        p.step("vector", lambda e: e.tensor_scalar(kept[:, :], posw[:, :], 1023.5,
                                                   None, op0=ALU.is_le),
               reads=["posw"], writes=["kept"])
        p.step("vector", lambda e: e.tensor_relu(wsel[:, :], rsel[:, :]),
               reads=["rsel"], writes=["wsel"])
        p.step("vector", lambda e: e.tensor_reduce(wsum[:, :], wsel[:, :], axis=AX.X,
                                                   op=ALU.add),
               reads=["wsel"], writes=["wsum"])
        p.step("vector", lambda e: e.reciprocal(wrcp[:, :], wsum[:, :]),
               reads=["wsum"], writes=["wrcp"])
        p.step("vector", lambda e: e.tensor_scalar(wsel[:, :], wsel[:, :], wrcp[:, :],
                                                   None, op0=ALU.mult),
               reads=["wsel", "wrcp"], writes=["wsel"])
        p.step("vector", lambda e: e.tensor_mul(cwt[:, :], wsel[:, :], kept[:, :]),
               reads=["wsel", "kept"], writes=["cwt"])
        p.step("tensor", lambda e: e.transpose(ps1[0:32, 32:160], cwt[:, :], idf[:, :]),
               reads=["cwt", "idf"], writes=["ps1"], name=f"cwtr{tt}")
        p.step("scalar", lambda e: e.copy(trv[:, :], ps1[0:32, 32:160]),
               reads=["ps1"], writes=["trv"])
        stores.append(p.dma("sync", cw_d.ap()[:, tt * 128:(tt + 1) * 128], trv[:, :],
                            reads=["trv"], writes=["cw_d"], name=f"stcw{tt}"))

    for oc in range(NCH):
        p.step("vector", lambda e, oc=oc: e.memset(P1[oc][:, :], 0.0),
               writes=[("P1", oc)])
    for ee in range(32):
        p.dma("sync", crow[0:1, :], cw_d.ap()[ee:ee + 1, :], reads=["cw_d"],
              writes=["crow"], name=f"ldcw{ee}")
        p.step("tensor", mm_group(ps0[:, :], [(ones_row[:, :], crow[0:1, :])], MMF),
               reads=["ones_row", "crow"], writes=["ps0"], name=f"cwb{ee}")
        p.step("scalar", lambda e: e.copy(cwb[:, :], ps0[:, :]),
               reads=["ps0"], writes=["cwb"])
        for g3 in range(3):
            half = (g3 % 2)
            for ic in range(NCH):
                p.dma("gpsimd", wA[:, half * 4096 + ic * 512: half * 4096 + (ic + 1) * 512],
                      gupT_all[ee, ic * 128:(ic + 1) * 128, g3 * 512:(g3 + 1) * 512],
                      writes=[("wA", half, ic)], name=f"gup{ee}_{g3}_{ic}",
                      sync_same=False)
            for k4 in range(4):
                fc = g3 * 4 + k4
                pairs = [(wA[:, half * 4096 + ic * 512 + k4 * 128:
                             half * 4096 + ic * 512 + k4 * 128 + 128], xn[ic][:, :])
                         for ic in range(NCH)]
                psv, pk = ((ps1, "ps1") if fc % 2 == 0 else (ps0, "ps0"))
                p.step("tensor", mm_group(psv[:, :], pairs, MMB),
                       reads=[("wA", half, ic) for ic in range(NCH)] + xk,
                       writes=[pk], name=f"gu{ee}_{fc}")
                if fc < 6:
                    p.step("scalar", lambda e, fc=fc, psv=psv: e.activation(
                        P2[fc][:, :], psv[:, :], AF.Silu),
                        reads=[pk], writes=[("P2", fc)], name=f"gate{ee}_{fc}")
                else:
                    p.step("vector", lambda e, fc=fc, psv=psv: e.tensor_mul(
                        P2[fc - 6][:, :], P2[fc - 6][:, :], psv[:, :]),
                        reads=[pk, ("P2", fc - 6)], writes=[("P2", fc - 6)],
                        name=f"hh{ee}_{fc}")
        for hc in range(6):
            p.dma("gpsimd", wB[:, hc * E:(hc + 1) * E],
                  downT_all[ee, hc * 128:(hc + 1) * 128, :], writes=[("wB", hc)],
                  name=f"dwn{ee}_{hc}", sync_same=False)
        for oc in range(NCH):
            pairs = [(wB[:, hc * E + oc * 128: hc * E + oc * 128 + 128], P2[hc][:, :])
                     for hc in range(6)]
            psv, pk = ((ps1, "ps1") if oc % 2 == 0 else (ps0, "ps0"))
            bo, bok = ((b4, "b4") if oc % 2 == 0 else (b2, "b2"))
            p.step("tensor", mm_group(psv[:, :], pairs, MMB),
                   reads=[("wB", hc) for hc in range(6)] +
                         [("P2", hc) for hc in range(6)], writes=[pk],
                   name=f"eo{ee}_{oc}")
            p.step("vector", lambda e, psv=psv, bo=bo: e.tensor_mul(
                bo[:, :], psv[:, :], cwb[:, :]),
                reads=[pk, "cwb"], writes=[bok])
            p.step("vector", lambda e, oc=oc, bo=bo: e.tensor_add(
                P1[oc][:, :], P1[oc][:, :], bo[:, :]),
                reads=[bok, ("P1", oc)], writes=[("P1", oc)], name=f"acc{ee}_{oc}")

    for oc in range(NCH):
        p.step("scalar", lambda e, oc=oc: e.activation(P1[oc][:, :], P1[oc][:, :], AF.Silu),
               reads=[("P1", oc)], writes=[("P1", oc)], name=f"g0_{oc}")
        p.dma("sync", f0[:, :], xT[oc * 128:(oc + 1) * 128, :], writes=["f0"])
        p.dma("sync", b2[:, :], s0_d.ap()[oc * 128:(oc + 1) * 128, :],
              reads=[("s0_d", oc)], writes=["b2"])
        p.step("vector", lambda e, oc=oc: e.tensor_mul(b4[:, :], b2[:, :], P1[oc][:, :]),
               reads=["b2", ("P1", oc)], writes=["b4"])
        p.step("vector", lambda e: e.tensor_add(f1[:, :], f0[:, :], b4[:, :]),
               reads=["f0", "b4"], writes=["f1"])
        stores.append(p.dma("sync", x1_d.ap()[oc * 128:(oc + 1) * 128, :], f1[:, :],
                            reads=["f1"], writes=[("x1_d", oc)], name=f"stx1{oc}"))
        p.step("vector", lambda e: e.tensor_copy(b4[:, :], f1[:, :]),
               reads=["f1"], writes=["b4"])
        p.step("scalar", lambda e: e.activation(b2[:, :], b4[:, :], AF.Square),
               reads=["b4"], writes=["b2"])
        p.step("tensor", mm_group(ps0[:, :], [(ones_sq[:, :], b4[:, :])], MMB,
                                  start=(oc == 0), stop=(oc == NCH - 1)),
               reads=["b4", "ones_sq"], writes=["ps0"], name=f"s1x{oc}")
        p.step("tensor", mm_group(ps1[:, :], [(ones_sq[:, :], b2[:, :])], MMB,
                                  start=(oc == 0), stop=(oc == NCH - 1)),
               reads=["b2", "ones_sq"], writes=["ps1"], name=f"s1q{oc}")

    def ln_finalize(eps_ap, eps_key, name):
        p.step("vector", lambda e: e.tensor_scalar(f0[0:1, :], ps0[0:1, :], 1.0 / E,
                                                   None, op0=ALU.mult),
               reads=["ps0"], writes=["f0"], name=f"{name}mu")
        p.step("vector", lambda e: e.tensor_scalar(f1[0:1, :], ps1[0:1, :], 1.0 / E,
                                                   None, op0=ALU.mult),
               reads=["ps1"], writes=["f1"], name=f"{name}msq")
        p.step("vector", lambda e: e.tensor_mul(crow[0:1, :], f0[0:1, :], f0[0:1, :]),
               reads=["f0"], writes=["crow"])
        p.step("vector", lambda e: e.tensor_sub(f1[0:1, :], f1[0:1, :], crow[0:1, :]),
               reads=["f1", "crow"], writes=["f1"])
        p.step("scalar", lambda e: e.activation(f1[0:1, :], f1[0:1, :], AF.Sqrt,
                                                bias=eps_ap),
               reads=["f1", eps_key], writes=["f1"], name=f"{name}sd")
        p.step("vector", lambda e: e.reciprocal(crow[0:1, :], f1[0:1, :]),
               reads=["f1"], writes=["crow"], name=f"{name}rstd")
        p.step("vector", lambda e: e.tensor_mul(f0[0:1, :], f0[0:1, :], crow[0:1, :]),
               reads=["f0", "crow"], writes=["f0"])
        p.step("vector", lambda e: e.tensor_scalar(f0[0:1, :], f0[0:1, :], -1.0,
                                                   None, op0=ALU.mult),
               reads=["f0"], writes=["f0"])
        p.step("tensor", mm_group(ps0[:, :], [(ones_row[:, :], crow[0:1, :])], MMF),
               reads=["ones_row", "crow"], writes=["ps0"], name=f"{name}bcr")
        p.step("tensor", mm_group(ps1[:, :], [(ones_row[:, :], f0[0:1, :])], MMF),
               reads=["ones_row", "f0"], writes=["ps1"], name=f"{name}bcn")

    ln_finalize(eps5[0:1, :], "eps5", "ln1")
    for oc in range(NCH):
        p.dma("sync", f0[:, :], x1_d.ap()[oc * 128:(oc + 1) * 128, :],
              reads=[("x1_d", oc)], writes=["f0"])
        p.step("vector", lambda e: e.tensor_mul(f1[:, :], f0[:, :], ps0[:, :]),
               reads=["f0", "ps0"], writes=["f1"])
        p.step("vector", lambda e, oc=oc: e.tensor_add(xn[oc][:, :], f1[:, :], ps1[:, :]),
               reads=["f1", "ps1"], writes=[("xn", oc)], name=f"xn1_{oc}")

    for ic in range(NCH):
        p.dma("gpsimd", wA[:, ic * 512:(ic + 1) * 512],
              wqkv[ic * 128:(ic + 1) * 128, 2 * E:2 * E + 512],
              writes=[("wA", 0, ic), ("wA", 1, ic)], name=f"ldwv_a{ic}", sync_same=False)
        p.dma("gpsimd", wA[:, 4096 + ic * 512: 4096 + (ic + 1) * 512],
              wqkv[ic * 128:(ic + 1) * 128, 2 * E + 512:3 * E],
              writes=[("wA", 0, ic), ("wA", 1, ic)], name=f"ldwv_b{ic}", sync_same=False)
    for tt in range(NTT):
        pairs = [(xn[ic][:, tt * 128:(tt + 1) * 128], wA[:, ic * 512:(ic + 1) * 512])
                 for ic in range(NCH)]
        pairs2 = [(xn[ic][:, tt * 128:(tt + 1) * 128],
                   wA[:, 4096 + ic * 512:4096 + (ic + 1) * 512]) for ic in range(NCH)]
        p.step("tensor", mm_group(ps0[:, 0:512], pairs, MMB),
               reads=[("wA", h_, ic) for h_ in (0, 1) for ic in range(NCH)] + xk, writes=["ps0"], name=f"v1a{tt}")
        p.step("tensor", mm_group(ps0[:, 512:1024], pairs2, MMB),
               reads=[("wA", h_, ic) for h_ in (0, 1) for ic in range(NCH)] + xk, writes=["ps0"], name=f"v1b{tt}")
        p.step("scalar", lambda e, tt=tt: e.copy(
            P1[tt // 2][:, (tt % 2) * 1024:(tt % 2) * 1024 + 1024], ps0[:, 0:1024]),
            reads=["ps0"], writes=[("P1", tt // 2)], name=f"v1e{tt}")

    p.dma("sync", gsig[0:1, 0:H], scale_r, writes=["gsig"])
    p.step("tensor", mm_group(ps1[:, 0:H], [(ones_row[:, :], gsig[0:1, :])], MMF),
           reads=["ones_row", "gsig"], writes=["ps1"])
    p.step("scalar", lambda e: e.copy(scvb[:, :], ps1[:, 0:H]),
           reads=["ps1"], writes=["scvb"])
    p.step("vector", lambda e: e.memset(kst[64:65, :], 1.0),
           reads=["kst"], writes=["kstones"])

    def v1ap(j, h):
        return P1[j // 2][:, (j % 2) * 1024 + h * 64:(j % 2) * 1024 + h * 64 + 64]

    for h in range(H):
        oc, h2 = h // 2, h % 2
        for ic in range(NCH):
            p.dma("gpsimd", qkw[:, ic * 64:(ic + 1) * 64],
                  wqkv[ic * 128:(ic + 1) * 128, h * 64:(h + 1) * 64],
                  writes=["qkw"], name=f"ldq{h}_{ic}")
            p.dma("gpsimd", qkw[:, 512 + ic * 64: 512 + (ic + 1) * 64],
                  wqkv[ic * 128:(ic + 1) * 128, E + h * 64:E + (h + 1) * 64],
                  writes=["qkw"], name=f"ldk{h}_{ic}")
        pairsQ = [(qkw[:, ic * 64:(ic + 1) * 64], xn[ic][:, :]) for ic in range(NCH)]
        pairsK = [(qkw[:, 512 + ic * 64:512 + (ic + 1) * 64], xn[ic][:, :])
                  for ic in range(NCH)]
        p.step("tensor", mm_group(ps0[0:64, :], pairsQ, MMB),
               reads=["qkw"] + xk, writes=["ps0"], name=f"qproj{h}")
        p.step("scalar", lambda e, h=h: e.activation(
            qst[0:64, :], ps0[0:64, :], AF.Identity, scale=scvb[0:64, h:h + 1]),
            reads=["ps0", "scvb"], writes=["qst"], name=f"qev{h}")
        p.step("tensor", mm_group(ps0[0:64, :], pairsK, MMB),
               reads=["qkw"] + xk, writes=["ps0"], name=f"kproj{h}")
        p.step("scalar", lambda e: e.copy(kst[0:64, :], ps0[0:64, :]),
               reads=["ps0"], writes=["kst"], name=f"kev{h}")
        for i in range(NTT):
            p.step("tensor", mm_group(ps0[:, :],
                                      [(qst[0:64, i * 128:(i + 1) * 128], kst[0:64, :])],
                                      MMB),
                   reads=["qst", "kst", "kstones"], writes=["ps0"], name=f"SA{h}_{i}")
            p.step("vector", lambda e, i=i: e.tensor_add(
                f0[:, 0:128], ps0[:, i * 128:(i + 1) * 128], tA[:, :]),
                reads=["ps0", "tA"], writes=["f0"])
            p.step("vector", lambda e: e.tensor_reduce(mx2[:, :], f0[:, 0:128],
                                                       axis=AX.X, op=ALU.max),
                   reads=["f0"], writes=["mx2"])
            if i > 0:
                p.step("vector", lambda e, i=i: e.tensor_reduce(
                    mx1[:, :], ps0[:, 0:i * 128], axis=AX.X, op=ALU.max),
                    reads=["ps0"], writes=["mx1"])
                p.step("vector", lambda e: e.tensor_max(mx2[:, :], mx2[:, :], mx1[:, :]),
                       reads=["mx1", "mx2"], writes=["mx2"])
            p.step("vector", lambda e, i=i: e.tensor_scalar(
                mcol[:, i:i + 1], mx2[:, :], -1e30, -1.0, op0=ALU.max, op1=ALU.mult),
                reads=["mx2"], writes=["mcol"], name=f"mneg{h}_{i}")
        p.step("tensor", lambda e: e.transpose(ps1[0:16, 0:128], mcol[:, :], idf[:, :]),
               reads=["mcol", "idf"], writes=["ps1"], name=f"mtr{h}")
        p.step("scalar", lambda e: e.copy(msA[:, :], ps1[0:16, 0:128]),
               reads=["ps1"], writes=["msA"], name=f"msev{h}")
        sid = p.dma("sync", mrow_d.ap(), msA[:, :], reads=["msA"],
                    writes=["mrow_d"], name=f"stm{h}")
        p.dma("gpsimd", qst[64:65, :],
              mrow_d.ap().rearrange("a b -> (a b)")[None, :],
              reads=["mrow_d", "qst"], writes=["qstm"], deps=[sid],
              name=f"ldm{h}")
        for j in range(NTT):
            n0 = j * 128
            p.step("tensor", mm_group(ps0[:, n0:T],
                                      [(kst[0:65, n0:n0 + 128], qst[0:65, n0:T])], MMB,
                                      bank_off=n0),
                   reads=["qst", "kst", "qstm", "kstones"], writes=["ps0"],
                   name=f"SB{h}_{j}")
            p.step("vector", lambda e, n0=n0: e.tensor_add(
                f0[:, 0:128], ps0[:, n0:n0 + 128], tB[:, :]),
                reads=["ps0", "tB"], writes=["f0"])
            p.step("scalar", lambda e, n0=n0: e.activation(
                b2[:, n0:n0 + 128], f0[:, 0:128], AF.Exp),
                reads=["f0"], writes=["b2"], name=f"expd{h}_{j}")
            if j < NTT - 1:
                p.step("scalar", lambda e, n0=n0: e.activation(
                    b2[:, n0 + 128:T], ps0[:, n0 + 128:T], AF.Exp),
                    reads=["ps0"], writes=["b2"], name=f"expr{h}_{j}")
            p.step("tensor", mm_denom(ps1, ones_sq, b2, n0, j),
                   reads=["b2", "ones_sq"], writes=["ps1"], name=f"dn{h}_{j}")
            p.step("tensor", mm_o(ps1, v1ap(j, h), b2, n0, j),
                   reads=["b2", ("P1", j // 2)], writes=["ps1"], name=f"O{h}_{j}")
        p.step("scalar", lambda e: e.copy(f0[0:1, :], ps1[0:1, :]),
               reads=["ps1"], writes=["f0"], name=f"dnev{h}")
        p.step("vector", lambda e: e.tensor_scalar(f1[0:1, :], f0[0:1, :], 0.0, None,
                                                   op0=ALU.is_equal),
               reads=["f0"], writes=["f1"])
        p.step("vector", lambda e: e.tensor_add(f0[0:1, :], f0[0:1, :], f1[0:1, :]),
               reads=["f0", "f1"], writes=["f0"])
        p.step("vector", lambda e: e.reciprocal(f0[0:1, :], f0[0:1, :]),
               reads=["f0"], writes=["f0"], name=f"rcp{h}")
        p.step("tensor", mm_group(ps0[:, :], [(ones_row[:, :], f0[0:1, :])], MMF),
               reads=["ones_row", "f0"], writes=["ps0"], name=f"rbb{h}")
        p.step("scalar", lambda e: e.copy(b4[:, :], ps0[:, :]),
               reads=["ps0"], writes=["b4"], name=f"rbe{h}")
        rows = slice(h2 * 64, h2 * 64 + 64)
        p.step("scalar", lambda e, rows=rows: e.copy(b2[rows, :], ps1[64:128, :]),
               reads=["ps1"], writes=["b2"], name=f"oev{h}")
        p.step("vector", lambda e, oc=oc, rows=rows: e.tensor_mul(
            P2[oc][rows, :], b2[rows, :], b4[rows, :]),
            reads=["b2", "b4"], writes=[("P2", oc)], name=f"onorm{h}")

    for ic in range(NCH):
        p.dma("gpsimd", wA[:, ic * E:(ic + 1) * E], wo[ic * 128:(ic + 1) * 128, :],
              writes=[("wA", 0, ic), ("wA", 1, ic)], name=f"ldwo{ic}", sync_same=False)
    for oc in range(NCH):
        p.dma("sync", bias_s[:, 0:1], b_out[oc], writes=["bias_s"])
        pairs = [(wA[:, ic * E + oc * 128: ic * E + oc * 128 + 128], P2[ic][:, :])
                 for ic in range(NCH)]
        p.step("tensor", mm_group(ps0[:, :], pairs, MMB),
               reads=[("wA", h_, ic) for h_ in (0, 1) for ic in range(NCH)] + [("P2", ic) for ic in range(NCH)],
               writes=["ps0"], name=f"oproj{oc}")
        p.step("scalar", lambda e: e.activation(b4[:, :], ps0[:, :], AF.Identity,
                                                bias=bias_s[:, 0:1]),
               reads=["ps0", "bias_s"], writes=["b4"], name=f"opev{oc}")
        stores.append(p.dma("sync", s1_d.ap()[oc * 128:(oc + 1) * 128, :], b4[:, :],
                            reads=["b4"], writes=[("s1_d", oc)], name=f"sts1{oc}"))

    for ic in range(NCH):
        p.dma("gpsimd", qkw[:, 1024 + ic * 8:1024 + (ic + 1) * 8],
              WrT[ic * 128:(ic + 1) * 128, :], writes=["qkw"], name=f"ldwr{ic}")
    p.dma("sync", bias_s[0:8, 1:2], br_col, writes=["bias_s"])
    pairs = [(qkw[:, 1024 + ic * 8:1024 + (ic + 1) * 8], xn[ic][:, :])
             for ic in range(NCH)]
    p.step("tensor", mm_group(ps0[0:8, :], pairs, MMB),
           reads=["qkw"] + xk, writes=["ps0"], name="r1mm")
    for tt in range(NTT):
        p.step("scalar", lambda e, tt=tt: e.activation(
            r8[:, :], ps0[0:8, tt * 128:(tt + 1) * 128], AF.Identity,
            bias=bias_s[0:8, 1:2]),
            reads=["ps0", "bias_s"], writes=["r8"], name=f"r1ev{tt}")
        p.step("tensor", lambda e: e.transpose(
            ps1[:, 0:8], r8[:, :], idf[0:8, 0:8]),
            reads=["r8", "idf"], writes=["ps1"], name=f"r1tr{tt}")
        p.step("scalar", lambda e: e.copy(lt8[:, :], ps1[0:128, 0:8]),
               reads=["ps1"], writes=["lt8"])
        p.step("vector", lambda e: e.tensor_reduce(m1c[:, :], lt8[:, :], axis=AX.X,
                                                   op=ALU.max),
               reads=["lt8"], writes=["m1c"])
        p.step("vector", lambda e: e.tensor_scalar(eq1[:, :], lt8[:, :], m1c[:, :],
                                                   None, op0=ALU.is_ge),
               reads=["lt8", "m1c"], writes=["eq1"])
        p.step("vector", lambda e: e.scalar_tensor_tensor(
            l2f[:, :], eq1[:, :], -1e30, lt8[:, :], op0=ALU.mult, op1=ALU.add),
            reads=["eq1", "lt8"], writes=["l2f"])
        p.step("vector", lambda e: e.tensor_reduce(m2c[:, :], l2f[:, :], axis=AX.X,
                                                   op=ALU.max),
               reads=["l2f"], writes=["m2c"])
        p.step("vector", lambda e: e.tensor_sub(w1c[:, :], m1c[:, :], m2c[:, :]),
               reads=["m1c", "m2c"], writes=["w1c"])
        p.step("scalar", lambda e: e.activation(w1c[:, :], w1c[:, :], AF.Sigmoid),
               reads=["w1c"], writes=["w1c"])
        p.step("vector", lambda e: e.tensor_scalar(w2c[:, :], w1c[:, :], -1.0, 1.0,
                                                   op0=ALU.mult, op1=ALU.add),
               reads=["w1c"], writes=["w2c"])
        p.step("vector", lambda e: e.tensor_scalar(eq2[:, :], l2f[:, :], m2c[:, :],
                                                   None, op0=ALU.is_ge),
               reads=["l2f", "m2c"], writes=["eq2"])
        p.step("vector", lambda e: e.tensor_scalar(cw1[:, :], eq1[:, :], w1c[:, :],
                                                   None, op0=ALU.mult),
               reads=["eq1", "w1c"], writes=["cw1"])
        p.step("vector", lambda e: e.tensor_scalar(eq2[:, :], eq2[:, :], w2c[:, :],
                                                   None, op0=ALU.mult),
               reads=["eq2", "w2c"], writes=["eq2"])
        p.step("vector", lambda e: e.tensor_add(cw1[:, :], cw1[:, :], eq2[:, :]),
               reads=["cw1", "eq2"], writes=["cw1"])
        p.step("tensor", lambda e, tt=tt: e.transpose(ps1[0:8, 8:136], cw1[:, :], idf[:, :]),
               reads=["cw1", "idf"], writes=["ps1"], name=f"cw1tr{tt}")
        p.step("scalar", lambda e: e.copy(trv[0:8, :], ps1[0:8, 8:136]),
               reads=["ps1"], writes=["trv"])
        stores.append(p.dma("sync", cw1_d.ap()[:, tt * 128:(tt + 1) * 128],
                            trv[0:8, :], reads=["trv"], writes=["cw1_d"],
                            name=f"stcw1{tt}"))

    for oc in range(NCH):
        p.step("vector", lambda e, oc=oc: e.memset(P2[oc][:, :], 0.0),
               writes=[("P2", oc)], name=f"acc1z{oc}")

    for ee in range(8):
        p.dma("sync", crow[0:1, :], cw1_d.ap()[ee:ee + 1, :], reads=["cw1_d"],
              writes=["crow"], name=f"ldc1{ee}")
        p.step("tensor", mm_group(ps0[:, :], [(ones_row[:, :], crow[0:1, :])], MMF),
               reads=["ones_row", "crow"], writes=["ps0"], name=f"c1b{ee}")
        p.step("scalar", lambda e: e.copy(cwb[:, :], ps0[:, :]),
               reads=["ps0"], writes=["cwb"], name=f"c1e{ee}")
        p.dma("sync", b1c[:, :], b1_r[ee], writes=["b1c"])
        p.dma("sync", b2c[:, :], b2_r[ee], writes=["b2c"])
        for q in range(4):
            q0 = q * 512
            for fcg in range(8):
                half = fcg % 2
                for ic in range(NCH):
                    p.dma("gpsimd",
                          wA[:, half * 4096 + ic * 512: half * 4096 + (ic + 1) * 512],
                          W1T_all[ee, ic * 128:(ic + 1) * 128,
                                  fcg * 512:(fcg + 1) * 512],
                          writes=[("wA", half, ic)], name=f"w1{ee}_{q}_{fcg}_{ic}",
                          sync_same=False)
                for k4 in range(4):
                    fc = fcg * 4 + k4
                    pairs = [(wA[:, half * 4096 + ic * 512 + k4 * 128:
                                 half * 4096 + ic * 512 + k4 * 128 + 128],
                              xn[ic][:, q0:q0 + 512]) for ic in range(NCH)]
                    p.step("tensor", mm_group(ps0[:, 0:512], pairs, MMB),
                           reads=[("wA", half, ic) for ic in range(NCH)] + xk,
                           writes=["ps0"], name=f"h{ee}_{q}_{fc}")
                    p.step("scalar", lambda e, fc=fc: e.activation(
                        P1[fc // 4][:, (fc % 4) * 512:(fc % 4) * 512 + 512],
                        ps0[:, 0:512], AF.Gelu, bias=b1c[:, fc:fc + 1]),
                        reads=["ps0", "b1c"], writes=[("P1", fc // 4)],
                        name=f"gelu{ee}_{q}_{fc}")
            for fc in range(32):
                half = fc % 2
                p.dma("gpsimd", w2r[:, half * 1024:(half + 1) * 1024],
                      W2T_all[ee, fc * 128:(fc + 1) * 128, :],
                      writes=[("w2r", half)], name=f"w2{ee}_{q}_{fc}")
                def w2emit(e, half=half, fc=fc):
                    last = None
                    for oc in range(NCH):
                        psv = (ps0 if oc < 4 else ps1)
                        col = (oc % 4) * 512
                        last = e.matmul(
                            psv[:, col:col + 512],
                            w2r[:, half * 1024 + oc * 128: half * 1024 + oc * 128 + 128],
                            P1[fc // 4][:, (fc % 4) * 512:(fc % 4) * 512 + 512],
                            start=(fc == 0), stop=(fc == 31))
                    return last
                p.step("tensor", w2emit,
                       reads=[("w2r", half), ("P1", fc // 4)],
                       writes=["ps0", "ps1"], name=f"w2m{ee}_{q}_{fc}")
            for oc in range(NCH):
                psv = (ps0 if oc < 4 else ps1)
                col = (oc % 4) * 512
                p.step("scalar", lambda e, psv=psv, col=col, oc=oc: e.activation(
                    f0[:, 0:512], psv[:, col:col + 512], AF.Identity,
                    bias=b2c[:, oc:oc + 1]),
                    reads=["ps0" if oc < 4 else "ps1", "b2c"], writes=["f0"], name=f"ye{ee}_{q}_{oc}")
                p.step("vector", lambda e, q0=q0: e.tensor_mul(
                    b4[:, 0:512], f0[:, 0:512], cwb[:, q0:q0 + 512]),
                    reads=["f0", "cwb"], writes=["b4"])
                p.step("vector", lambda e, oc=oc, q0=q0: e.tensor_add(
                    P2[oc][:, q0:q0 + 512], P2[oc][:, q0:q0 + 512], b4[:, 0:512]),
                    reads=["b4", ("P2", oc)], writes=[("P2", oc)],
                    name=f"a1{ee}_{q}_{oc}")

    for oc in range(NCH):
        p.step("vector", lambda e, oc=oc: e.tensor_add(
            P1[oc][:, :], P2[oc][:, :], xn[oc][:, :]),
            reads=[("P2", oc), ("xn", oc)], writes=[("P1", oc)], name=f"z{oc}")
        p.step("scalar", lambda e, oc=oc: e.activation(b2[:, :], P1[oc][:, :], AF.Square),
               reads=[("P1", oc)], writes=["b2"])
        p.step("tensor", mm_group(ps0[:, :], [(ones_sq[:, :], P1[oc][:, :])], MMB,
                                  start=(oc == 0), stop=(oc == NCH - 1)),
               reads=[("P1", oc), "ones_sq"], writes=["ps0"], name=f"s2x{oc}")
        p.step("tensor", mm_group(ps1[:, :], [(ones_sq[:, :], b2[:, :])], MMB,
                                  start=(oc == 0), stop=(oc == NCH - 1)),
               reads=["b2", "ones_sq"], writes=["ps1"], name=f"s2q{oc}")
    ln_finalize(eps2[0:1, :], "eps2", "ln2")
    for oc in range(NCH):
        p.dma("sync", bias_s[:, 2:3], lnw[oc], writes=["bias_s"])
        p.dma("sync", bias_s[:, 3:4], lnb[oc], writes=["bias_s2"])
        p.step("vector", lambda e, oc=oc: e.tensor_mul(f0[:, :], P1[oc][:, :], ps0[:, :]),
               reads=[("P1", oc), "ps0"], writes=["f0"], name=f"zn1_{oc}")
        p.step("vector", lambda e: e.tensor_add(f0[:, :], f0[:, :], ps1[:, :]),
               reads=["f0", "ps1"], writes=["f0"])
        p.step("vector", lambda e: e.tensor_scalar(f0[:, :], f0[:, :], bias_s[:, 2:3],
                                                   bias_s[:, 3:4], op0=ALU.mult,
                                                   op1=ALU.add),
               reads=["f0", "bias_s", "bias_s2"], writes=["f0"], name=f"affine{oc}")
        p.step("scalar", lambda e: e.activation(b4[:, :], f0[:, :], AF.Silu),
               reads=["f0"], writes=["b4"], name=f"g1_{oc}")
        p.dma("sync", f0[:, :], x1_d.ap()[oc * 128:(oc + 1) * 128, :],
              reads=[("x1_d", oc)], writes=["f0"])
        p.dma("sync", b2[:, :], s1_d.ap()[oc * 128:(oc + 1) * 128, :],
              reads=[("s1_d", oc)], writes=["b2"])
        p.step("vector", lambda e: e.tensor_mul(b2[:, :], b2[:, :], b4[:, :]),
               reads=["b2", "b4"], writes=["b2"])
        p.step("vector", lambda e: e.tensor_add(f1[:, :], f0[:, :], b2[:, :]),
               reads=["f0", "b2"], writes=["f1"], name=f"x2_{oc}")
        stores.append(p.dma("sync", outT[oc * 128:(oc + 1) * 128, :], f1[:, :],
                            reads=["f1"], name=f"stout{oc}"))


    p.barrier_step("sync", stores)
    with nc.Block() as block:
        p.emit(block)
    return nc

def _rotary_tables():
    inv = (1.0 / 10000.0 ** (np.arange(0, D, 2, dtype=np.float64) / D))
    t = np.arange(T, dtype=np.float64)
    freqs = (t[:, None] * inv[None, :]).astype(np.float32)
    emb = np.concatenate([freqs, freqs], axis=-1)
    cos = np.ascontiguousarray(np.cos(emb).T.astype(np.float32))
    sin = np.ascontiguousarray(np.sin(emb).T.astype(np.float32))
    return cos, sin


def _rot_mat2():
    R = np.zeros((D, D), np.float32)
    for i in range(D // 2):
        R[2 * i, 2 * i + 1] = -1.0
        R[2 * i + 1, 2 * i] = 1.0
    RT = np.ascontiguousarray(R.T)
    return np.ascontiguousarray(np.vstack([RT, RT]))


def _chunk_bias(b):
    return np.ascontiguousarray(b.reshape(NCH, 128, 1).astype(np.float32))


def launch1_inputs(inp):
    cos, sin = _rotary_tables()
    shared = {
        "wqT": np.ascontiguousarray(inp["L0_attn_Wq"].T),
        "wkT": np.ascontiguousarray(inp["L0_attn_Wk"].T),
        "wvT": np.ascontiguousarray(inp["L0_attn_Wv"].T),
        "wgT": np.ascontiguousarray(inp["L0_attn_Wg"].T),
        "bq": _chunk_bias(np.asarray(inp["L0_attn_bq"])),
        "bk": _chunk_bias(np.asarray(inp["L0_attn_bk"])),
        "bg": _chunk_bias(np.asarray(inp["L0_attn_bg"])),
        "bv_row": np.ascontiguousarray(np.asarray(inp["L0_attn_bv"]).reshape(1, E)),
        "gamma": np.ascontiguousarray(np.asarray(inp["L0_attn_gamma"]).reshape(1, H)),
        "gwT": np.ascontiguousarray(np.asarray(inp["L0_ffn_gate_weight"]).T),
        "coss": np.vstack([cos, cos]) / np.sqrt(8.0),
        "sins": np.vstack([sin, sin]) / np.sqrt(8.0),
        "rotT2": _rot_mat2(),
    }
    shared = {k: np.ascontiguousarray(v.astype(np.float32)) for k, v in shared.items()}
    x = np.asarray(inp["L0_attn_x"])
    maps = []
    for b in range(8):
        m = dict(shared)
        m["xT"] = np.ascontiguousarray(x[b].T.astype(np.float32))
        maps.append(m)
    return maps


def launch2_inputs(inp, res1):
    f32 = np.float32
    b1 = np.asarray(inp["L1_ffn_b1"], f32)
    b2 = np.asarray(inp["L1_ffn_b2"], f32)
    triA = np.where(np.arange(128)[None, :] < np.arange(128)[:, None], 0.0,
                    -1e38).astype(f32)
    triB = np.where(np.arange(128)[None, :] > np.arange(128)[:, None], 0.0,
                    -1e38).astype(f32)
    shared = {
        "gupT_all": np.ascontiguousarray(
            np.asarray(inp["L0_ffn_gate_up_proj"], f32).transpose(0, 2, 1)),
        "downT_all": np.ascontiguousarray(
            np.asarray(inp["L0_ffn_down_proj"], f32).transpose(0, 2, 1)),
        "wqkv": np.ascontiguousarray(np.asarray(inp["L1_attn_w_qkv"], f32)),
        "scale_r": np.asarray(inp["L1_attn_scale"], f32).reshape(1, H).copy(),
        "wo": np.ascontiguousarray(np.asarray(inp["L1_attn_w_out"], f32)),
        "b_out": _chunk_bias(np.asarray(inp["L1_attn_b_out"], f32)),
        "WrT": np.ascontiguousarray(np.asarray(inp["L1_ffn_Wr"], f32).T),
        "br_col": np.asarray(inp["L1_ffn_br"], f32).reshape(8, 1).copy(),
        "W1T_all": np.ascontiguousarray(
            np.asarray(inp["L1_ffn_W1"], f32).transpose(0, 2, 1)),
        "b1_r": np.ascontiguousarray(b1.reshape(8, 32, 128).transpose(0, 2, 1)),
        "W2T_all": np.ascontiguousarray(
            np.asarray(inp["L1_ffn_W2"], f32).transpose(0, 2, 1)),
        "b2_r": np.ascontiguousarray(b2.reshape(8, 8, 128).transpose(0, 2, 1)),
        "lnw": _chunk_bias(np.asarray(inp["L1_ffn_ln_w"], f32)),
        "lnb": _chunk_bias(np.asarray(inp["L1_ffn_ln_b"], f32)),
        "triA": triA, "triB": triB,
        "iotaP": np.arange(128, dtype=f32).reshape(128, 1).copy(),
    }
    selp_all = np.ascontiguousarray(
        np.concatenate([r["selp"] for r in res1.results], axis=0))
    x = np.asarray(inp["L0_attn_x"])
    maps = []
    for b in range(8):
        m = dict(shared)
        m["xT"] = np.ascontiguousarray(x[b].T.astype(f32))
        m["xn0T"] = res1.results[b]["xn_out"]
        m["s0Ti"] = res1.results[b]["s0T"]
        m["selp_all"] = selp_all
        m["selp_my"] = np.ascontiguousarray(selp_all[b * T:(b + 1) * T])
        m["maskb"] = np.ascontiguousarray(
            (np.arange(16384).reshape(128, 128, 1) < T * b).astype(f32))
        maps.append(m)
    return maps


def fused_inputs(inp):
    m1 = launch1_inputs(inp)

    class _R:
        results = [dict(selp=np.zeros((T, 32), np.float32),
                        xn_out=np.zeros((E, T), BF),
                        s0T=np.zeros((E, T), BF)) for _ in range(8)]
    m2 = launch2_inputs(inp, _R())
    x = np.asarray(inp["L0_attn_x"], np.float32)
    xTa = np.ascontiguousarray(x.reshape(8 * T, E).T)
    maps = []
    for b in range(8):
        m = {}
        m.update(m1[b])
        m.update(m2[b])
        for k in ("xn0T", "s0Ti", "selp_all", "selp_my", "iotaP"):
            m.pop(k, None)
        m["xTa"] = xTa
        m["myoff"] = (T * b + np.arange(128, dtype=np.float32)).reshape(128, 1).copy()
        maps.append(m)
    return maps


def kernel(**inp):
    import os
    if os.environ.get("KERNEL_UNFUSED"):
        maps1 = launch1_inputs(inp)
        nc1 = build_launch1()
        res1 = run_bass_kernel_spmd(nc1, maps1, core_ids=list(range(8)))
        maps2 = launch2_inputs(inp, res1)
        nc2 = build_launch2()
        res2 = run_bass_kernel_spmd(nc2, maps2, core_ids=list(range(8)))
        out = np.stack([res2.results[b]["outT"].T for b in range(8)])
        return np.ascontiguousarray(out.astype(np.float32))
    maps = fused_inputs(inp)
    nc = build_fused()
    res = run_bass_kernel_spmd(nc, maps, core_ids=list(range(8)))
    out = np.stack([res.results[b]["outT"].T for b in range(8)])
    return np.ascontiguousarray(out.astype(np.float32))
